# version 05
# baseline (speedup 1.0000x reference)
import numpy as np
import concourse.bass as bass
import concourse.mybir as mybir
from concourse.bass_utils import run_bass_kernel_spmd

F32 = mybir.dt.float32
BF16 = mybir.dt.bfloat16
AF = mybir.ActivationFunctionType
ALU = mybir.AluOpType
AX = mybir.AxisListType

D, T, H, HD, E, HF = 1024, 2048, 16, 64, 8, 2048
DC, HC, NTT, NQB = 8, 16, 16, 4
NCORE, NSEQ = 8, 2
NEG = -1e30

BIAS_OFF = {"b0": 0, "b1": 128, "b2": 256, "bo": 320, "gb": 328, "z": 336,
            "ones": 337, "id16": 465, "id128": 481, "fc": 609}
EPS_BIAS = float(1024.0 * 1024.0 * 1e-5)
LN1024 = float(np.log(1024.0))
FCONSTS = [EPS_BIAS, 1024.0, -1.0, 1.0, -1e30, 2.0, 0.0, 0.125, LN1024, 0.5, -0.5]
BIAS_W = 609 + len(FCONSTS)
CB_W = 1408


def _host_blobs(ins):
    f = np.float32
    import ml_dtypes
    out = {}

    def mtile(W):
        return np.ascontiguousarray(W.reshape(8, 128, 8, 128).transpose(2, 1, 0, 3)).reshape(8, 128, 1024)

    out["wq"] = mtile(ins["L0_attn_a_Wq"])
    out["wk"] = mtile(ins["L0_attn_a_Wk"])
    out["wo"] = mtile(ins["L0_attn_a_Wo"])
    out["wv"] = np.ascontiguousarray(ins["L0_attn_a_Wv"])
    uq = np.ascontiguousarray(ins["L0_attn_b_W_q"].transpose(1, 0, 2).reshape(1024, 1024))
    uk = np.ascontiguousarray(ins["L0_attn_b_W_k"].transpose(1, 0, 2).reshape(1024, 1024))
    uv = np.ascontiguousarray(ins["L0_attn_b_W_v"].transpose(1, 0, 2).reshape(1024, 1024))
    out["uq"] = mtile(uq)
    out["uk"] = mtile(uk)
    out["uo"] = mtile(ins["L0_attn_b_W_o"])
    out["uv"] = uv
    out["w0t"] = np.ascontiguousarray(
        ins["L0_ffn_w0"].reshape(8, 16, 128, 8, 128).transpose(0, 1, 4, 3, 2)).reshape(8, 16, 128, 1024)
    out["w1t"] = np.ascontiguousarray(
        ins["L0_ffn_w1"].reshape(8, 16, 128, 8, 128).transpose(0, 1, 4, 3, 2)).reshape(8, 16, 128, 1024)
    out["w2t"] = np.ascontiguousarray(
        ins["L0_ffn_w2"].reshape(8, 8, 128, 16, 128).transpose(0, 1, 4, 3, 2)).reshape(8, 8, 128, 2048)
    out["gwt"] = np.ascontiguousarray(
        ins["L0_ffn_gate_w"].T.reshape(8, 128, 8).transpose(1, 0, 2)).reshape(128, 64)

    b0c = np.ascontiguousarray(ins["L0_ffn_b0"].reshape(8, 16, 128).transpose(2, 0, 1)).reshape(128, 128)
    b1c = np.ascontiguousarray(ins["L0_ffn_b1"].reshape(8, 16, 128).transpose(2, 0, 1)).reshape(128, 128)
    b2c = np.ascontiguousarray(ins["L0_ffn_b2"].reshape(8, 8, 128).transpose(2, 0, 1)).reshape(128, 64)
    boc = np.ascontiguousarray(ins["L0_attn_a_bo"].reshape(8, 128).T)
    gbb = np.broadcast_to(ins["L0_ffn_gate_b"].astype(f), (128, 8))
    zeros = np.zeros((128, 1), f)
    onesf = np.ones((128, 128), f)
    idf16 = np.zeros((128, 16), f)
    idf16[:16, :] = np.eye(16, dtype=f)
    idf128 = np.eye(128, dtype=f)
    fcc = np.broadcast_to(np.asarray(FCONSTS, f)[None, :], (128, len(FCONSTS)))
    blob = np.concatenate([b0c, b1c, b2c, boc, gbb, zeros, onesf, idf16, idf128, fcc],
                          axis=1).astype(f)
    assert blob.shape[1] == BIAS_W, blob.shape
    out["biasblob"] = np.ascontiguousarray(blob)

    idn = np.eye(128, dtype=f)
    hind = np.zeros((128, 128), f)
    for c in range(8):
        for p in range(128):
            hind[p, c * 16 + (2 * c + p // 64)] = 1.0
    onesb = np.ones((128, 128), f)
    sel = np.zeros((128, 8 * 128), f)
    for e in range(8):
        sel[e, e * 128:(e + 1) * 128] = 1.0
    cb = np.concatenate([idn, hind, onesb, sel], axis=1)
    assert cb.shape[1] == CB_W
    out["constsb"] = np.ascontiguousarray(cb.astype(ml_dtypes.bfloat16))

    tm = np.full((128, 2048), NEG, f)
    for r in range(4):
        for p in range(128):
            j0 = r * 128 + p
            if j0 < 512:
                tm[p, r * 512 + j0: (r + 1) * 512] = 0.0
    out["trimask"] = tm.astype(ml_dtypes.bfloat16)
    return out


class B:
    def __init__(self, n_sub=4):
        self.n_sub = n_sub
        nc = bass.Bass("TRN2", target_bir_lowering=False)
        self.nc = nc
        self._ctx = []
        self.d_xt = nc.dram_tensor("xt", [NSEQ, D, T], F32, kind="ExternalInput").ap()
        self.d_out = nc.dram_tensor("out", [NSEQ, D, T], F32, kind="ExternalOutput").ap()
        for n, shp in [("wq", [8, 128, 1024]), ("wk", [8, 128, 1024]), ("wo", [8, 128, 1024]),
                       ("wv", [1024, 1024]), ("uq", [8, 128, 1024]), ("uk", [8, 128, 1024]),
                       ("uo", [8, 128, 1024]), ("uv", [1024, 1024]),
                       ("w0t", [8, 16, 128, 1024]), ("w1t", [8, 16, 128, 1024]),
                       ("w2t", [8, 8, 128, 2048]), ("gwt", [128, 64])]:
            setattr(self, "d_" + n, nc.dram_tensor(n, shp, F32, kind="ExternalInput").ap())
        self.d_trimask = nc.dram_tensor("trimask", [128, 2048], BF16, kind="ExternalInput").ap()
        self.d_bias = nc.dram_tensor("biasblob", [128, BIAS_W], F32, kind="ExternalInput").ap()
        self.d_rsd = nc.dram_tensor("rsd", [2, 1024], BF16, kind="Internal").ap()
        self.d_ncd = nc.dram_tensor("ncd", [32], F32, kind="Internal").ap()
        self.d_cb = nc.dram_tensor("constsb", [128, CB_W], BF16, kind="ExternalInput").ap()

        off = [16512]

        def alloc(name, shape, dt, at=None):
            nbytes = int(np.prod(shape[1:])) * (2 if dt == BF16 else 4)
            if at is None:
                at = off[0]
                off[0] = (at + nbytes + 31) & ~31
            return nc.alloc_sbuf_tensor_at(name, shape, dt, offset=at).ap()

        self.X = [alloc(f"x{c}", [128, 2048], F32) for c in range(DC)]
        self.XN = [alloc(f"xn{c}", [128, 2048], BF16) for c in range(DC)]
        self.BIAS = alloc("bias", [128, BIAS_W], F32)
        self.CB = alloc("cb", [128, CB_W], BF16)
        self.KMX = alloc("kmx", [16, 1], F32)
        AR0 = off[0]
        self.AR0 = AR0
        assert 229344 - AR0 >= 109000, (AR0, 229344 - AR0)

        def ar(name, shape, dt, o):
            return alloc(name, shape, dt, at=AR0 + o)

        self.K = [ar(f"k{c}", [128, 2048], BF16, c * 4096) for c in range(DC)]
        self.VH = [ar(f"vh{t}", [128, 1040], BF16, 32768 + t * 2080) for t in range(NTT)]
        self.Q = [ar(f"q{c}", [128, 512], BF16, 66048 + c * 1024) for c in range(DC)]
        self.CTX = [ar(f"ctx{c}", [128, 512], BF16, 74240 + c * 1024) for c in range(DC)]
        self.PTB = ar("ptb", [128, 2048], BF16, 82432)
        self.WS = [ar(f"ws{i}", [128, 1024], BF16, 86528 + i * 2048) for i in range(2)]
        self.WV = ar("wvs", [128, 8 * 1024], BF16, 90624)
        self.TM = ar("tm", [128, 2048], BF16, 90624)
        self.RBS = [ar("rbs0", [64, 1024], BF16, 94720),
                    ar("rbs1", [64, 1024], BF16, 96768)]
        self.RROW = [ar("rrow0", [1, 1024], F32, 98816),
                     ar("rrow1", [1, 1024], F32, 102912)]
        self.KSQ = ar("ksq", [128, 2048], BF16, 66048)
        self.PTA = ar("pta", [128, 1024], BF16, 107008)
        self.QSQ = ar("qsq", [128, 512], BF16, 107008)
        self.CBIA = ar("cbia", [128, 32], F32, 109056)

        self.ACT = [ar(f"act{hh}", [128, 2048], BF16, hh * 4096) for hh in range(HC)]
        self.W01 = [ar(f"w01{i}", [128, 2048], BF16, 65536 + i * 4096) for i in range(2)]
        self.W2B = [ar(f"w2b{i}", [128, 2048], BF16, 73728 + i * 4096) for i in range(2)]
        self.SIL = ar("sil", [128, 2048], F32, 81920)
        self.MBS = ar("mbs", [128, 2048], F32, 90112)
        self.MT = ar("mt", [8, 2048], BF16, 98304)
        self.WG = ar("wg", [128, 64], BF16, 102400)
        self.GSC = ar("gsc", [128, 8], F32, 102528)
        self.GC1 = ar("gc1", [128, 8], F32, 102560)
        self.GC2 = ar("gc2", [128, 8], F32, 102592)
        self.GCV = ar("gcv", [128, 8], F32, 102624)
        self.MSC = ar("msc", [128, 8], BF16, 102656)
        self.TMPF = ar("tmpf", [128, 512], F32, 102688)

        self.XB = [ar(f"xb{i}", [128, 2048], BF16, i * 4096) for i in range(2)]
        self.XQ = [ar(f"xq{i}", [128, 2048], BF16, 8192 + i * 4096) for i in range(2)]
        self.LNA = ar("lna", [1, 2048], F32, 16384)
        self.LNB = ar("lnb", [1, 2048], F32, 24576)
        self.LNS = ar("lns", [1, 1024], F32, 32768)
        self.TXN = ar("txn", [128, 512], F32, 36864)

        self.PSA = self._cm(nc.psum_tensor("psa", [128, 2048], F32))
        self.PSB = self._cm(nc.psum_tensor("psb", [128, 2048], F32))

        self.DSEM = {}
        self.DCNT = {}
        for nm in ["C", "XL", "TM", "O", "WS0", "WS1", "WV", "W010", "W011",
                   "W2B0", "W2B1", "WG", "RS0", "RS1", "RB0", "RB1", "NC1", "NC2", "NC3", "NC4"]:
            self.DSEM[nm] = self._cm(nc.semaphore("s" + nm))
            self.DCNT[nm] = 0
        self.sP = self._cm(nc.semaphore("sP"))
        self.sA = self._cm(nc.semaphore("sA"))
        self.sV = self._cm(nc.semaphore("sV"))
        self.C = {"P": 0, "A": 0, "V": 0}
        self.OPS = {"sync": [], "gp": [], "pe": [], "act": [], "dve": []}

        bo = BIAS_OFF
        self.b_b0 = lambda e, hh: self.BIAS[:, bo["b0"] + e * 16 + hh: bo["b0"] + e * 16 + hh + 1]
        self.b_b1 = lambda e, hh: self.BIAS[:, bo["b1"] + e * 16 + hh: bo["b1"] + e * 16 + hh + 1]
        self.b_b2 = lambda e, dc: self.BIAS[:, bo["b2"] + e * 8 + dc: bo["b2"] + e * 8 + dc + 1]
        self.b_bo = lambda dc: self.BIAS[:, bo["bo"] + dc: bo["bo"] + dc + 1]
        self.b_gb = self.BIAS[:, bo["gb"]: bo["gb"] + 8]
        self.b_z = self.BIAS[:, bo["z"]: bo["z"] + 1]
        self.onesf_row = lambda n: self.BIAS[0:1, bo["ones"]: bo["ones"] + n]
        self.id16 = self.BIAS[0:16, bo["id16"]: bo["id16"] + 16]
        for j, v in enumerate(FCONSTS):
            nc.const_aps.register(F32, v, self.BIAS[:, bo["fc"] + j: bo["fc"] + j + 1])
        self.cb_id = self.CB[:, 0:128]
        self.cb_hind = lambda c: self.CB[:, 128 + c * 16: 128 + c * 16 + 16]
        self.cb_onecol = self.CB[:, 256:257]
        self.cb_sel = lambda e: self.CB[0:8, 384 + e * 128: 384 + (e + 1) * 128]

        self.x_dep = []
        self.xn_free = 0
        self.xn_ready = 0
        self.ws_free = 0
        self.w01_free = 0
        self.w2b_free = 0
        self.prev_cmb = 0
        self.prev_ffn_stt = {0: 0, 1: 0}
        self.nc2_prev = None
        self.nc4_prev = None

    def _cm(self, cm):
        v = cm.__enter__()
        self._ctx.append(cm)
        return v

    def wP(self, v): return (self.sP, v)
    def wA(self, v): return (self.sA, v)
    def wV(self, v): return (self.sV, v)

    def _emit(self, eng, fn, inc, waits, chain=False):
        sem = {"pe": self.sP, "act": self.sA, "dve": self.sV}[eng]
        cname = {"pe": "P", "act": "A", "dve": "V"}[eng]
        if chain:
            waits = list(waits) + [(sem, self.C[cname])]
        for s_, v_ in waits:
            if v_ > 0:
                self.OPS[eng].append(lambda q, s=s_, v=v_: q.wait_ge(s, v))
        if inc:
            self.C[cname] += 1
            self.OPS[eng].append(lambda q, fn=fn, sem=sem: fn(q).then_inc(sem, 1))
        else:
            self.OPS[eng].append(lambda q, fn=fn: fn(q))
        return self.C[cname]

    def pe(self, fn, inc=False, waits=(), chain=False):
        return self._emit("pe", fn, inc, waits, chain)

    def act(self, fn, inc=True, waits=(), chain=False):
        return self._emit("act", fn, True, waits, chain)

    def dve(self, fn, inc=True, waits=(), chain=False):
        return self._emit("dve", fn, True, waits, chain)

    def ld(self, dst, src, sem, waits=()):
        for s_, v_ in waits:
            if v_ > 0:
                self.OPS["sync"].append(lambda q, s=s_, v=v_: q.wait_ge(s, v))
        self.DCNT[sem] += 16
        sh = self.DSEM[sem]
        self.OPS["sync"].append(
            lambda q, dst=dst, src=src, sh=sh: q.dma_start(out=dst, in_=src).then_inc(sh, 16))
        return (sh, self.DCNT[sem])

    def wld(self, dst, src, sem, waits=()):
        for s_, v_ in waits:
            if v_ > 0:
                self.OPS["gp"].append(lambda q, s=s_, v=v_: q.wait_ge(s, v))
        self.DCNT[sem] += 16
        sh = self.DSEM[sem]
        self.OPS["gp"].append(
            lambda q, dst=dst, src=src, sh=sh: q.dma_start(out=dst, in_=src).then_inc(sh, 16))
        return (sh, self.DCNT[sem])

    def phase_gate(self):
        return [self.wP(self.C["P"]), self.wA(self.C["A"]), self.wV(self.C["V"])]

    def emit_prologue(self):
        self.ld(self.BIAS[:], self.d_bias, "C")
        self.consts_ready = self.ld(self.CB[:], self.d_cb, "C")

    def emit_load_x(self, s, after=()):
        for c in range(DC):
            t = self.ld(self.X[c][:], self.d_xt[s, c * 128:(c + 1) * 128, :], "XL",
                        waits=(after if c == 0 else ()))
        self.x_dep = [t]

    def emit_store_x(self, s):
        for c in range(DC):
            t = self.ld(self.d_out[s, c * 128:(c + 1) * 128, :], self.X[c][:], "O",
                        waits=(self.x_dep if c == 0 else ()))
        return t

    def emit_ln(self):
        PSA, PSB = self.PSA, self.PSB
        x_dep = list(self.x_dep)
        xn_free = self.xn_free
        a_xb, v_xq = [], []
        stats_m = {}
        for c in range(DC):
            i = c % 2
            w = []
            if c == 0:
                w += x_dep + [self.consts_ready]
            if c >= 2:
                w.append(self.wP(stats_m[c - 2]))
            a = self.act(lambda q, c=c, i=i: q.activation(self.XB[i][:], self.X[c][:], AF.Copy, scale=1.0),
                         inc=True, waits=w)
            a_xb.append(a)
            v = self.dve(lambda q, i=i: q.tensor_tensor(self.XQ[i][:], self.XB[i][:], self.XB[i][:], ALU.mult),
                         inc=True, waits=[self.wA(a)])
            v_xq.append(v)
            for g in range(4):
                self.pe(lambda q, i=i, g=g, c=c: q.matmul(
                    PSA[0:1, g * 512:(g + 1) * 512], self.cb_onecol, self.XB[i][:, g * 512:(g + 1) * 512],
                    start=(c == 0), stop=(c == DC - 1), skip_group_check=True),
                    waits=([self.wA(a)] if g == 0 else ()))
            for g in range(4):
                m = self.pe(lambda q, i=i, g=g, c=c: q.matmul(
                    PSB[0:1, g * 512:(g + 1) * 512], self.cb_onecol, self.XQ[i][:, g * 512:(g + 1) * 512],
                    start=(c == 0), stop=(c == DC - 1), skip_group_check=True),
                    inc=(g == 3), waits=([self.wV(v)] if g == 0 else ()))
            stats_m[c] = m
        stats_done = stats_m[DC - 1]
        vrows = 0
        for g in range(4):
            S1 = PSA[0:1, g * 512:(g + 1) * 512]
            S2 = PSB[0:1, g * 512:(g + 1) * 512]
            mmr = self.LNS[0:1, 0:512]
            rr = self.LNS[0:1, 512:1024]
            Ar = self.LNA[0:1, g * 512:(g + 1) * 512]
            B2r = self.LNB[0:1, g * 512:(g + 1) * 512]
            asq = self.act(lambda q, S1=S1, mmr=mmr: q.activation(mmr, S1, AF.Square, scale=1.0),
                           waits=([self.wP(stats_done)] if g == 0 else ()), chain=True)
            v0 = self.dve(lambda q, S2=S2, mmr=mmr: q.scalar_tensor_tensor(
                mmr, S2, 1024.0, mmr, ALU.mult, ALU.subtract), inc=True, waits=[self.wA(asq)])
            self.act(lambda q, mmr=mmr: q.activation(mmr, mmr, AF.Ln, bias=EPS_BIAS, scale=1.0),
                     waits=[self.wV(v0)], chain=True)
            a = self.act(lambda q, mmr=mmr, rr=rr: q.activation(rr, mmr, AF.Exp, scale=-0.5),
                         chain=True)
            self.dve(lambda q, rr=rr, Ar=Ar: q.tensor_scalar(Ar, rr, 1024.0, None, ALU.mult),
                     waits=[self.wA(a)])
            vrows = self.dve(lambda q, S1=S1, rr=rr, B2r=B2r: q.tensor_tensor(B2r, S1, rr, ALU.mult),
                             inc=True)
        apply_v = {}
        for g in range(4):
            Ab = PSB[:, 0:512] if g % 2 == 0 else PSB[:, 1024:1536]
            B2b = PSB[:, 512:1024] if g % 2 == 0 else PSB[:, 1536:2048]
            Ar = self.LNA[0:1, g * 512:(g + 1) * 512]
            B2r = self.LNB[0:1, g * 512:(g + 1) * 512]
            w = [self.wV(vrows)] if g < 2 else [self.wV(apply_v[g - 2])]
            self.pe(lambda q, Ab=Ab, Ar=Ar: q.matmul(Ab, self.onesf_row(128), Ar,
                                                     start=True, stop=True, skip_group_check=True), waits=w)
            m = self.pe(lambda q, B2b=B2b, B2r=B2r: q.matmul(B2b, self.onesf_row(128), B2r,
                                                             start=True, stop=True, skip_group_check=True),
                        inc=True)
            for c in range(DC):
                w2 = [self.wP(m)] if c == 0 else []
                if g == 0 and c == 0 and xn_free:
                    w2.append(self.wP(xn_free))
                self.dve(lambda q, c=c, g=g, Ab=Ab: q.tensor_tensor(
                    self.TXN[:], self.X[c][:, g * 512:(g + 1) * 512], Ab, ALU.mult), waits=w2,
                    chain=True)
                lastv = self.dve(lambda q, c=c, g=g, B2b=B2b: q.tensor_tensor(
                    self.XN[c][:, g * 512:(g + 1) * 512], self.TXN[:], B2b, ALU.subtract),
                    chain=True)
            apply_v[g] = lastv
        self.xn_ready = apply_v[3]

    def emit_attn(self, causal, wq_d, wk_d, wv_d, wo_d, with_bias):
        PSA, PSB = self.PSA, self.PSB
        xn_ready = self.xn_ready

        gate = self.phase_gate()
        ws_cons = {-2: self.ws_free, -1: self.ws_free}
        k_evac = {}
        for m in range(DC):
            slot = m % 2
            w = [self.wP(ws_cons[m - 2])] if ws_cons[m - 2] else []
            if m == 0:
                w += gate
            wv_ = self.wld(self.WS[slot][:], wk_d[m], "WS%d" % slot, waits=w)
            ps = PSA if m % 2 == 0 else PSB
            first_w = [wv_, self.wV(xn_ready)]
            if m >= 2:
                first_w.append(self.wA(k_evac[m - 2]))
            last = 0
            for g in range(4):
                for k in range(DC):
                    last = self.pe(lambda q, ps=ps, g=g, k=k, slot=slot: q.matmul(
                        ps[:, g * 512:(g + 1) * 512], self.WS[slot][:, k * 128:(k + 1) * 128],
                        self.XN[k][:, g * 512:(g + 1) * 512], start=(k == 0), stop=(k == DC - 1),
                        skip_group_check=True),
                        inc=(g == 3 and k == DC - 1),
                        waits=(first_w if (g == 0 and k == 0) else ()))
            ws_cons[m] = last
            k_evac[m] = self.act(lambda q, m=m, ps=ps: q.activation(self.K[m][:], ps[:], AF.Copy, scale=0.125),
                                 inc=True, waits=[self.wP(last)])
        kn_done = 0
        for m in range(DC):
            wk_ = [self.wA(k_evac[m])]
            if m > 0:
                wk_.append(self.wP(kn_done))
            v = self.dve(lambda q, m=m: q.tensor_tensor(self.KSQ[:], self.K[m][:], self.K[m][:], ALU.mult),
                         waits=wk_)
            for g in range(4):
                wkn = ([self.wV(v)] if g == 0 else [])
                if m == 0 and g == 0:
                    wkn.append(self.wA(k_evac[DC - 2]))
                kn_done = self.pe(lambda q, m=m, g=g: q.matmul(
                    PSA[0:16, g * 512:(g + 1) * 512], self.cb_hind(m), self.KSQ[:, g * 512:(g + 1) * 512],
                    start=(m == 0), stop=(m == DC - 1), skip_group_check=True),
                    inc=(g == 3), waits=wkn)
        km_v = self.dve(lambda q: q.reduce_max(self.KMX[:], PSA[0:16, 0:2048], axis=AX.X),
                        inc=True, waits=[self.wP(kn_done)])

        wv_last = 0
        for k in range(DC):
            wv_last = self.wld(self.WV[:, k * 1024:(k + 1) * 1024], wv_d[k * 128:(k + 1) * 128, :],
                               "WV", waits=([self.wP(kn_done)] if k == 0 else ()))
        ones_done = 0
        for t in range(NTT):
            ones_done = self.dve(lambda q, t=t: q.memset(
                bass.AP(self.VH[t].tensor, 64, [[1040, 128], [65, 16]]), 1.0),
                inc=(t == NTT - 1), waits=([self.wV(km_v)] if t == 0 else ()))
        v_evac = {}
        for t in range(NTT):
            ps = PSA if t % 2 == 0 else PSB
            w0 = [wv_last, self.wV(km_v)]
            if t >= 2:
                w0.append(self.wV(v_evac[t - 2]))
            last = 0
            for k in range(DC):
                for hh in range(2):
                    last = self.pe(lambda q, ps=ps, t=t, k=k, hh=hh: q.matmul(
                        ps[:, hh * 512:(hh + 1) * 512], self.XN[k][:, t * 128:(t + 1) * 128],
                        self.WV[:, k * 1024 + hh * 512: k * 1024 + (hh + 1) * 512],
                        start=(k == 0), stop=(k == DC - 1), skip_group_check=True),
                        inc=(k == DC - 1 and hh == 1),
                        waits=(w0 if (k == 0 and hh == 0) else ()))
            v_evac[t] = self.dve(lambda q, t=t, ps=ps: q.tensor_copy(
                bass.AP(self.VH[t].tensor, 0, [[1040, 128], [65, 16], [1, 64]]),
                bass.AP(ps, 0, [[2048, 128], [64, 16], [1, 64]])),
                inc=True, waits=[self.wP(last), self.wV(ones_done)])
        v_done = v_evac[NTT - 1]

        if causal:
            tm_tgt = self.ld(self.TM[:], self.d_trimask, "TM", waits=[self.wV(v_done)])
        else:
            tm_tgt = None

        exp_side = {"A": 0, "B1": 0, "B2": 0}
        av_done = {"A": 0, "B1": 0, "B2": 0}
        epi_mult = {-1: 0}
        ctx_free = 0
        prev_op_done = 0
        q_last = 0
        epi_idx = [0]
        for qb in range(NQB):
            q_evac = {}
            ws_cons[-2], ws_cons[-1] = ws_cons[DC - 2], ws_cons[DC - 1]
            for m in range(DC):
                slot = m % 2
                w = [self.wP(ws_cons[m - 2])] if ws_cons[m - 2] else []
                wv_ = self.wld(self.WS[slot][:], wq_d[m], "WS%d" % slot, waits=w)
                psq = PSB[:, 0:512] if m % 2 == 0 else PSB[:, 512:1024]
                w0 = [wv_]
                if qb == 0 and m == 0:
                    w0.append(self.wV(v_done))
                    if causal:
                        w0.append(tm_tgt)
                if qb > 0 and m == 0:
                    w0.append(self.wA(exp_side["B1"]))
                if m >= 2:
                    w0.append(self.wA(q_evac[m - 2]))
                last = 0
                for k in range(DC):
                    last = self.pe(lambda q, psq=psq, k=k, slot=slot, qb=qb: q.matmul(
                        psq, self.WS[slot][:, k * 128:(k + 1) * 128],
                        self.XN[k][:, qb * 512:(qb + 1) * 512], start=(k == 0), stop=(k == DC - 1),
                        skip_group_check=True),
                        inc=(k == DC - 1), waits=(w0 if k == 0 else ()))
                ws_cons[m] = last
                q_last = last
                q_evac[m] = self.act(lambda q, m=m, psq=psq: q.activation(self.Q[m][:], psq, AF.Copy, scale=1.0),
                                     waits=[self.wP(last)])
            qn = 0
            for m in range(DC):
                wq_ = [self.wA(q_evac[m])]
                if m > 0:
                    wq_.append(self.wP(qn))
                if m == 0 and prev_op_done:
                    wq_.append(self.wV(prev_op_done))
                v = self.dve(lambda q, m=m: q.tensor_tensor(self.QSQ[:], self.Q[m][:], self.Q[m][:], ALU.mult),
                             waits=wq_)
                qn = self.pe(lambda q, m=m: q.matmul(
                    PSB[0:16, 1024:1536], self.cb_hind(m), self.QSQ[:],
                    start=(m == 0), stop=(m == DC - 1), skip_group_check=True),
                    inc=True, waits=[self.wV(v)])
            rmax_v = self.dve(lambda q: q.reduce_max(self.CBIA[0:16, 17:18], PSB[0:16, 1024:1536],
                                                     axis=AX.X), waits=[self.wP(qn)])
            vcs = self.dve(lambda q: q.tensor_tensor(self.CBIA[0:16, 18:19], self.CBIA[0:16, 17:18],
                                                     self.KMX[:], ALU.mult), chain=True)
            wnc = [self.wV(vcs)]
            if self.nc2_prev:
                wnc.append(self.nc2_prev)
            d1 = self.ld(bass.AP(self.d_ncd.tensor, 0, [[1, 16], [1, 1]]), self.CBIA[0:16, 18:19],
                         "NC1", waits=wnc)
            d2 = self.ld(self.CBIA[0:1, 0:16], bass.AP(self.d_ncd.tensor, 0, [[16, 1], [1, 16]]),
                         "NC2", waits=[d1])
            self.nc2_prev = d2
            vgm = self.dve(lambda q: q.reduce_max(self.CBIA[0:1, 19:20], self.CBIA[0:1, 0:16], axis=AX.X),
                           waits=[d2], chain=True)
            self.act(lambda q: q.activation(self.CBIA[0:1, 20:21], self.CBIA[0:1, 19:20], AF.Ln, scale=1.0),
                     waits=[self.wV(vgm)])
            ace = self.act(lambda q: q.activation(self.CBIA[0:1, 20:21], self.CBIA[0:1, 20:21], AF.Exp,
                                                  scale=0.5), chain=True)
            vng = self.dve(lambda q: q.tensor_scalar(self.CBIA[0:1, 21:22], self.CBIA[0:1, 20:21],
                                                     -1.0, None, ALU.mult), waits=[self.wA(ace)])
            wnc3 = [self.wV(vng)]
            if self.nc4_prev:
                wnc3.append(self.nc4_prev)
            d3 = self.ld(bass.AP(self.d_ncd.tensor, 16, [[1, 1], [1, 1]]), self.CBIA[0:1, 21:22],
                         "NC3", waits=wnc3)
            cb_rdy = self.ld(self.CBIA[:, 16:17], bass.AP(self.d_ncd.tensor, 16, [[0, 128], [1, 1]]),
                             "NC4", waits=[d3])
            self.nc4_prev = cb_rdy

            nkt = 4 * qb + 4 if causal else 16
            SIDES = ("A", "B1", "B2")
            GRP = {"A": (PSA, 1024), "B1": (PSB, 0), "B2": (PSB, 1024)}
            PTS = {"A": self.PTA, "B1": self.PTB[:, 0:1024], "B2": self.PTB[:, 1024:2048]}
            first_use = {"A": True, "B1": True, "B2": True}

            def emit_S(c, side, kt, first_of_qb):
                base, go = GRP[side]
                w = []
                if exp_side[side]:
                    w.append(self.wA(exp_side[side]))
                if first_of_qb:
                    if qb == 0:
                        w.append(self.wV(v_done))
                if first_use[side]:
                    first_use[side] = False
                    if side == "B2":
                        w.append(self.wV(rmax_v))
                    if side == "B1":
                        w.append(self.wA(q_evac[DC - 1]))
                pm = 0
                for hh in range(2):
                    so = go + hh * 512
                    pm = self.pe(lambda q, base=base, so=so, c=c, hh=hh, kt=kt: q.matmul(
                        base[:, so:so + 512], self.K[c][hh * 64:(hh + 1) * 64, kt * 128:(kt + 1) * 128],
                        self.Q[c][hh * 64:(hh + 1) * 64, :], start=True, stop=True,
                        skip_group_check=True),
                        inc=(hh == 1), waits=(w if hh == 0 else ()))
                mask_v = 0
                if causal:
                    r = kt - 4 * qb
                    if 0 <= r <= 3:
                        mask_v = self.dve(lambda q, base=base, go=go, r=r: q.tensor_tensor(
                            base[:, go:go + 1024], base[:, go:go + 1024],
                            bass.AP(self.TM.tensor, r * 512, [[2048, 128], [0, 2], [1, 512]]),
                            ALU.add), waits=[self.wP(pm)])
                return pm, mask_v

            def emit_E(c, side, kt, pm, mask_v, first_exp):
                base, go = GRP[side]
                ptb = PTS[side]
                we = [self.wP(pm)]
                if mask_v:
                    we.append(self.wV(mask_v))
                if av_done[side]:
                    we.append(self.wP(av_done[side]))
                if first_exp:
                    we.append(cb_rdy)
                ae = self.act(lambda q, base=base, go=go, ptb=ptb: q.activation(
                    ptb, base[:, go:go + 1024], AF.Exp, bias=self.CBIA[:, 16:17], scale=1.0),
                    waits=we)
                exp_side[side] = ae
                return ae

            def emit_AV(item):
                c, side, kt, ae = item
                ptb = PTS[side]
                last = 0
                for hh in range(2):
                    so = hh * 512
                    w = [self.wA(ae)]
                    if kt == 0 and hh == 0 and epi_mult[epi_idx[0] - 1]:
                        w.append(self.wV(epi_mult[epi_idx[0] - 1]))
                    last = self.pe(lambda q, kt=kt, c=c, hh=hh, so=so, ptb=ptb, nkt=nkt: q.matmul(
                        PSA[0:65, hh * 512: hh * 512 + 512],
                        self.VH[kt][:, (2 * c + hh) * 65:(2 * c + hh) * 65 + 65],
                        ptb[:, so:so + 512],
                        start=(kt == 0), stop=(kt == nkt - 1), skip_group_check=True),
                        inc=(hh == 1), waits=(w if hh == 0 else ()))
                av_done[side] = last
                return last

            def emit_epilogue(c, av_last, ei):
                par = ei % 2
                self.act(lambda q, par=par: q.activation(self.RROW[par][:], PSA[64:65, 0:1024],
                                                         AF.Ln, scale=1.0),
                         waits=[self.wP(av_last)])
                ra = self.act(lambda q, par=par: q.activation(self.RBS[par][0:1, :], self.RROW[par][:],
                                                              AF.Exp, scale=-1.0), chain=True)
                d1 = self.ld(self.d_rsd[par, :], self.RBS[par][0:1, :], "RS%d" % par,
                             waits=[self.wA(ra)])
                wd2 = [d1]
                if epi_mult.get(ei - 2):
                    wd2.append(self.wV(epi_mult[ei - 2]))
                d2 = self.ld(self.RBS[par][0:64, :],
                             bass.AP(self.d_rsd.tensor, par * 1024, [[0, 64], [1, 1024]]),
                             "RB%d" % par, waits=wd2)
                wctx = [d2]
                if ctx_free:
                    wctx.append(self.wP(ctx_free))
                self.dve(lambda q, c=c, par=par: q.tensor_tensor(
                    self.CTX[c][0:64, :], PSA[0:64, 0:512], self.RBS[par][0:64, 0:512], ALU.mult),
                    waits=wctx)
                epi_mult[ei] = self.dve(lambda q, c=c, par=par: q.tensor_tensor(
                    self.CTX[c][64:128, :], PSA[0:64, 512:1024], self.RBS[par][0:64, 512:1024],
                    ALU.mult), chain=True)

            groups = []
            for c in range(DC):
                for kt in range(nkt):
                    groups.append((c, SIDES[(c * nkt + kt) % 3], kt))
            fifo = []
            first_exp_of_qb = True
            for gi, (c, side, kt) in enumerate(groups):
                pm, mask_v = emit_S(c, side, kt, first_of_qb=(gi == 0))
                ae = emit_E(c, side, kt, pm, mask_v, first_exp_of_qb)
                first_exp_of_qb = False
                fifo.append((c, side, kt, ae))
                while len(fifo) > 2:
                    item = fifo.pop(0)
                    av_last = emit_AV(item)
                    if item[2] == nkt - 1:
                        ei = epi_idx[0]
                        emit_epilogue(item[0], av_last, ei)
                        epi_idx[0] = ei + 1
            while fifo:
                item = fifo.pop(0)
                av_last = emit_AV(item)
                if item[2] == nkt - 1:
                    ei = epi_idx[0]
                    emit_epilogue(item[0], av_last, ei)
                    epi_idx[0] = ei + 1
            ctx_done = epi_mult[epi_idx[0] - 1]

            op_v = {}
            ws_cons[-2], ws_cons[-1] = ws_cons[DC - 2], ws_cons[DC - 1]
            for m in range(DC):
                slot = m % 2
                w = [self.wP(ws_cons[m - 2])] if ws_cons[m - 2] else []
                wv_ = self.wld(self.WS[slot][:], wo_d[m], "WS%d" % slot, waits=w)
                op = PSB[:, 1024:1536] if m % 2 == 0 else PSB[:, 1536:2048]
                w0 = [wv_]
                if m == 0:
                    w0 += [self.wV(ctx_done), self.wA(exp_side["B2"])]
                if m >= 2:
                    w0.append(self.wV(op_v[m - 2]))
                last = 0
                for k in range(DC):
                    last = self.pe(lambda q, op=op, k=k, slot=slot: q.matmul(
                        op, self.WS[slot][:, k * 128:(k + 1) * 128], self.CTX[k][:],
                        start=(k == 0), stop=(k == DC - 1), skip_group_check=True),
                        inc=(k == DC - 1), waits=(w0 if k == 0 else ()))
                ws_cons[m] = last
                bias_col = self.b_bo(m) if with_bias else self.b_z
                wst = [self.wP(last)] + (self.x_dep if (qb == 0 and m == 0) else [])
                op_v[m] = self.dve(lambda q, m=m, qb=qb, op=op, bias_col=bias_col: q.scalar_tensor_tensor(
                    self.X[m][:, qb * 512:(qb + 1) * 512], op, bias_col,
                    self.X[m][:, qb * 512:(qb + 1) * 512], ALU.add, ALU.add),
                    waits=wst)
            ctx_free = ws_cons[DC - 1]
            prev_op_done = op_v[DC - 1]
        self.ws_free = ws_cons[DC - 1]
        self.x_dep = [self.wV(self.C["V"])]
        self.xn_free = q_last

    def emit_ffn(self, w0_d, w1_d, w2_d, gw_d):
        PSA, PSB = self.PSA, self.PSB
        xn_ready = self.xn_ready
        wg_tgt = self.wld(self.WG[:], gw_d, "WG", waits=self.phase_gate())
        gate_v = {}
        mt_v = 0
        for tt in range(NTT):
            gb = PSA[:, 0:8] if tt % 2 == 0 else PSA[:, 512:520]
            w0 = [wg_tgt, self.wV(xn_ready)] if tt == 0 else []
            if tt >= 2:
                w0 = [self.wV(gate_v[tt - 2])]
            last = 0
            for dc in range(DC):
                last = self.pe(lambda q, gb=gb, dc=dc, tt=tt: q.matmul(
                    gb, self.XN[dc][:, tt * 128:(tt + 1) * 128], self.WG[:, dc * 8:(dc + 1) * 8],
                    start=(dc == 0), stop=(dc == DC - 1), skip_group_check=True),
                    inc=(dc == DC - 1), waits=(w0 if dc == 0 else ()))
            self.dve(lambda q, gb=gb: q.tensor_tensor(self.GSC[:], gb, self.b_gb, ALU.add),
                     waits=[self.wP(last)], chain=True)
            self.dve(lambda q: q.reduce_max(self.GCV[:, 0:1], self.GSC[:], axis=AX.X), chain=True)
            self.dve(lambda q: q.tensor_scalar(self.GC1[:], self.GSC[:], self.GCV[:, 0:1], None, ALU.is_equal),
                     chain=True)
            self.dve(lambda q: q.scalar_tensor_tensor(self.GC2[:], self.GC1[:], NEG, self.GSC[:],
                                                      ALU.mult, ALU.add), chain=True)
            self.dve(lambda q: q.reduce_max(self.GCV[:, 1:2], self.GC2[:], axis=AX.X), chain=True)
            self.dve(lambda q: q.tensor_scalar(self.GC2[:], self.GC2[:], self.GCV[:, 1:2], None, ALU.is_equal),
                     chain=True)
            vd = self.dve(lambda q: q.tensor_tensor(self.GCV[:, 2:3], self.GCV[:, 1:2], self.GCV[:, 0:1],
                                                    ALU.subtract), chain=True)
            asg = self.act(lambda q: q.activation(self.GCV[:, 3:4], self.GCV[:, 2:3], AF.Sigmoid, scale=1.0),
                           waits=[self.wV(vd)])
            self.dve(lambda q: q.tensor_scalar(self.GCV[:, 2:3], self.GCV[:, 3:4], -1.0, 1.0,
                                               ALU.mult, ALU.add), waits=[self.wA(asg)], chain=True)
            self.dve(lambda q: q.tensor_scalar(self.GC1[:], self.GC1[:], self.GCV[:, 2:3], None, ALU.mult),
                     chain=True)
            vm = self.dve(lambda q: q.scalar_tensor_tensor(self.MSC[:], self.GC2[:], self.GCV[:, 3:4],
                                                           self.GC1[:], ALU.mult, ALU.add), chain=True)
            mt_m = self.pe(lambda q: q.matmul(PSA[0:8, 1024:1152], self.MSC[:], self.cb_id,
                                              start=True, stop=True, skip_group_check=True),
                           inc=True, waits=[self.wV(vm)] + ([self.wV(mt_v)] if tt >= 1 else []))
            mt_v = self.dve(lambda q, tt=tt: q.tensor_copy(self.MT[0:8, tt * 128:(tt + 1) * 128],
                                                           PSA[0:8, 1024:1152]),
                            inc=True, waits=[self.wP(mt_m)])
            gate_v[tt] = mt_v
        gate_done = mt_v

        w01_cons = {-2: self.w01_free, -1: self.w01_free}
        xn_last = 0
        last_stt = dict(self.prev_ffn_stt)
        last_cmb = self.prev_cmb
        for e in range(E):
            if e > 0:
                w01_cons[-2] = w01_cons[HC - 2]
                w01_cons[-1] = w01_cons[HC - 1]
            act_v = {}
            for hh in range(HC):
                slot = hh % 2
                w = [self.wP(w01_cons[hh - 2])] if w01_cons[hh - 2] else []
                wa = self.wld(self.W01[slot][:, 0:1024], w0_d[e, hh], "W01%d" % slot, waits=w)
                wb = self.wld(self.W01[slot][:, 1024:2048], w1_d[e, hh], "W01%d" % slot)
                for half in range(2):
                    hs = half * 1024
                    w0 = [wb]
                    if e == 0 and hh == 0 and half == 0:
                        w0.append(self.wV(gate_done))
                    if hh == 0 and last_cmb:
                        w0.append(self.wV(last_cmb))
                    if act_v.get((hh - 1, half)):
                        w0.append(self.wV(act_v[(hh - 1, half)]))
                    last = 0
                    for dc in range(DC):
                        for sub in range(2):
                            ss = hs + sub * 512
                            self.pe(lambda q, ss=ss, dc=dc, slot=slot: q.matmul(
                                PSA[:, ss:ss + 512], self.W01[slot][:, dc * 128:(dc + 1) * 128],
                                self.XN[dc][:, ss:ss + 512], start=(dc == 0), stop=(dc == DC - 1),
                                skip_group_check=True),
                                waits=(w0 if (dc == 0 and sub == 0) else ()))
                            last = self.pe(lambda q, ss=ss, dc=dc, slot=slot: q.matmul(
                                PSB[:, ss:ss + 512], self.W01[slot][:, 1024 + dc * 128: 1024 + (dc + 1) * 128],
                                self.XN[dc][:, ss:ss + 512], start=(dc == 0), stop=(dc == DC - 1),
                                skip_group_check=True),
                                inc=(dc == DC - 1 and sub == 1))
                    if half == 1:
                        w01_cons[hh] = last
                    xn_last = last
                    wsl = [self.wP(last)]
                    prev_stt = act_v.get((hh - 1, half), 0) or last_stt.get(half, 0)
                    if prev_stt:
                        wsl.append(self.wV(prev_stt))
                    a = self.act(lambda q, hs=hs, e=e, hh=hh: q.activation(
                        self.SIL[:, hs:hs + 1024], PSB[:, hs:hs + 1024], AF.Silu,
                        bias=self.b_b1(e, hh), scale=1.0), waits=wsl)
                    act_v[(hh, half)] = self.dve(lambda q, hs=hs, e=e, hh=hh: q.scalar_tensor_tensor(
                        self.ACT[hh][:, hs:hs + 1024], PSA[:, hs:hs + 1024], self.b_b0(e, hh),
                        self.SIL[:, hs:hs + 1024], ALU.add, ALU.mult), waits=[self.wA(a)])
            act_done = act_v[(HC - 1, 1)]
            mb_a = 0
            for ts in range(4):
                w0 = [self.wV(act_done)] if ts == 0 else [self.wA(mb_a)]
                if ts == 0 and last_cmb:
                    w0.append(self.wV(last_cmb))
                mb_m = self.pe(lambda q, ts=ts, e=e: q.matmul(
                    PSB[:, 0:512], self.cb_sel(e), self.MT[0:8, ts * 512:(ts + 1) * 512],
                    start=True, stop=True, skip_group_check=True), inc=True, waits=w0)
                mb_a = self.act(lambda q, ts=ts: q.activation(self.MBS[:, ts * 512:(ts + 1) * 512],
                                                              PSB[:, 0:512], AF.Copy, scale=1.0),
                                inc=True, waits=[self.wP(mb_m)])
            if e == 0:
                self._w2carry = [self.w2b_free, self.w2b_free]
            w2_cons = {-2: self._w2carry[0], -1: self._w2carry[1]}
            cmb_v = {}
            for dc in range(DC):
                slot = dc % 2
                w = [self.wP(w2_cons[dc - 2])] if w2_cons[dc - 2] else []
                wv_ = self.wld(self.W2B[slot][:], w2_d[e, dc], "W2B%d" % slot, waits=w)
                for ts in range(4):
                    op = PSA[:, ts * 512:(ts + 1) * 512]
                    w0 = [wv_]
                    if dc == 0 and ts == 0:
                        w0.append(self.wA(mb_a))
                    if cmb_v.get((dc - 1, ts)):
                        w0.append(self.wV(cmb_v[(dc - 1, ts)]))
                    last = 0
                    for hc in range(HC):
                        last = self.pe(lambda q, op=op, hc=hc, ts=ts, slot=slot: q.matmul(
                            op, self.W2B[slot][:, hc * 128:(hc + 1) * 128],
                            self.ACT[hc][:, ts * 512:(ts + 1) * 512],
                            start=(hc == 0), stop=(hc == HC - 1), skip_group_check=True),
                            inc=(hc == HC - 1), waits=(w0 if hc == 0 else ()))
                    if ts == 3:
                        w2_cons[dc] = last
                    wst = [self.wP(last)]
                    if e == 0 and dc == 0 and ts == 0:
                        wst += self.x_dep
                    self.dve(lambda q, op=op, e=e, dc=dc, ts=ts: q.scalar_tensor_tensor(
                        self.TMPF[:], op, self.b_b2(e, dc), self.MBS[:, ts * 512:(ts + 1) * 512],
                        ALU.add, ALU.mult), waits=wst, chain=True)
                    cmb_v[(dc, ts)] = self.dve(lambda q, dc=dc, ts=ts: q.tensor_tensor(
                        self.X[dc][:, ts * 512:(ts + 1) * 512], self.X[dc][:, ts * 512:(ts + 1) * 512],
                        self.TMPF[:], ALU.add), chain=True)
            self._w2carry = [w2_cons[DC - 2], w2_cons[DC - 1]]
            last_cmb = cmb_v[(DC - 1, 3)]
            last_stt = {0: act_v[(HC - 1, 0)], 1: act_v[(HC - 1, 1)]}
        self.w01_free = w01_cons[HC - 1]
        self.w2b_free = w2_cons[DC - 1]
        self.prev_cmb = last_cmb
        self.prev_ffn_stt = dict(last_stt)
        self.x_dep = [self.wV(self.C["V"])]
        self.xn_free = xn_last

    def emit_all(self):
        self.emit_prologue()
        store_done = None
        for s in range(NSEQ):
            self.emit_load_x(s, after=([store_done] if s else ()))
            subs = ["attn_a", "ffn", "attn_b", "ffn"][: self.n_sub]
            for sub in subs:
                self.emit_ln()
                if sub == "attn_a":
                    self.emit_attn(True, self.d_wq, self.d_wk, self.d_wv, self.d_wo, True)
                elif sub == "attn_b":
                    self.emit_attn(False, self.d_uq, self.d_uk, self.d_uv, self.d_uo, False)
                else:
                    self.emit_ffn(self.d_w0t, self.d_w1t, self.d_w2t, self.d_gwt)
            store_done = self.emit_store_x(s)
            self.xn_free = 0
            self.xn_free = self.C["P"] and 0
        self.final_out = store_done

    def finalize(self):
        nc = self.nc
        OPS = self.OPS
        final_out = self.final_out
        with nc.Block() as block:
            @block.sync
            def _(q):
                for fn in OPS["sync"]:
                    fn(q)
                q.wait_ge(final_out[0], final_out[1])

            @block.gpsimd
            def _(q):
                for fn in OPS["gp"]:
                    fn(q)

            @block.tensor
            def _(q):
                for fn in OPS["pe"]:
                    fn(q)

            @block.scalar
            def _(q):
                for fn in OPS["act"]:
                    fn(q)

            @block.vector
            def _(q):
                for fn in OPS["dve"]:
                    fn(q)
        for cm in reversed(self._ctx):
            cm.__exit__(None, None, None)
        return nc


def build_program(n_sub=4):
    b = B(n_sub)
    b.emit_all()
    return b.finalize()


def kernel(**inputs):
    ins = {k: np.ascontiguousarray(np.asarray(v, dtype=np.float32)) for k, v in inputs.items()}
    blobs = _host_blobs(ins)
    x = ins["L0_attn_a_x"]
    nc = build_program(4)
    shared = {k: blobs[k] for k in ["wq", "wk", "wv", "wo", "uq", "uk", "uv", "uo",
                                    "w0t", "w1t", "w2t", "gwt", "biasblob", "constsb", "trimask"]}
    in_maps = []
    for c in range(NCORE):
        xt = np.ascontiguousarray(x[c * NSEQ:(c + 1) * NSEQ].transpose(0, 2, 1))
        m = dict(shared)
        m["xt"] = xt
        in_maps.append(m)
    res = run_bass_kernel_spmd(nc, in_maps, core_ids=list(range(NCORE)))
    outs = []
    for c in range(NCORE):
        o = res.results[c]["out"]
        outs.append(np.asarray(o).transpose(0, 2, 1))
    return np.ascontiguousarray(np.concatenate(outs, axis=0))


# version 06
# speedup vs baseline: 1.0334x; 1.0334x over previous
import numpy as np
import concourse.bass as bass
import concourse.mybir as mybir
from concourse.bass_utils import run_bass_kernel_spmd

F32 = mybir.dt.float32
BF16 = mybir.dt.bfloat16
AF = mybir.ActivationFunctionType
ALU = mybir.AluOpType
AX = mybir.AxisListType

D, T, H, HD, E, HF = 1024, 2048, 16, 64, 8, 2048
DC, HC, NTT, NQB = 8, 16, 16, 4
NCORE, NSEQ = 8, 2
NEG = -1e30

BIAS_OFF = {"b0": 0, "b1": 128, "b2": 256, "bo": 320, "gb": 328, "z": 336,
            "ones": 337, "id16": 465, "id128": 481, "fc": 609}
EPS_BIAS = float(1024.0 * 1024.0 * 1e-5)
LN1024 = float(np.log(1024.0))
FCONSTS = [EPS_BIAS, 1024.0, -1.0, 1.0, -1e30, 2.0, 0.0, 0.125, LN1024, 0.5, -0.5]
BIAS_W = 609 + len(FCONSTS)
CB_W = 1408


def _host_blobs(ins):
    f = np.float32
    import ml_dtypes
    out = {}

    def mtile(W):
        return np.ascontiguousarray(W.reshape(8, 128, 8, 128).transpose(2, 1, 0, 3)).reshape(8, 128, 1024)

    out["wq"] = mtile(ins["L0_attn_a_Wq"])
    out["wk"] = mtile(ins["L0_attn_a_Wk"])
    out["wo"] = mtile(ins["L0_attn_a_Wo"])
    out["wv"] = np.ascontiguousarray(ins["L0_attn_a_Wv"])
    uq = np.ascontiguousarray(ins["L0_attn_b_W_q"].transpose(1, 0, 2).reshape(1024, 1024))
    uk = np.ascontiguousarray(ins["L0_attn_b_W_k"].transpose(1, 0, 2).reshape(1024, 1024))
    uv = np.ascontiguousarray(ins["L0_attn_b_W_v"].transpose(1, 0, 2).reshape(1024, 1024))
    out["uq"] = mtile(uq)
    out["uk"] = mtile(uk)
    out["uo"] = mtile(ins["L0_attn_b_W_o"])
    out["uv"] = uv
    out["w0t"] = np.ascontiguousarray(
        ins["L0_ffn_w0"].reshape(8, 16, 128, 8, 128).transpose(0, 1, 4, 3, 2)).reshape(8, 16, 128, 1024)
    out["w1t"] = np.ascontiguousarray(
        ins["L0_ffn_w1"].reshape(8, 16, 128, 8, 128).transpose(0, 1, 4, 3, 2)).reshape(8, 16, 128, 1024)
    out["w2t"] = np.ascontiguousarray(
        ins["L0_ffn_w2"].reshape(8, 8, 128, 16, 128).transpose(0, 1, 4, 3, 2)).reshape(8, 8, 128, 2048)
    out["gwt"] = np.ascontiguousarray(
        ins["L0_ffn_gate_w"].T.reshape(8, 128, 8).transpose(1, 0, 2)).reshape(128, 64)

    b0c = np.ascontiguousarray(ins["L0_ffn_b0"].reshape(8, 16, 128).transpose(2, 0, 1)).reshape(128, 128)
    b1c = np.ascontiguousarray(ins["L0_ffn_b1"].reshape(8, 16, 128).transpose(2, 0, 1)).reshape(128, 128)
    b2c = np.ascontiguousarray(ins["L0_ffn_b2"].reshape(8, 8, 128).transpose(2, 0, 1)).reshape(128, 64)
    boc = np.ascontiguousarray(ins["L0_attn_a_bo"].reshape(8, 128).T)
    gbb = np.broadcast_to(ins["L0_ffn_gate_b"].astype(f), (128, 8))
    zeros = np.zeros((128, 1), f)
    onesf = np.ones((128, 128), f)
    idf16 = np.zeros((128, 16), f)
    idf16[:16, :] = np.eye(16, dtype=f)
    idf128 = np.eye(128, dtype=f)
    fcc = np.broadcast_to(np.asarray(FCONSTS, f)[None, :], (128, len(FCONSTS)))
    blob = np.concatenate([b0c, b1c, b2c, boc, gbb, zeros, onesf, idf16, idf128, fcc],
                          axis=1).astype(f)
    assert blob.shape[1] == BIAS_W, blob.shape
    out["biasblob"] = np.ascontiguousarray(blob)

    idn = np.eye(128, dtype=f)
    hind = np.zeros((128, 128), f)
    for c in range(8):
        for p in range(128):
            hind[p, c * 16 + (2 * c + p // 64)] = 1.0
    onesb = np.ones((128, 128), f)
    sel = np.zeros((128, 8 * 128), f)
    for e in range(8):
        sel[e, e * 128:(e + 1) * 128] = 1.0
    cb = np.concatenate([idn, hind, onesb, sel], axis=1)
    assert cb.shape[1] == CB_W
    out["constsb"] = np.ascontiguousarray(cb.astype(ml_dtypes.bfloat16))

    tm = np.full((128, 2048), NEG, f)
    for r in range(4):
        for p in range(128):
            j0 = r * 128 + p
            if j0 < 512:
                tm[p, r * 512 + j0: (r + 1) * 512] = 0.0
    out["trimask"] = tm.astype(ml_dtypes.bfloat16)
    return out


class B:
    def __init__(self, n_sub=4):
        self.n_sub = n_sub
        nc = bass.Bass("TRN2", target_bir_lowering=False)
        self.nc = nc
        self._ctx = []
        self.d_xt = nc.dram_tensor("xt", [NSEQ, D, T], F32, kind="ExternalInput").ap()
        self.d_out = nc.dram_tensor("out", [NSEQ, D, T], F32, kind="ExternalOutput").ap()
        for n, shp in [("wq", [8, 128, 1024]), ("wk", [8, 128, 1024]), ("wo", [8, 128, 1024]),
                       ("wv", [1024, 1024]), ("uq", [8, 128, 1024]), ("uk", [8, 128, 1024]),
                       ("uo", [8, 128, 1024]), ("uv", [1024, 1024]),
                       ("w0t", [8, 16, 128, 1024]), ("w1t", [8, 16, 128, 1024]),
                       ("w2t", [8, 8, 128, 2048]), ("gwt", [128, 64])]:
            setattr(self, "d_" + n, nc.dram_tensor(n, shp, F32, kind="ExternalInput").ap())
        self.d_trimask = nc.dram_tensor("trimask", [128, 2048], BF16, kind="ExternalInput").ap()
        self.d_bias = nc.dram_tensor("biasblob", [128, BIAS_W], F32, kind="ExternalInput").ap()
        self.d_rsd = nc.dram_tensor("rsd", [2, 1024], BF16, kind="Internal").ap()
        self.d_ncd = nc.dram_tensor("ncd", [32], F32, kind="Internal").ap()
        self.d_cb = nc.dram_tensor("constsb", [128, CB_W], BF16, kind="ExternalInput").ap()

        off = [16512]

        def alloc(name, shape, dt, at=None):
            nbytes = int(np.prod(shape[1:])) * (2 if dt == BF16 else 4)
            if at is None:
                at = off[0]
                off[0] = (at + nbytes + 31) & ~31
            return nc.alloc_sbuf_tensor_at(name, shape, dt, offset=at).ap()

        self.X = [alloc(f"x{c}", [128, 2048], F32) for c in range(DC)]
        self.XN = [alloc(f"xn{c}", [128, 2048], BF16) for c in range(DC)]
        self.BIAS = alloc("bias", [128, BIAS_W], F32)
        self.CB = alloc("cb", [128, CB_W], BF16)
        self.KMX = alloc("kmx", [16, 1], F32)
        AR0 = off[0]
        self.AR0 = AR0
        assert 229344 - AR0 >= 109000, (AR0, 229344 - AR0)

        def ar(name, shape, dt, o):
            return alloc(name, shape, dt, at=AR0 + o)

        self.K = [ar(f"k{c}", [128, 2048], BF16, c * 4096) for c in range(DC)]
        self.VH = [ar(f"vh{t}", [128, 1040], BF16, 32768 + t * 2080) for t in range(NTT)]
        self.Q = [ar(f"q{c}", [128, 512], BF16, 66048 + c * 1024) for c in range(DC)]
        self.CTX = [ar(f"ctx{c}", [128, 512], BF16, 74240 + c * 1024) for c in range(DC)]
        self.PTB = ar("ptb", [128, 2048], BF16, 82432)
        self.WS = [ar(f"ws{i}", [128, 1024], BF16, 86528 + i * 2048) for i in range(2)]
        self.WV = ar("wvs", [128, 8 * 1024], BF16, 90624)
        self.TM = ar("tm", [128, 2048], BF16, 90624)
        self.RBS = [ar("rbs0", [64, 1024], BF16, 94720),
                    ar("rbs1", [64, 1024], BF16, 96768)]
        self.RROW = ar("rrow", [1, 1024], F32, 98816)
        self.CTXRAW = [ar("ctxraw0", [66, 1024], BF16, 102912),
                       ar("ctxraw1", [66, 1024], BF16, 104960)]
        self.KSQ = ar("ksq", [128, 2048], BF16, 66048)
        self.PTA = ar("pta", [128, 1024], BF16, 107008)
        self.QSQ = ar("qsq", [128, 512], BF16, 107008)
        self.CBIA = ar("cbia", [128, 32], F32, 109056)

        self.ACT = [ar(f"act{hh}", [128, 2048], BF16, hh * 4096) for hh in range(HC)]
        self.W01 = [ar(f"w01{i}", [128, 2048], BF16, 65536 + i * 4096) for i in range(2)]
        self.W2B = [ar(f"w2b{i}", [128, 2048], BF16, 73728 + i * 4096) for i in range(2)]
        self.SIL = ar("sil", [128, 2048], F32, 81920)
        self.MBS = ar("mbs", [128, 2048], F32, 90112)
        self.MT = ar("mt", [8, 2048], BF16, 98304)
        self.WG = ar("wg", [128, 64], BF16, 102400)
        self.GSC = ar("gsc", [128, 8], F32, 102528)
        self.GC1 = ar("gc1", [128, 8], F32, 102560)
        self.GC2 = ar("gc2", [128, 8], F32, 102592)
        self.GCV = ar("gcv", [128, 8], F32, 102624)
        self.MSC = ar("msc", [128, 8], BF16, 102656)
        self.TMPF = ar("tmpf", [128, 512], F32, 102688)

        self.XB = [ar(f"xb{i}", [128, 2048], BF16, i * 4096) for i in range(2)]
        self.XQ = [ar(f"xq{i}", [128, 2048], BF16, 8192 + i * 4096) for i in range(2)]
        self.LNA = ar("lna", [1, 2048], F32, 16384)
        self.LNB = ar("lnb", [1, 2048], F32, 24576)
        self.LNS = ar("lns", [1, 1024], F32, 32768)
        self.TXN = ar("txn", [128, 512], F32, 36864)

        self.PSA = self._cm(nc.psum_tensor("psa", [128, 2048], F32))
        self.PSB = self._cm(nc.psum_tensor("psb", [128, 2048], F32))

        self.DSEM = {}
        self.DCNT = {}
        for nm in ["C", "XL", "TM", "O", "WS0", "WS1", "WV", "W010", "W011",
                   "W2B0", "W2B1", "WG", "RS0", "RS1", "RB0", "RB1", "NC1", "NC2", "NC3", "NC4"]:
            self.DSEM[nm] = self._cm(nc.semaphore("s" + nm))
            self.DCNT[nm] = 0
        self.sP = self._cm(nc.semaphore("sP"))
        self.sA = self._cm(nc.semaphore("sA"))
        self.sV = self._cm(nc.semaphore("sV"))
        self.C = {"P": 0, "A": 0, "V": 0}
        self.OPS = {"sync": [], "gp": [], "pe": [], "act": [], "dve": []}

        bo = BIAS_OFF
        self.b_b0 = lambda e, hh: self.BIAS[:, bo["b0"] + e * 16 + hh: bo["b0"] + e * 16 + hh + 1]
        self.b_b1 = lambda e, hh: self.BIAS[:, bo["b1"] + e * 16 + hh: bo["b1"] + e * 16 + hh + 1]
        self.b_b2 = lambda e, dc: self.BIAS[:, bo["b2"] + e * 8 + dc: bo["b2"] + e * 8 + dc + 1]
        self.b_bo = lambda dc: self.BIAS[:, bo["bo"] + dc: bo["bo"] + dc + 1]
        self.b_gb = self.BIAS[:, bo["gb"]: bo["gb"] + 8]
        self.b_z = self.BIAS[:, bo["z"]: bo["z"] + 1]
        self.onesf_row = lambda n: self.BIAS[0:1, bo["ones"]: bo["ones"] + n]
        self.id16 = self.BIAS[0:16, bo["id16"]: bo["id16"] + 16]
        for j, v in enumerate(FCONSTS):
            nc.const_aps.register(F32, v, self.BIAS[:, bo["fc"] + j: bo["fc"] + j + 1])
        self.cb_id = self.CB[:, 0:128]
        self.cb_hind = lambda c: self.CB[:, 128 + c * 16: 128 + c * 16 + 16]
        self.cb_onecol = self.CB[:, 256:257]
        self.cb_sel = lambda e: self.CB[0:8, 384 + e * 128: 384 + (e + 1) * 128]

        self.x_dep = []
        self.xn_free = 0
        self.xn_ready = 0
        self.ws_free = 0
        self.w01_free = 0
        self.w2b_free = 0
        self.prev_cmb = 0
        self.prev_ffn_stt = {0: 0, 1: 0}
        self.nc2_prev = None
        self.nc4_prev = None

    def _cm(self, cm):
        v = cm.__enter__()
        self._ctx.append(cm)
        return v

    def wP(self, v): return (self.sP, v)
    def wA(self, v): return (self.sA, v)
    def wV(self, v): return (self.sV, v)

    def _emit(self, eng, fn, inc, waits, chain=False):
        sem = {"pe": self.sP, "act": self.sA, "dve": self.sV}[eng]
        cname = {"pe": "P", "act": "A", "dve": "V"}[eng]
        if chain:
            waits = list(waits) + [(sem, self.C[cname])]
        for s_, v_ in waits:
            if v_ > 0:
                self.OPS[eng].append(lambda q, s=s_, v=v_: q.wait_ge(s, v))
        if inc:
            self.C[cname] += 1
            self.OPS[eng].append(lambda q, fn=fn, sem=sem: fn(q).then_inc(sem, 1))
        else:
            self.OPS[eng].append(lambda q, fn=fn: fn(q))
        return self.C[cname]

    def pe(self, fn, inc=False, waits=(), chain=False):
        return self._emit("pe", fn, inc, waits, chain)

    def act(self, fn, inc=True, waits=(), chain=False):
        return self._emit("act", fn, True, waits, chain)

    def dve(self, fn, inc=True, waits=(), chain=False):
        return self._emit("dve", fn, True, waits, chain)

    def ld(self, dst, src, sem, waits=()):
        for s_, v_ in waits:
            if v_ > 0:
                self.OPS["sync"].append(lambda q, s=s_, v=v_: q.wait_ge(s, v))
        self.DCNT[sem] += 16
        sh = self.DSEM[sem]
        self.OPS["sync"].append(
            lambda q, dst=dst, src=src, sh=sh: q.dma_start(out=dst, in_=src).then_inc(sh, 16))
        return (sh, self.DCNT[sem])

    def wld(self, dst, src, sem, waits=()):
        for s_, v_ in waits:
            if v_ > 0:
                self.OPS["gp"].append(lambda q, s=s_, v=v_: q.wait_ge(s, v))
        self.DCNT[sem] += 16
        sh = self.DSEM[sem]
        self.OPS["gp"].append(
            lambda q, dst=dst, src=src, sh=sh: q.dma_start(out=dst, in_=src).then_inc(sh, 16))
        return (sh, self.DCNT[sem])

    def phase_gate(self):
        return [self.wP(self.C["P"]), self.wA(self.C["A"]), self.wV(self.C["V"])]

    def emit_prologue(self):
        self.ld(self.BIAS[:], self.d_bias, "C")
        self.consts_ready = self.ld(self.CB[:], self.d_cb, "C")

    def emit_load_x(self, s, after=()):
        for c in range(DC):
            t = self.ld(self.X[c][:], self.d_xt[s, c * 128:(c + 1) * 128, :], "XL",
                        waits=(after if c == 0 else ()))
        self.x_dep = [t]

    def emit_store_x(self, s):
        for c in range(DC):
            t = self.ld(self.d_out[s, c * 128:(c + 1) * 128, :], self.X[c][:], "O",
                        waits=(self.x_dep if c == 0 else ()))
        return t

    def emit_ln(self):
        PSA, PSB = self.PSA, self.PSB
        x_dep = list(self.x_dep)
        xn_free = self.xn_free
        a_xb, v_xq = [], []
        stats_m = {}
        for c in range(DC):
            i = c % 2
            w = []
            if c == 0:
                w += x_dep + [self.consts_ready]
            if c >= 2:
                w.append(self.wP(stats_m[c - 2]))
            a = self.act(lambda q, c=c, i=i: q.activation(self.XB[i][:], self.X[c][:], AF.Copy, scale=1.0),
                         inc=True, waits=w)
            a_xb.append(a)
            v = self.dve(lambda q, i=i: q.tensor_tensor(self.XQ[i][:], self.XB[i][:], self.XB[i][:], ALU.mult),
                         inc=True, waits=[self.wA(a)])
            v_xq.append(v)
            for g in range(4):
                self.pe(lambda q, i=i, g=g, c=c: q.matmul(
                    PSA[0:1, g * 512:(g + 1) * 512], self.cb_onecol, self.XB[i][:, g * 512:(g + 1) * 512],
                    start=(c == 0), stop=(c == DC - 1), skip_group_check=True),
                    waits=([self.wA(a)] if g == 0 else ()))
            for g in range(4):
                m = self.pe(lambda q, i=i, g=g, c=c: q.matmul(
                    PSB[0:1, g * 512:(g + 1) * 512], self.cb_onecol, self.XQ[i][:, g * 512:(g + 1) * 512],
                    start=(c == 0), stop=(c == DC - 1), skip_group_check=True),
                    inc=(g == 3), waits=([self.wV(v)] if g == 0 else ()))
            stats_m[c] = m
        stats_done = stats_m[DC - 1]
        vrows = 0
        for g in range(4):
            S1 = PSA[0:1, g * 512:(g + 1) * 512]
            S2 = PSB[0:1, g * 512:(g + 1) * 512]
            mmr = self.LNS[0:1, 0:512]
            rr = self.LNS[0:1, 512:1024]
            Ar = self.LNA[0:1, g * 512:(g + 1) * 512]
            B2r = self.LNB[0:1, g * 512:(g + 1) * 512]
            asq = self.act(lambda q, S1=S1, mmr=mmr: q.activation(mmr, S1, AF.Square, scale=1.0),
                           waits=([self.wP(stats_done)] if g == 0 else ()), chain=True)
            v0 = self.dve(lambda q, S2=S2, mmr=mmr: q.scalar_tensor_tensor(
                mmr, S2, 1024.0, mmr, ALU.mult, ALU.subtract), inc=True, waits=[self.wA(asq)])
            self.act(lambda q, mmr=mmr: q.activation(mmr, mmr, AF.Ln, bias=EPS_BIAS, scale=1.0),
                     waits=[self.wV(v0)], chain=True)
            a = self.act(lambda q, mmr=mmr, rr=rr: q.activation(rr, mmr, AF.Exp, scale=-0.5),
                         chain=True)
            self.dve(lambda q, rr=rr, Ar=Ar: q.tensor_scalar(Ar, rr, 1024.0, None, ALU.mult),
                     waits=[self.wA(a)])
            vrows = self.dve(lambda q, S1=S1, rr=rr, B2r=B2r: q.tensor_tensor(B2r, S1, rr, ALU.mult),
                             inc=True)
        apply_v = {}
        for g in range(4):
            Ab = PSB[:, 0:512] if g % 2 == 0 else PSB[:, 1024:1536]
            B2b = PSB[:, 512:1024] if g % 2 == 0 else PSB[:, 1536:2048]
            Ar = self.LNA[0:1, g * 512:(g + 1) * 512]
            B2r = self.LNB[0:1, g * 512:(g + 1) * 512]
            w = [self.wV(vrows)] if g < 2 else [self.wV(apply_v[g - 2])]
            self.pe(lambda q, Ab=Ab, Ar=Ar: q.matmul(Ab, self.onesf_row(128), Ar,
                                                     start=True, stop=True, skip_group_check=True), waits=w)
            m = self.pe(lambda q, B2b=B2b, B2r=B2r: q.matmul(B2b, self.onesf_row(128), B2r,
                                                             start=True, stop=True, skip_group_check=True),
                        inc=True)
            for c in range(DC):
                w2 = [self.wP(m)] if c == 0 else []
                if g == 0 and c == 0 and xn_free:
                    w2.append(self.wP(xn_free))
                self.dve(lambda q, c=c, g=g, Ab=Ab: q.tensor_tensor(
                    self.TXN[:], self.X[c][:, g * 512:(g + 1) * 512], Ab, ALU.mult), waits=w2,
                    chain=True)
                lastv = self.dve(lambda q, c=c, g=g, B2b=B2b: q.tensor_tensor(
                    self.XN[c][:, g * 512:(g + 1) * 512], self.TXN[:], B2b, ALU.subtract),
                    chain=True)
            apply_v[g] = lastv
        self.xn_ready = apply_v[3]

    def emit_attn(self, causal, wq_d, wk_d, wv_d, wo_d, with_bias):
        PSA, PSB = self.PSA, self.PSB
        xn_ready = self.xn_ready

        gate = self.phase_gate()
        ws_cons = {-2: self.ws_free, -1: self.ws_free}
        k_evac = {}
        for m in range(DC):
            slot = m % 2
            w = [self.wP(ws_cons[m - 2])] if ws_cons[m - 2] else []
            if m == 0:
                w += gate
            wv_ = self.wld(self.WS[slot][:], wk_d[m], "WS%d" % slot, waits=w)
            ps = PSA if m % 2 == 0 else PSB
            first_w = [wv_, self.wV(xn_ready)]
            if m >= 2:
                first_w.append(self.wA(k_evac[m - 2]))
            last = 0
            for g in range(4):
                for k in range(DC):
                    last = self.pe(lambda q, ps=ps, g=g, k=k, slot=slot: q.matmul(
                        ps[:, g * 512:(g + 1) * 512], self.WS[slot][:, k * 128:(k + 1) * 128],
                        self.XN[k][:, g * 512:(g + 1) * 512], start=(k == 0), stop=(k == DC - 1),
                        skip_group_check=True),
                        inc=(g == 3 and k == DC - 1),
                        waits=(first_w if (g == 0 and k == 0) else ()))
            ws_cons[m] = last
            k_evac[m] = self.act(lambda q, m=m, ps=ps: q.activation(self.K[m][:], ps[:], AF.Copy, scale=0.125),
                                 inc=True, waits=[self.wP(last)])
        kn_done = 0
        for m in range(DC):
            wk_ = [self.wA(k_evac[m])]
            if m > 0:
                wk_.append(self.wP(kn_done))
            v = self.dve(lambda q, m=m: q.tensor_tensor(self.KSQ[:], self.K[m][:], self.K[m][:], ALU.mult),
                         waits=wk_)
            for g in range(4):
                wkn = ([self.wV(v)] if g == 0 else [])
                if m == 0 and g == 0:
                    wkn.append(self.wA(k_evac[DC - 2]))
                kn_done = self.pe(lambda q, m=m, g=g: q.matmul(
                    PSA[0:16, g * 512:(g + 1) * 512], self.cb_hind(m), self.KSQ[:, g * 512:(g + 1) * 512],
                    start=(m == 0), stop=(m == DC - 1), skip_group_check=True),
                    inc=(g == 3), waits=wkn)
        km_v = self.dve(lambda q: q.reduce_max(self.KMX[:], PSA[0:16, 0:2048], axis=AX.X),
                        inc=True, waits=[self.wP(kn_done)])

        wv_last = 0
        for k in range(DC):
            wv_last = self.wld(self.WV[:, k * 1024:(k + 1) * 1024], wv_d[k * 128:(k + 1) * 128, :],
                               "WV", waits=([self.wP(kn_done)] if k == 0 else ()))
        ones_done = 0
        for t in range(NTT):
            ones_done = self.dve(lambda q, t=t: q.memset(
                bass.AP(self.VH[t].tensor, 64, [[1040, 128], [65, 16]]), 1.0),
                inc=(t == NTT - 1), waits=([self.wV(km_v)] if t == 0 else ()))
        v_evac = {}
        for t in range(NTT):
            ps = PSA if t % 2 == 0 else PSB
            w0 = [wv_last, self.wV(km_v)]
            if t >= 2:
                w0.append(self.wV(v_evac[t - 2]))
            last = 0
            for k in range(DC):
                for hh in range(2):
                    last = self.pe(lambda q, ps=ps, t=t, k=k, hh=hh: q.matmul(
                        ps[:, hh * 512:(hh + 1) * 512], self.XN[k][:, t * 128:(t + 1) * 128],
                        self.WV[:, k * 1024 + hh * 512: k * 1024 + (hh + 1) * 512],
                        start=(k == 0), stop=(k == DC - 1), skip_group_check=True),
                        inc=(k == DC - 1 and hh == 1),
                        waits=(w0 if (k == 0 and hh == 0) else ()))
            v_evac[t] = self.dve(lambda q, t=t, ps=ps: q.tensor_copy(
                bass.AP(self.VH[t].tensor, 0, [[1040, 128], [65, 16], [1, 64]]),
                bass.AP(ps, 0, [[2048, 128], [64, 16], [1, 64]])),
                inc=True, waits=[self.wP(last), self.wV(ones_done)])
        v_done = v_evac[NTT - 1]

        if causal:
            tm_tgt = self.ld(self.TM[:], self.d_trimask, "TM", waits=[self.wV(v_done)])
        else:
            tm_tgt = None

        exp_side = {"A": 0, "B1": 0, "B2": 0}
        av_done = {"A": 0, "B1": 0, "B2": 0}
        epi_mult = {-1: 0}
        ctx_free = 0
        prev_op_done = 0
        q_last = 0
        epi_idx = [0]
        ctx_banks_free = [0]
        for qb in range(NQB):
            q_evac = {}
            ws_cons[-2], ws_cons[-1] = ws_cons[DC - 2], ws_cons[DC - 1]
            for m in range(DC):
                slot = m % 2
                w = [self.wP(ws_cons[m - 2])] if ws_cons[m - 2] else []
                wv_ = self.wld(self.WS[slot][:], wq_d[m], "WS%d" % slot, waits=w)
                psq = PSB[:, 0:512] if m % 2 == 0 else PSB[:, 512:1024]
                w0 = [wv_]
                if qb == 0 and m == 0:
                    w0.append(self.wV(v_done))
                    if causal:
                        w0.append(tm_tgt)
                if qb > 0 and m == 0:
                    w0.append(self.wA(exp_side["B1"]))
                if m >= 2:
                    w0.append(self.wA(q_evac[m - 2]))
                last = 0
                for k in range(DC):
                    last = self.pe(lambda q, psq=psq, k=k, slot=slot, qb=qb: q.matmul(
                        psq, self.WS[slot][:, k * 128:(k + 1) * 128],
                        self.XN[k][:, qb * 512:(qb + 1) * 512], start=(k == 0), stop=(k == DC - 1),
                        skip_group_check=True),
                        inc=(k == DC - 1), waits=(w0 if k == 0 else ()))
                ws_cons[m] = last
                q_last = last
                q_evac[m] = self.act(lambda q, m=m, psq=psq: q.activation(self.Q[m][:], psq, AF.Copy, scale=1.0),
                                     waits=[self.wP(last)])
            qn = 0
            for m in range(DC):
                wq_ = [self.wA(q_evac[m])]
                if m > 0:
                    wq_.append(self.wP(qn))
                if m == 0 and prev_op_done:
                    wq_.append(self.wV(prev_op_done))
                v = self.dve(lambda q, m=m: q.tensor_tensor(self.QSQ[:], self.Q[m][:], self.Q[m][:], ALU.mult),
                             waits=wq_)
                qn = self.pe(lambda q, m=m: q.matmul(
                    PSB[0:16, 1024:1536], self.cb_hind(m), self.QSQ[:],
                    start=(m == 0), stop=(m == DC - 1), skip_group_check=True),
                    inc=True, waits=[self.wV(v)])
            rmax_v = self.dve(lambda q: q.reduce_max(self.CBIA[0:16, 17:18], PSB[0:16, 1024:1536],
                                                     axis=AX.X), waits=[self.wP(qn)])
            vcs = self.dve(lambda q: q.tensor_tensor(self.CBIA[0:16, 18:19], self.CBIA[0:16, 17:18],
                                                     self.KMX[:], ALU.mult), chain=True)
            wnc = [self.wV(vcs)]
            if self.nc2_prev:
                wnc.append(self.nc2_prev)
            d1 = self.ld(bass.AP(self.d_ncd.tensor, 0, [[1, 16], [1, 1]]), self.CBIA[0:16, 18:19],
                         "NC1", waits=wnc)
            d2 = self.ld(self.CBIA[0:1, 0:16], bass.AP(self.d_ncd.tensor, 0, [[16, 1], [1, 16]]),
                         "NC2", waits=[d1])
            self.nc2_prev = d2
            vgm = self.dve(lambda q: q.reduce_max(self.CBIA[0:1, 19:20], self.CBIA[0:1, 0:16], axis=AX.X),
                           waits=[d2], chain=True)
            self.act(lambda q: q.activation(self.CBIA[0:1, 20:21], self.CBIA[0:1, 19:20], AF.Ln, scale=1.0),
                     waits=[self.wV(vgm)])
            ace = self.act(lambda q: q.activation(self.CBIA[0:1, 20:21], self.CBIA[0:1, 20:21], AF.Exp,
                                                  scale=0.5), chain=True)
            vng = self.dve(lambda q: q.tensor_scalar(self.CBIA[0:1, 21:22], self.CBIA[0:1, 20:21],
                                                     -1.0, None, ALU.mult), waits=[self.wA(ace)])
            wnc3 = [self.wV(vng)]
            if self.nc4_prev:
                wnc3.append(self.nc4_prev)
            d3 = self.ld(bass.AP(self.d_ncd.tensor, 16, [[1, 1], [1, 1]]), self.CBIA[0:1, 21:22],
                         "NC3", waits=wnc3)
            cb_rdy = self.ld(self.CBIA[:, 16:17], bass.AP(self.d_ncd.tensor, 16, [[0, 128], [1, 1]]),
                             "NC4", waits=[d3])
            self.nc4_prev = cb_rdy

            nkt = 4 * qb + 4 if causal else 16
            SIDES = ("A", "B1", "B2")
            GRP = {"A": (PSA, 1024), "B1": (PSB, 0), "B2": (PSB, 1024)}
            PTS = {"A": self.PTA, "B1": self.PTB[:, 0:1024], "B2": self.PTB[:, 1024:2048]}
            first_use = {"A": True, "B1": True, "B2": True}

            def emit_S(c, side, kt, first_of_qb):
                base, go = GRP[side]
                w = []
                if exp_side[side]:
                    w.append(self.wA(exp_side[side]))
                if first_of_qb:
                    if qb == 0:
                        w.append(self.wV(v_done))
                if first_use[side]:
                    first_use[side] = False
                    if side == "B2":
                        w.append(self.wV(rmax_v))
                    if side == "B1":
                        w.append(self.wA(q_evac[DC - 1]))
                pm = 0
                for hh in range(2):
                    so = go + hh * 512
                    pm = self.pe(lambda q, base=base, so=so, c=c, hh=hh, kt=kt: q.matmul(
                        base[:, so:so + 512], self.K[c][hh * 64:(hh + 1) * 64, kt * 128:(kt + 1) * 128],
                        self.Q[c][hh * 64:(hh + 1) * 64, :], start=True, stop=True,
                        skip_group_check=True),
                        inc=(hh == 1), waits=(w if hh == 0 else ()))
                mask_v = 0
                if causal:
                    r = kt - 4 * qb
                    if 0 <= r <= 3:
                        mask_v = self.dve(lambda q, base=base, go=go, r=r: q.tensor_tensor(
                            base[:, go:go + 1024], base[:, go:go + 1024],
                            bass.AP(self.TM.tensor, r * 512, [[2048, 128], [0, 2], [1, 512]]),
                            ALU.add), waits=[self.wP(pm)])
                return pm, mask_v

            def emit_E(c, side, kt, pm, mask_v, first_exp):
                base, go = GRP[side]
                ptb = PTS[side]
                we = [self.wP(pm)]
                if mask_v:
                    we.append(self.wV(mask_v))
                if av_done[side]:
                    we.append(self.wP(av_done[side]))
                if first_exp:
                    we.append(cb_rdy)
                ae = self.act(lambda q, base=base, go=go, ptb=ptb: q.activation(
                    ptb, base[:, go:go + 1024], AF.Exp, bias=self.CBIA[:, 16:17], scale=1.0),
                    waits=we)
                exp_side[side] = ae
                return ae

            def emit_AV(item):
                c, side, kt, ae = item
                ptb = PTS[side]
                last = 0
                for hh in range(2):
                    so = hh * 512
                    w = [self.wA(ae)]
                    if kt == 0 and hh == 0 and ctx_banks_free[0]:
                        w.append(self.wA(ctx_banks_free[0]))
                    last = self.pe(lambda q, kt=kt, c=c, hh=hh, so=so, ptb=ptb, nkt=nkt: q.matmul(
                        PSA[0:65, hh * 512: hh * 512 + 512],
                        self.VH[kt][:, (2 * c + hh) * 65:(2 * c + hh) * 65 + 65],
                        ptb[:, so:so + 512],
                        start=(kt == 0), stop=(kt == nkt - 1), skip_group_check=True),
                        inc=(hh == 1), waits=(w if hh == 0 else ()))
                av_done[side] = last
                return last

            def emit_epilogue(c, av_last, ei):
                par = ei % 2
                wev = [self.wP(av_last)]
                if epi_mult.get(ei - 2):
                    wev.append(self.wV(epi_mult[ei - 2]))
                ev = self.act(lambda q, par=par: q.activation(
                    self.CTXRAW[par][0:65, :], PSA[0:65, 0:1024], AF.Copy, scale=1.0), waits=wev)
                ctx_banks_free[0] = ev
                self.act(lambda q, par=par: q.activation(
                    self.RROW[:], self.CTXRAW[par][64:65, :], AF.Ln, scale=1.0), chain=True)
                ra = self.act(lambda q, par=par: q.activation(
                    self.RBS[par][0:1, :], self.RROW[:], AF.Exp, scale=-1.0), chain=True)
                d1 = self.ld(self.d_rsd[par, :], self.RBS[par][0:1, :], "RS%d" % par,
                             waits=[self.wA(ra)])
                wd2 = [d1]
                if epi_mult.get(ei - 2):
                    wd2.append(self.wV(epi_mult[ei - 2]))
                d2 = self.ld(self.RBS[par][0:64, :],
                             bass.AP(self.d_rsd.tensor, par * 1024, [[0, 64], [1, 1024]]),
                             "RB%d" % par, waits=wd2)
                wctx = [d2]
                if ctx_free:
                    wctx.append(self.wP(ctx_free))
                self.dve(lambda q, c=c, par=par: q.tensor_tensor(
                    self.CTX[c][0:64, :], self.CTXRAW[par][0:64, 0:512],
                    self.RBS[par][0:64, 0:512], ALU.mult), waits=wctx)
                epi_mult[ei] = self.dve(lambda q, c=c, par=par: q.tensor_tensor(
                    self.CTX[c][64:128, :], self.CTXRAW[par][0:64, 512:1024],
                    self.RBS[par][0:64, 512:1024], ALU.mult), chain=True)

            groups = []
            for c in range(DC):
                for kt in range(nkt):
                    groups.append((c, SIDES[(c * nkt + kt) % 3], kt))
            fifo = []
            first_exp_of_qb = True
            for gi, (c, side, kt) in enumerate(groups):
                pm, mask_v = emit_S(c, side, kt, first_of_qb=(gi == 0))
                ae = emit_E(c, side, kt, pm, mask_v, first_exp_of_qb)
                first_exp_of_qb = False
                fifo.append((c, side, kt, ae))
                while len(fifo) > 2:
                    item = fifo.pop(0)
                    av_last = emit_AV(item)
                    if item[2] == nkt - 1:
                        ei = epi_idx[0]
                        emit_epilogue(item[0], av_last, ei)
                        epi_idx[0] = ei + 1
            while fifo:
                item = fifo.pop(0)
                av_last = emit_AV(item)
                if item[2] == nkt - 1:
                    ei = epi_idx[0]
                    emit_epilogue(item[0], av_last, ei)
                    epi_idx[0] = ei + 1
            ctx_done = epi_mult[epi_idx[0] - 1]

            op_v = {}
            ws_cons[-2], ws_cons[-1] = ws_cons[DC - 2], ws_cons[DC - 1]
            for m in range(DC):
                slot = m % 2
                w = [self.wP(ws_cons[m - 2])] if ws_cons[m - 2] else []
                wv_ = self.wld(self.WS[slot][:], wo_d[m], "WS%d" % slot, waits=w)
                op = PSB[:, 1024:1536] if m % 2 == 0 else PSB[:, 1536:2048]
                w0 = [wv_]
                if m == 0:
                    w0 += [self.wV(ctx_done), self.wA(exp_side["B2"])]
                if m >= 2:
                    w0.append(self.wV(op_v[m - 2]))
                last = 0
                for k in range(DC):
                    last = self.pe(lambda q, op=op, k=k, slot=slot: q.matmul(
                        op, self.WS[slot][:, k * 128:(k + 1) * 128], self.CTX[k][:],
                        start=(k == 0), stop=(k == DC - 1), skip_group_check=True),
                        inc=(k == DC - 1), waits=(w0 if k == 0 else ()))
                ws_cons[m] = last
                bias_col = self.b_bo(m) if with_bias else self.b_z
                wst = [self.wP(last)] + (self.x_dep if (qb == 0 and m == 0) else [])
                op_v[m] = self.dve(lambda q, m=m, qb=qb, op=op, bias_col=bias_col: q.scalar_tensor_tensor(
                    self.X[m][:, qb * 512:(qb + 1) * 512], op, bias_col,
                    self.X[m][:, qb * 512:(qb + 1) * 512], ALU.add, ALU.add),
                    waits=wst)
            ctx_free = ws_cons[DC - 1]
            prev_op_done = op_v[DC - 1]
        self.ws_free = ws_cons[DC - 1]
        self.x_dep = [self.wV(self.C["V"])]
        self.xn_free = q_last

    def emit_ffn(self, w0_d, w1_d, w2_d, gw_d):
        PSA, PSB = self.PSA, self.PSB
        xn_ready = self.xn_ready
        wg_tgt = self.wld(self.WG[:], gw_d, "WG", waits=self.phase_gate())
        gate_v = {}
        mt_v = 0
        for tt in range(NTT):
            gb = PSA[:, 0:8] if tt % 2 == 0 else PSA[:, 512:520]
            w0 = [wg_tgt, self.wV(xn_ready)] if tt == 0 else []
            if tt >= 2:
                w0 = [self.wV(gate_v[tt - 2])]
            last = 0
            for dc in range(DC):
                last = self.pe(lambda q, gb=gb, dc=dc, tt=tt: q.matmul(
                    gb, self.XN[dc][:, tt * 128:(tt + 1) * 128], self.WG[:, dc * 8:(dc + 1) * 8],
                    start=(dc == 0), stop=(dc == DC - 1), skip_group_check=True),
                    inc=(dc == DC - 1), waits=(w0 if dc == 0 else ()))
            self.dve(lambda q, gb=gb: q.tensor_tensor(self.GSC[:], gb, self.b_gb, ALU.add),
                     waits=[self.wP(last)], chain=True)
            self.dve(lambda q: q.reduce_max(self.GCV[:, 0:1], self.GSC[:], axis=AX.X), chain=True)
            self.dve(lambda q: q.tensor_scalar(self.GC1[:], self.GSC[:], self.GCV[:, 0:1], None, ALU.is_equal),
                     chain=True)
            self.dve(lambda q: q.scalar_tensor_tensor(self.GC2[:], self.GC1[:], NEG, self.GSC[:],
                                                      ALU.mult, ALU.add), chain=True)
            self.dve(lambda q: q.reduce_max(self.GCV[:, 1:2], self.GC2[:], axis=AX.X), chain=True)
            self.dve(lambda q: q.tensor_scalar(self.GC2[:], self.GC2[:], self.GCV[:, 1:2], None, ALU.is_equal),
                     chain=True)
            vd = self.dve(lambda q: q.tensor_tensor(self.GCV[:, 2:3], self.GCV[:, 1:2], self.GCV[:, 0:1],
                                                    ALU.subtract), chain=True)
            asg = self.act(lambda q: q.activation(self.GCV[:, 3:4], self.GCV[:, 2:3], AF.Sigmoid, scale=1.0),
                           waits=[self.wV(vd)])
            self.dve(lambda q: q.tensor_scalar(self.GCV[:, 2:3], self.GCV[:, 3:4], -1.0, 1.0,
                                               ALU.mult, ALU.add), waits=[self.wA(asg)], chain=True)
            self.dve(lambda q: q.tensor_scalar(self.GC1[:], self.GC1[:], self.GCV[:, 2:3], None, ALU.mult),
                     chain=True)
            vm = self.dve(lambda q: q.scalar_tensor_tensor(self.MSC[:], self.GC2[:], self.GCV[:, 3:4],
                                                           self.GC1[:], ALU.mult, ALU.add), chain=True)
            mt_m = self.pe(lambda q: q.matmul(PSA[0:8, 1024:1152], self.MSC[:], self.cb_id,
                                              start=True, stop=True, skip_group_check=True),
                           inc=True, waits=[self.wV(vm)] + ([self.wV(mt_v)] if tt >= 1 else []))
            mt_v = self.dve(lambda q, tt=tt: q.tensor_copy(self.MT[0:8, tt * 128:(tt + 1) * 128],
                                                           PSA[0:8, 1024:1152]),
                            inc=True, waits=[self.wP(mt_m)])
            gate_v[tt] = mt_v
        gate_done = mt_v

        w01_cons = {-2: self.w01_free, -1: self.w01_free}
        xn_last = 0
        last_stt = dict(self.prev_ffn_stt)
        last_cmb = self.prev_cmb
        for e in range(E):
            if e > 0:
                w01_cons[-2] = w01_cons[HC - 2]
                w01_cons[-1] = w01_cons[HC - 1]
            act_v = {}
            for hh in range(HC):
                slot = hh % 2
                w = [self.wP(w01_cons[hh - 2])] if w01_cons[hh - 2] else []
                wa = self.wld(self.W01[slot][:, 0:1024], w0_d[e, hh], "W01%d" % slot, waits=w)
                wb = self.wld(self.W01[slot][:, 1024:2048], w1_d[e, hh], "W01%d" % slot)
                for half in range(2):
                    hs = half * 1024
                    w0 = [wb]
                    if e == 0 and hh == 0 and half == 0:
                        w0.append(self.wV(gate_done))
                    if hh == 0 and last_cmb:
                        w0.append(self.wV(last_cmb))
                    if act_v.get((hh - 1, half)):
                        w0.append(self.wV(act_v[(hh - 1, half)]))
                    last = 0
                    for dc in range(DC):
                        for sub in range(2):
                            ss = hs + sub * 512
                            self.pe(lambda q, ss=ss, dc=dc, slot=slot: q.matmul(
                                PSA[:, ss:ss + 512], self.W01[slot][:, dc * 128:(dc + 1) * 128],
                                self.XN[dc][:, ss:ss + 512], start=(dc == 0), stop=(dc == DC - 1),
                                skip_group_check=True),
                                waits=(w0 if (dc == 0 and sub == 0) else ()))
                            last = self.pe(lambda q, ss=ss, dc=dc, slot=slot: q.matmul(
                                PSB[:, ss:ss + 512], self.W01[slot][:, 1024 + dc * 128: 1024 + (dc + 1) * 128],
                                self.XN[dc][:, ss:ss + 512], start=(dc == 0), stop=(dc == DC - 1),
                                skip_group_check=True),
                                inc=(dc == DC - 1 and sub == 1))
                    if half == 1:
                        w01_cons[hh] = last
                    xn_last = last
                    wsl = [self.wP(last)]
                    prev_stt = act_v.get((hh - 1, half), 0) or last_stt.get(half, 0)
                    if prev_stt:
                        wsl.append(self.wV(prev_stt))
                    a = self.act(lambda q, hs=hs, e=e, hh=hh: q.activation(
                        self.SIL[:, hs:hs + 1024], PSB[:, hs:hs + 1024], AF.Silu,
                        bias=self.b_b1(e, hh), scale=1.0), waits=wsl)
                    act_v[(hh, half)] = self.dve(lambda q, hs=hs, e=e, hh=hh: q.scalar_tensor_tensor(
                        self.ACT[hh][:, hs:hs + 1024], PSA[:, hs:hs + 1024], self.b_b0(e, hh),
                        self.SIL[:, hs:hs + 1024], ALU.add, ALU.mult), waits=[self.wA(a)])
            act_done = act_v[(HC - 1, 1)]
            mb_a = 0
            for ts in range(4):
                w0 = [self.wV(act_done)] if ts == 0 else [self.wA(mb_a)]
                if ts == 0 and last_cmb:
                    w0.append(self.wV(last_cmb))
                mb_m = self.pe(lambda q, ts=ts, e=e: q.matmul(
                    PSB[:, 0:512], self.cb_sel(e), self.MT[0:8, ts * 512:(ts + 1) * 512],
                    start=True, stop=True, skip_group_check=True), inc=True, waits=w0)
                mb_a = self.act(lambda q, ts=ts: q.activation(self.MBS[:, ts * 512:(ts + 1) * 512],
                                                              PSB[:, 0:512], AF.Copy, scale=1.0),
                                inc=True, waits=[self.wP(mb_m)])
            if e == 0:
                self._w2carry = [self.w2b_free, self.w2b_free]
            w2_cons = {-2: self._w2carry[0], -1: self._w2carry[1]}
            cmb_v = {}
            for dc in range(DC):
                slot = dc % 2
                w = [self.wP(w2_cons[dc - 2])] if w2_cons[dc - 2] else []
                wv_ = self.wld(self.W2B[slot][:], w2_d[e, dc], "W2B%d" % slot, waits=w)
                for ts in range(4):
                    op = PSA[:, ts * 512:(ts + 1) * 512]
                    w0 = [wv_]
                    if dc == 0 and ts == 0:
                        w0.append(self.wA(mb_a))
                    if cmb_v.get((dc - 1, ts)):
                        w0.append(self.wV(cmb_v[(dc - 1, ts)]))
                    last = 0
                    for hc in range(HC):
                        last = self.pe(lambda q, op=op, hc=hc, ts=ts, slot=slot: q.matmul(
                            op, self.W2B[slot][:, hc * 128:(hc + 1) * 128],
                            self.ACT[hc][:, ts * 512:(ts + 1) * 512],
                            start=(hc == 0), stop=(hc == HC - 1), skip_group_check=True),
                            inc=(hc == HC - 1), waits=(w0 if hc == 0 else ()))
                    if ts == 3:
                        w2_cons[dc] = last
                    wst = [self.wP(last)]
                    if e == 0 and dc == 0 and ts == 0:
                        wst += self.x_dep
                    self.dve(lambda q, op=op, e=e, dc=dc, ts=ts: q.scalar_tensor_tensor(
                        self.TMPF[:], op, self.b_b2(e, dc), self.MBS[:, ts * 512:(ts + 1) * 512],
                        ALU.add, ALU.mult), waits=wst, chain=True)
                    cmb_v[(dc, ts)] = self.dve(lambda q, dc=dc, ts=ts: q.tensor_tensor(
                        self.X[dc][:, ts * 512:(ts + 1) * 512], self.X[dc][:, ts * 512:(ts + 1) * 512],
                        self.TMPF[:], ALU.add), chain=True)
            self._w2carry = [w2_cons[DC - 2], w2_cons[DC - 1]]
            last_cmb = cmb_v[(DC - 1, 3)]
            last_stt = {0: act_v[(HC - 1, 0)], 1: act_v[(HC - 1, 1)]}
        self.w01_free = w01_cons[HC - 1]
        self.w2b_free = w2_cons[DC - 1]
        self.prev_cmb = last_cmb
        self.prev_ffn_stt = dict(last_stt)
        self.x_dep = [self.wV(self.C["V"])]
        self.xn_free = xn_last

    def emit_all(self):
        self.emit_prologue()
        store_done = None
        for s in range(NSEQ):
            self.emit_load_x(s, after=([store_done] if s else ()))
            subs = ["attn_a", "ffn", "attn_b", "ffn"][: self.n_sub]
            for sub in subs:
                self.emit_ln()
                if sub == "attn_a":
                    self.emit_attn(True, self.d_wq, self.d_wk, self.d_wv, self.d_wo, True)
                elif sub == "attn_b":
                    self.emit_attn(False, self.d_uq, self.d_uk, self.d_uv, self.d_uo, False)
                else:
                    self.emit_ffn(self.d_w0t, self.d_w1t, self.d_w2t, self.d_gwt)
            store_done = self.emit_store_x(s)
            self.xn_free = 0
            self.xn_free = self.C["P"] and 0
        self.final_out = store_done

    def finalize(self):
        nc = self.nc
        OPS = self.OPS
        final_out = self.final_out
        with nc.Block() as block:
            @block.sync
            def _(q):
                for fn in OPS["sync"]:
                    fn(q)
                q.wait_ge(final_out[0], final_out[1])

            @block.gpsimd
            def _(q):
                for fn in OPS["gp"]:
                    fn(q)

            @block.tensor
            def _(q):
                for fn in OPS["pe"]:
                    fn(q)

            @block.scalar
            def _(q):
                for fn in OPS["act"]:
                    fn(q)

            @block.vector
            def _(q):
                for fn in OPS["dve"]:
                    fn(q)
        for cm in reversed(self._ctx):
            cm.__exit__(None, None, None)
        return nc


def build_program(n_sub=4):
    b = B(n_sub)
    b.emit_all()
    return b.finalize()


def kernel(**inputs):
    ins = {k: np.ascontiguousarray(np.asarray(v, dtype=np.float32)) for k, v in inputs.items()}
    blobs = _host_blobs(ins)
    x = ins["L0_attn_a_x"]
    nc = build_program(4)
    shared = {k: blobs[k] for k in ["wq", "wk", "wv", "wo", "uq", "uk", "uv", "uo",
                                    "w0t", "w1t", "w2t", "gwt", "biasblob", "constsb", "trimask"]}
    in_maps = []
    for c in range(NCORE):
        xt = np.ascontiguousarray(x[c * NSEQ:(c + 1) * NSEQ].transpose(0, 2, 1))
        m = dict(shared)
        m["xt"] = xt
        in_maps.append(m)
    res = run_bass_kernel_spmd(nc, in_maps, core_ids=list(range(NCORE)))
    outs = []
    for c in range(NCORE):
        o = res.results[c]["out"]
        outs.append(np.asarray(o).transpose(0, 2, 1))
    return np.ascontiguousarray(np.concatenate(outs, axis=0))


# version 09
# speedup vs baseline: 1.1256x; 1.0892x over previous
import numpy as np
import concourse.bass as bass
import concourse.mybir as mybir
from concourse.bass_utils import run_bass_kernel_spmd

F32 = mybir.dt.float32
BF16 = mybir.dt.bfloat16
AF = mybir.ActivationFunctionType
ALU = mybir.AluOpType
AX = mybir.AxisListType

D, T, H, HD, E, HF = 1024, 2048, 16, 64, 8, 2048
DC, HC, NTT, NQB = 8, 16, 16, 4
NCORE, NSEQ = 8, 2
NEG = -1e30

BIAS_OFF = {"b0": 0, "b1": 128, "b2": 256, "bo": 320, "gb": 328, "z": 336,
            "ones": 337, "id16": 465, "id128": 481, "fc": 609}
EPS_BIAS = float(1024.0 * 1024.0 * 1e-5)
LN1024 = float(np.log(1024.0))
FCONSTS = [EPS_BIAS, 1024.0, -1.0, 1.0, -1e30, 2.0, 0.0, 0.125, LN1024, 0.5, -0.5]
BIAS_W = 609 + len(FCONSTS)
CB_W = 1408


def _host_blobs(ins):
    f = np.float32
    import ml_dtypes
    out = {}

    def mtile(W):
        return np.ascontiguousarray(W.reshape(8, 128, 8, 128).transpose(2, 1, 0, 3)).reshape(8, 128, 1024)

    out["wq"] = mtile(ins["L0_attn_a_Wq"])
    out["wk"] = mtile(ins["L0_attn_a_Wk"])
    out["wo"] = mtile(ins["L0_attn_a_Wo"])
    out["wv"] = np.ascontiguousarray(ins["L0_attn_a_Wv"])
    uq = np.ascontiguousarray(ins["L0_attn_b_W_q"].transpose(1, 0, 2).reshape(1024, 1024))
    uk = np.ascontiguousarray(ins["L0_attn_b_W_k"].transpose(1, 0, 2).reshape(1024, 1024))
    uv = np.ascontiguousarray(ins["L0_attn_b_W_v"].transpose(1, 0, 2).reshape(1024, 1024))
    out["uq"] = mtile(uq)
    out["uk"] = mtile(uk)
    out["uo"] = mtile(ins["L0_attn_b_W_o"])
    out["uv"] = uv
    out["w0t"] = np.ascontiguousarray(
        ins["L0_ffn_w0"].reshape(8, 16, 128, 8, 128).transpose(0, 1, 4, 3, 2)).reshape(8, 16, 128, 1024)
    out["w1t"] = np.ascontiguousarray(
        ins["L0_ffn_w1"].reshape(8, 16, 128, 8, 128).transpose(0, 1, 4, 3, 2)).reshape(8, 16, 128, 1024)
    out["w2t"] = np.ascontiguousarray(
        ins["L0_ffn_w2"].reshape(8, 8, 128, 16, 128).transpose(0, 1, 4, 3, 2)).reshape(8, 8, 128, 2048)
    out["gwt"] = np.ascontiguousarray(
        ins["L0_ffn_gate_w"].T.reshape(8, 128, 8).transpose(1, 0, 2)).reshape(128, 64)

    b0c = np.ascontiguousarray(ins["L0_ffn_b0"].reshape(8, 16, 128).transpose(2, 0, 1)).reshape(128, 128)
    b1c = np.ascontiguousarray(ins["L0_ffn_b1"].reshape(8, 16, 128).transpose(2, 0, 1)).reshape(128, 128)
    b2c = np.ascontiguousarray(ins["L0_ffn_b2"].reshape(8, 8, 128).transpose(2, 0, 1)).reshape(128, 64)
    boc = np.ascontiguousarray(ins["L0_attn_a_bo"].reshape(8, 128).T)
    gbb = np.broadcast_to(ins["L0_ffn_gate_b"].astype(f), (128, 8))
    zeros = np.zeros((128, 1), f)
    onesf = np.ones((128, 128), f)
    idf16 = np.zeros((128, 16), f)
    idf16[:16, :] = np.eye(16, dtype=f)
    idf128 = np.eye(128, dtype=f)
    fcc = np.broadcast_to(np.asarray(FCONSTS, f)[None, :], (128, len(FCONSTS)))
    blob = np.concatenate([b0c, b1c, b2c, boc, gbb, zeros, onesf, idf16, idf128, fcc],
                          axis=1).astype(f)
    assert blob.shape[1] == BIAS_W, blob.shape
    out["biasblob"] = np.ascontiguousarray(blob)

    idn = np.eye(128, dtype=f)
    hind = np.zeros((128, 128), f)
    for c in range(8):
        for p in range(128):
            hind[p, c * 16 + (2 * c + p // 64)] = 1.0
    onesb = np.ones((128, 128), f)
    sel = np.zeros((128, 8 * 128), f)
    for e in range(8):
        sel[e, e * 128:(e + 1) * 128] = 1.0
    cb = np.concatenate([idn, hind, onesb, sel], axis=1)
    assert cb.shape[1] == CB_W
    out["constsb"] = np.ascontiguousarray(cb.astype(ml_dtypes.bfloat16))

    tm = np.full((128, 2048), NEG, f)
    for r in range(4):
        for p in range(128):
            j0 = r * 128 + p
            if j0 < 512:
                tm[p, r * 512 + j0: (r + 1) * 512] = 0.0
    out["trimask"] = tm.astype(ml_dtypes.bfloat16)
    return out


class B:
    def __init__(self, n_sub=4):
        self.n_sub = n_sub
        nc = bass.Bass("TRN2", target_bir_lowering=False)
        self.nc = nc
        self._ctx = []
        self.d_xt = nc.dram_tensor("xt", [NSEQ, D, T], F32, kind="ExternalInput").ap()
        self.d_out = nc.dram_tensor("out", [NSEQ, D, T], F32, kind="ExternalOutput").ap()
        for n, shp in [("wq", [8, 128, 1024]), ("wk", [8, 128, 1024]), ("wo", [8, 128, 1024]),
                       ("wv", [1024, 1024]), ("uq", [8, 128, 1024]), ("uk", [8, 128, 1024]),
                       ("uo", [8, 128, 1024]), ("uv", [1024, 1024]),
                       ("w0t", [8, 16, 128, 1024]), ("w1t", [8, 16, 128, 1024]),
                       ("w2t", [8, 8, 128, 2048]), ("gwt", [128, 64])]:
            setattr(self, "d_" + n, nc.dram_tensor(n, shp, F32, kind="ExternalInput").ap())
        self.d_trimask = nc.dram_tensor("trimask", [128, 2048], BF16, kind="ExternalInput").ap()
        self.d_bias = nc.dram_tensor("biasblob", [128, BIAS_W], F32, kind="ExternalInput").ap()
        self.d_rsd = nc.dram_tensor("rsd", [2, 1024], BF16, kind="Internal").ap()
        self.d_ncd = nc.dram_tensor("ncd", [32], F32, kind="Internal").ap()
        self.d_lnr = nc.dram_tensor("lnr", [2, 2048], F32, kind="Internal").ap()
        self.d_cb = nc.dram_tensor("constsb", [128, CB_W], BF16, kind="ExternalInput").ap()

        off = [16512]

        def alloc(name, shape, dt, at=None):
            nbytes = int(np.prod(shape[1:])) * (2 if dt == BF16 else 4)
            if at is None:
                at = off[0]
                off[0] = (at + nbytes + 31) & ~31
            return nc.alloc_sbuf_tensor_at(name, shape, dt, offset=at).ap()

        self.X = [alloc(f"x{c}", [128, 2048], F32) for c in range(DC)]
        self.XN = [alloc(f"xn{c}", [128, 2048], BF16) for c in range(DC)]
        self.BIAS = alloc("bias", [128, BIAS_W], F32)
        self.CB = alloc("cb", [128, CB_W], BF16)
        self.KMX = alloc("kmx", [16, 1], F32)
        AR0 = off[0]
        self.AR0 = AR0
        assert 229344 - AR0 >= 109000, (AR0, 229344 - AR0)

        def ar(name, shape, dt, o):
            return alloc(name, shape, dt, at=AR0 + o)

        self.K = [ar(f"k{c}", [128, 2048], BF16, c * 4096) for c in range(DC)]
        self.VH = [ar(f"vh{t}", [128, 1040], BF16, 32768 + t * 2080) for t in range(NTT)]
        self.Q = [ar(f"q{c}", [128, 512], BF16, 66048 + c * 1024) for c in range(DC)]
        self.CTX = [ar(f"ctx{c}", [128, 512], BF16, 74240 + c * 1024) for c in range(DC)]
        self.PTB = ar("ptb", [128, 2048], BF16, 82432)
        self.WS = [ar(f"ws{i}", [128, 1024], BF16, 86528 + i * 2048) for i in range(2)]
        self.WV = ar("wvs", [128, 8 * 1024], BF16, 90624)
        self.TM = ar("tm", [128, 2048], BF16, 90624)
        self.RBS = [ar("rbs0", [64, 1024], BF16, 94720),
                    ar("rbs1", [64, 1024], BF16, 96768)]
        self.RROW = ar("rrow", [1, 1024], F32, 98816)
        self.CTXRAW = [ar("ctxraw0", [66, 1024], BF16, 102912),
                       ar("ctxraw1", [66, 1024], BF16, 104960)]
        self.KSQ = ar("ksq", [128, 2048], BF16, 66048)
        self.PTA = ar("pta", [128, 1024], BF16, 107008)
        self.QSQ = ar("qsq", [128, 512], BF16, 107008)
        self.CBIA = ar("cbia", [128, 32], F32, 109056)

        self.ACT = [ar(f"act{hh}", [128, 2048], BF16, hh * 4096) for hh in range(HC)]
        self.W01 = [ar(f"w01{i}", [128, 2048], BF16, 65536 + i * 4096) for i in range(2)]
        self.W2B = [ar(f"w2b{i}", [128, 2048], BF16, 73728 + i * 4096) for i in range(2)]
        self.SIL = ar("sil", [128, 2048], F32, 81920)
        self.MBS = ar("mbs", [128, 2048], F32, 90112)
        self.MT = ar("mt", [8, 2048], BF16, 98304)
        self.WG = ar("wg", [128, 64], BF16, 102400)
        self.GF = ar("gf", [128, 128], F32, 102528)
        self.GM = ar("gm", [128, 128], F32, 103040)
        self.GI = ar("gi", [128, 128], F32, 103552)
        self.GV = ar("gv", [128, 64], F32, 104064)
        self.MSC = ar("msc", [128, 128], BF16, 104320)
        self.TMPF = ar("tmpf", [128, 512], F32, 104576)

        self.XB = [ar(f"xb{i}", [128, 2048], BF16, i * 4096) for i in range(2)]
        self.XQ = [ar(f"xq{i}", [128, 2048], BF16, 8192 + i * 4096) for i in range(2)]
        self.LNA = ar("lna", [1, 2048], F32, 16384)
        self.LNB = ar("lnb", [1, 2048], F32, 24576)
        self.LNS = ar("lns", [1, 1024], F32, 32768)
        self.TXN = ar("txn", [128, 2048], F32, 36864)
        self.ABT = ar("abt", [128, 2048], BF16, 45056)
        self.B2T = ar("b2t", [128, 2048], BF16, 49152)

        self.PSA = self._cm(nc.psum_tensor("psa", [128, 2048], F32))
        self.PSB = self._cm(nc.psum_tensor("psb", [128, 2048], F32))

        self.DSEM = {}
        self.DCNT = {}
        for nm in ["C", "XL", "TM", "O", "WS0", "WS1", "WV", "W010", "W011",
                   "W2B0", "W2B1", "WG", "RS0", "RS1", "RB0", "RB1", "NC1", "NC2", "NC3", "NC4",
                   "LR1", "LR2", "LB1", "LB2"]:
            self.DSEM[nm] = self._cm(nc.semaphore("s" + nm))
            self.DCNT[nm] = 0
        self.sP = self._cm(nc.semaphore("sP"))
        self.sA = self._cm(nc.semaphore("sA"))
        self.sV = self._cm(nc.semaphore("sV"))
        self.C = {"P": 0, "A": 0, "V": 0}
        self.OPS = {"sync": [], "gp": [], "pe": [], "act": [], "dve": []}

        bo = BIAS_OFF
        self.b_b0 = lambda e, hh: self.BIAS[:, bo["b0"] + e * 16 + hh: bo["b0"] + e * 16 + hh + 1]
        self.b_b1 = lambda e, hh: self.BIAS[:, bo["b1"] + e * 16 + hh: bo["b1"] + e * 16 + hh + 1]
        self.b_b2 = lambda e, dc: self.BIAS[:, bo["b2"] + e * 8 + dc: bo["b2"] + e * 8 + dc + 1]
        self.b_bo = lambda dc: self.BIAS[:, bo["bo"] + dc: bo["bo"] + dc + 1]
        self.b_gb = self.BIAS[:, bo["gb"]: bo["gb"] + 8]
        self.b_z = self.BIAS[:, bo["z"]: bo["z"] + 1]
        self.onesf_row = lambda n: self.BIAS[0:1, bo["ones"]: bo["ones"] + n]
        self.id16 = self.BIAS[0:16, bo["id16"]: bo["id16"] + 16]
        for j, v in enumerate(FCONSTS):
            nc.const_aps.register(F32, v, self.BIAS[:, bo["fc"] + j: bo["fc"] + j + 1])
        self.cb_id = self.CB[:, 0:128]
        self.cb_hind = lambda c: self.CB[:, 128 + c * 16: 128 + c * 16 + 16]
        self.cb_onecol = self.CB[:, 256:257]
        self.cb_sel = lambda e: self.CB[0:8, 384 + e * 128: 384 + (e + 1) * 128]

        self.x_dep = []
        self.xn_free = 0
        self.xn_ready = 0
        self.ws_free = 0
        self.w01_free = 0
        self.w2b_free = 0
        self.prev_cmb = 0
        self.prev_ffn_stt = {0: 0, 1: 0}
        self.nc2_prev = None
        self.nc4_prev = None
        self.ln_apply_done = 0
        self.x_chunk_dep = [[] for _ in range(DC)]
        self.ln_gate = []

    def _cm(self, cm):
        v = cm.__enter__()
        self._ctx.append(cm)
        return v

    def wP(self, v): return (self.sP, v)
    def wA(self, v): return (self.sA, v)
    def wV(self, v): return (self.sV, v)

    def _emit(self, eng, fn, inc, waits, chain=False):
        sem = {"pe": self.sP, "act": self.sA, "dve": self.sV}[eng]
        cname = {"pe": "P", "act": "A", "dve": "V"}[eng]
        if chain:
            waits = list(waits) + [(sem, self.C[cname])]
        for s_, v_ in waits:
            if v_ > 0:
                self.OPS[eng].append(lambda q, s=s_, v=v_: q.wait_ge(s, v))
        if inc:
            self.C[cname] += 1
            self.OPS[eng].append(lambda q, fn=fn, sem=sem: fn(q).then_inc(sem, 1))
        else:
            self.OPS[eng].append(lambda q, fn=fn: fn(q))
        return self.C[cname]

    def pe(self, fn, inc=False, waits=(), chain=False):
        return self._emit("pe", fn, inc, waits, chain)

    def act(self, fn, inc=True, waits=(), chain=False):
        return self._emit("act", fn, True, waits, chain)

    def dve(self, fn, inc=True, waits=(), chain=False):
        return self._emit("dve", fn, True, waits, chain)

    def ld(self, dst, src, sem, waits=()):
        for s_, v_ in waits:
            if v_ > 0:
                self.OPS["sync"].append(lambda q, s=s_, v=v_: q.wait_ge(s, v))
        self.DCNT[sem] += 16
        sh = self.DSEM[sem]
        self.OPS["sync"].append(
            lambda q, dst=dst, src=src, sh=sh: q.dma_start(out=dst, in_=src).then_inc(sh, 16))
        return (sh, self.DCNT[sem])

    def wld(self, dst, src, sem, waits=()):
        for s_, v_ in waits:
            if v_ > 0:
                self.OPS["gp"].append(lambda q, s=s_, v=v_: q.wait_ge(s, v))
        self.DCNT[sem] += 16
        sh = self.DSEM[sem]
        self.OPS["gp"].append(
            lambda q, dst=dst, src=src, sh=sh: q.dma_start(out=dst, in_=src).then_inc(sh, 16))
        return (sh, self.DCNT[sem])

    def phase_gate(self):
        return [self.wP(self.C["P"]), self.wA(self.C["A"]), self.wV(self.C["V"])]

    def emit_prologue(self):
        self.ld(self.BIAS[:], self.d_bias, "C")
        self.consts_ready = self.ld(self.CB[:], self.d_cb, "C")

    def emit_load_x(self, s, after=()):
        for c in range(DC):
            t = self.ld(self.X[c][:], self.d_xt[s, c * 128:(c + 1) * 128, :], "XL",
                        waits=(after if c == 0 else ()))
        for c in range(DC):
            self.x_chunk_dep[c] = [t]
        self.x_dep = [t]

    def emit_store_x(self, s):
        for c in range(DC):
            t = self.ld(self.d_out[s, c * 128:(c + 1) * 128, :], self.X[c][:], "O",
                        waits=(self.x_dep if c == 0 else ()))
        return t

    def emit_ln(self):
        PSA, PSB = self.PSA, self.PSB
        xn_free = self.xn_free
        a_xb, v_xq = [], []
        stats_m = {}
        for c in range(DC):
            i = c % 2
            w = list(self.x_chunk_dep[c])
            if c == 0:
                w.append(self.consts_ready)
                w += self.ln_gate
            if c >= 2:
                w.append(self.wP(stats_m[c - 2]))
            a = self.act(lambda q, c=c, i=i: q.activation(self.XB[i][:], self.X[c][:], AF.Copy, scale=1.0),
                         waits=w)
            a_xb.append(a)
            v = self.dve(lambda q, i=i: q.tensor_tensor(self.XQ[i][:], self.XB[i][:], self.XB[i][:], ALU.mult),
                         waits=[self.wA(a)])
            v_xq.append(v)
            for g in range(4):
                self.pe(lambda q, i=i, g=g, c=c: q.matmul(
                    PSA[0:1, g * 512:(g + 1) * 512], self.cb_onecol, self.XB[i][:, g * 512:(g + 1) * 512],
                    start=(c == 0), stop=(c == DC - 1), skip_group_check=True),
                    waits=([self.wA(a)] if g == 0 else ()))
            for g in range(4):
                m = self.pe(lambda q, i=i, g=g, c=c: q.matmul(
                    PSB[0:1, g * 512:(g + 1) * 512], self.cb_onecol, self.XQ[i][:, g * 512:(g + 1) * 512],
                    start=(c == 0), stop=(c == DC - 1), skip_group_check=True),
                    inc=(g == 3), waits=([self.wV(v)] if g == 0 else ()))
            stats_m[c] = m
        stats_done = stats_m[DC - 1]
        vrows = 0
        for g in range(4):
            S1 = PSA[0:1, g * 512:(g + 1) * 512]
            S2 = PSB[0:1, g * 512:(g + 1) * 512]
            mmr = self.LNS[0:1, 0:512]
            rr = self.LNS[0:1, 512:1024]
            Ar = self.LNA[0:1, g * 512:(g + 1) * 512]
            B2r = self.LNB[0:1, g * 512:(g + 1) * 512]
            asq = self.act(lambda q, S1=S1, mmr=mmr: q.activation(mmr, S1, AF.Square, scale=1.0),
                           waits=([self.wP(stats_done)] if g == 0 else ()), chain=True)
            v0 = self.dve(lambda q, S2=S2, mmr=mmr: q.scalar_tensor_tensor(
                mmr, S2, 1024.0, mmr, ALU.mult, ALU.subtract), waits=[self.wA(asq)])
            self.act(lambda q, mmr=mmr: q.activation(mmr, mmr, AF.Ln, bias=EPS_BIAS, scale=1.0),
                     waits=[self.wV(v0)], chain=True)
            a = self.act(lambda q, mmr=mmr, rr=rr: q.activation(rr, mmr, AF.Exp, scale=-0.5),
                         chain=True)
            self.dve(lambda q, rr=rr, Ar=Ar: q.tensor_scalar(Ar, rr, 1024.0, None, ALU.mult),
                     waits=[self.wA(a)])
            vrows = self.dve(lambda q, S1=S1, rr=rr, B2r=B2r: q.tensor_tensor(B2r, S1, rr, ALU.mult),
                             chain=True)
        da = self.ld(self.d_lnr[0, :], self.LNA[0:1, :], "LR1", waits=[self.wV(vrows)])
        db = self.ld(self.d_lnr[1, :], self.LNB[0:1, :], "LR2")
        wba = [da]
        if self.ln_apply_done:
            wba.append(self.wV(self.ln_apply_done))
        ba = self.wld(self.ABT[:], bass.AP(self.d_lnr.tensor, 0, [[0, 128], [1, 2048]]), "LB1",
                      waits=wba)
        bb = self.wld(self.B2T[:], bass.AP(self.d_lnr.tensor, 2048, [[0, 128], [1, 2048]]), "LB2",
                      waits=[db])
        lastv = 0
        for c in range(DC):
            w2 = [ba, bb] if c == 0 else []
            if c == 0 and xn_free:
                w2.append(self.wP(xn_free))
            self.dve(lambda q, c=c: q.tensor_tensor(self.TXN[:], self.X[c][:], self.ABT[:], ALU.mult),
                     waits=w2, chain=True)
            lastv = self.dve(lambda q, c=c: q.tensor_tensor(self.XN[c][:], self.TXN[:], self.B2T[:],
                                                            ALU.subtract), chain=True)
        self.ln_apply_done = lastv
        self.xn_ready = lastv

    def emit_attn(self, causal, wq_d, wk_d, wv_d, wo_d, with_bias):
        PSA, PSB = self.PSA, self.PSB
        xn_ready = self.xn_ready

        gate = self.phase_gate()
        ws_cons = {-2: self.ws_free, -1: self.ws_free}
        k_evac = {}
        for m in range(DC):
            slot = m % 2
            w = [self.wP(ws_cons[m - 2])] if ws_cons[m - 2] else []
            if m == 0:
                w += gate
            wv_ = self.wld(self.WS[slot][:], wk_d[m], "WS%d" % slot, waits=w)
            ps = PSA if m % 2 == 0 else PSB
            first_w = [wv_, self.wV(xn_ready)]
            if m >= 2:
                first_w.append(self.wA(k_evac[m - 2]))
            last = 0
            for g in range(4):
                for k in range(DC):
                    last = self.pe(lambda q, ps=ps, g=g, k=k, slot=slot: q.matmul(
                        ps[:, g * 512:(g + 1) * 512], self.WS[slot][:, k * 128:(k + 1) * 128],
                        self.XN[k][:, g * 512:(g + 1) * 512], start=(k == 0), stop=(k == DC - 1),
                        skip_group_check=True),
                        inc=(g == 3 and k == DC - 1),
                        waits=(first_w if (g == 0 and k == 0) else ()))
            ws_cons[m] = last
            k_evac[m] = self.act(lambda q, m=m, ps=ps: q.activation(self.K[m][:], ps[:], AF.Copy, scale=0.125),
                                 inc=True, waits=[self.wP(last)])
        kn_done = 0
        for m in range(DC):
            wk_ = [self.wA(k_evac[m])]
            if m > 0:
                wk_.append(self.wP(kn_done))
            v = self.dve(lambda q, m=m: q.tensor_tensor(self.KSQ[:], self.K[m][:], self.K[m][:], ALU.mult),
                         waits=wk_)
            for g in range(4):
                wkn = ([self.wV(v)] if g == 0 else [])
                if m == 0 and g == 0:
                    wkn.append(self.wA(k_evac[DC - 2]))
                kn_done = self.pe(lambda q, m=m, g=g: q.matmul(
                    PSA[0:16, g * 512:(g + 1) * 512], self.cb_hind(m), self.KSQ[:, g * 512:(g + 1) * 512],
                    start=(m == 0), stop=(m == DC - 1), skip_group_check=True),
                    inc=(g == 3), waits=wkn)
        km_v = self.dve(lambda q: q.reduce_max(self.KMX[:], PSA[0:16, 0:2048], axis=AX.X),
                        inc=True, waits=[self.wP(kn_done)])

        wv_last = 0
        for k in range(DC):
            wv_last = self.wld(self.WV[:, k * 1024:(k + 1) * 1024], wv_d[k * 128:(k + 1) * 128, :],
                               "WV", waits=([self.wP(kn_done)] if k == 0 else ()))
        ones_done = 0
        for t in range(NTT):
            ones_done = self.dve(lambda q, t=t: q.memset(
                bass.AP(self.VH[t].tensor, 64, [[1040, 128], [65, 16]]), 1.0),
                inc=(t == NTT - 1), waits=([self.wV(km_v)] if t == 0 else ()))
        v_evac = {}
        for t in range(NTT):
            ps = PSA if t % 2 == 0 else PSB
            w0 = [wv_last, self.wV(km_v)]
            if t >= 2:
                w0.append(self.wV(v_evac[t - 2]))
            last = 0
            for k in range(DC):
                for hh in range(2):
                    last = self.pe(lambda q, ps=ps, t=t, k=k, hh=hh: q.matmul(
                        ps[:, hh * 512:(hh + 1) * 512], self.XN[k][:, t * 128:(t + 1) * 128],
                        self.WV[:, k * 1024 + hh * 512: k * 1024 + (hh + 1) * 512],
                        start=(k == 0), stop=(k == DC - 1), skip_group_check=True),
                        inc=(k == DC - 1 and hh == 1),
                        waits=(w0 if (k == 0 and hh == 0) else ()))
            v_evac[t] = self.dve(lambda q, t=t, ps=ps: q.tensor_copy(
                bass.AP(self.VH[t].tensor, 0, [[1040, 128], [65, 16], [1, 64]]),
                bass.AP(ps, 0, [[2048, 128], [64, 16], [1, 64]])),
                inc=True, waits=[self.wP(last), self.wV(ones_done)])
        v_done = v_evac[NTT - 1]

        if causal:
            tm_tgt = self.ld(self.TM[:], self.d_trimask, "TM", waits=[self.wV(v_done)])
        else:
            tm_tgt = None

        exp_side = {"A": 0, "B1": 0, "B2": 0}
        av_done = {"A": 0, "B1": 0, "B2": 0}
        epi_mult = {-1: 0}
        ctx_free = 0
        prev_op_done = 0
        q_last = 0
        epi_idx = [0]
        ctx_banks_free = [0]
        for qb in range(NQB):
            q_evac = {}
            ws_cons[-2], ws_cons[-1] = ws_cons[DC - 2], ws_cons[DC - 1]
            for m in range(DC):
                slot = m % 2
                w = [self.wP(ws_cons[m - 2])] if ws_cons[m - 2] else []
                wv_ = self.wld(self.WS[slot][:], wq_d[m], "WS%d" % slot, waits=w)
                psq = PSB[:, 0:512] if m % 2 == 0 else PSB[:, 512:1024]
                w0 = [wv_]
                if qb == 0 and m == 0:
                    w0.append(self.wV(v_done))
                    if causal:
                        w0.append(tm_tgt)
                if qb > 0 and m == 0:
                    w0.append(self.wA(exp_side["B1"]))
                if m >= 2:
                    w0.append(self.wA(q_evac[m - 2]))
                last = 0
                for k in range(DC):
                    last = self.pe(lambda q, psq=psq, k=k, slot=slot, qb=qb: q.matmul(
                        psq, self.WS[slot][:, k * 128:(k + 1) * 128],
                        self.XN[k][:, qb * 512:(qb + 1) * 512], start=(k == 0), stop=(k == DC - 1),
                        skip_group_check=True),
                        inc=(k == DC - 1), waits=(w0 if k == 0 else ()))
                ws_cons[m] = last
                q_last = last
                q_evac[m] = self.act(lambda q, m=m, psq=psq: q.activation(self.Q[m][:], psq, AF.Copy, scale=1.0),
                                     waits=[self.wP(last)])
            qn = 0
            for m in range(DC):
                wq_ = [self.wA(q_evac[m])]
                if m > 0:
                    wq_.append(self.wP(qn))
                if m == 0 and prev_op_done:
                    wq_.append(self.wV(prev_op_done))
                v = self.dve(lambda q, m=m: q.tensor_tensor(self.QSQ[:], self.Q[m][:], self.Q[m][:], ALU.mult),
                             waits=wq_)
                qn = self.pe(lambda q, m=m: q.matmul(
                    PSB[0:16, 1024:1536], self.cb_hind(m), self.QSQ[:],
                    start=(m == 0), stop=(m == DC - 1), skip_group_check=True),
                    inc=True, waits=[self.wV(v)])
            rmax_v = self.dve(lambda q: q.reduce_max(self.CBIA[0:16, 17:18], PSB[0:16, 1024:1536],
                                                     axis=AX.X), waits=[self.wP(qn)])
            vcs = self.dve(lambda q: q.tensor_tensor(self.CBIA[0:16, 18:19], self.CBIA[0:16, 17:18],
                                                     self.KMX[:], ALU.mult), chain=True)
            wnc = [self.wV(vcs)]
            if self.nc2_prev:
                wnc.append(self.nc2_prev)
            d1 = self.ld(bass.AP(self.d_ncd.tensor, 0, [[1, 16], [1, 1]]), self.CBIA[0:16, 18:19],
                         "NC1", waits=wnc)
            d2 = self.ld(self.CBIA[0:1, 0:16], bass.AP(self.d_ncd.tensor, 0, [[16, 1], [1, 16]]),
                         "NC2", waits=[d1])
            self.nc2_prev = d2
            vgm = self.dve(lambda q: q.reduce_max(self.CBIA[0:1, 19:20], self.CBIA[0:1, 0:16], axis=AX.X),
                           waits=[d2], chain=True)
            self.act(lambda q: q.activation(self.CBIA[0:1, 20:21], self.CBIA[0:1, 19:20], AF.Ln, scale=1.0),
                     waits=[self.wV(vgm)])
            ace = self.act(lambda q: q.activation(self.CBIA[0:1, 20:21], self.CBIA[0:1, 20:21], AF.Exp,
                                                  scale=0.5), chain=True)
            vng = self.dve(lambda q: q.tensor_scalar(self.CBIA[0:1, 21:22], self.CBIA[0:1, 20:21],
                                                     -1.0, None, ALU.mult), waits=[self.wA(ace)])
            wnc3 = [self.wV(vng)]
            if self.nc4_prev:
                wnc3.append(self.nc4_prev)
            d3 = self.ld(bass.AP(self.d_ncd.tensor, 16, [[1, 1], [1, 1]]), self.CBIA[0:1, 21:22],
                         "NC3", waits=wnc3)
            cb_rdy = self.ld(self.CBIA[:, 16:17], bass.AP(self.d_ncd.tensor, 16, [[0, 128], [1, 1]]),
                             "NC4", waits=[d3])
            self.nc4_prev = cb_rdy

            nkt = 4 * qb + 4 if causal else 16
            SIDES = ("A", "B1", "B2")
            GRP = {"A": (PSA, 1024), "B1": (PSB, 0), "B2": (PSB, 1024)}
            PTS = {"A": self.PTA, "B1": self.PTB[:, 0:1024], "B2": self.PTB[:, 1024:2048]}
            first_use = {"A": True, "B1": True, "B2": True}

            def emit_S(c, side, kt, first_of_qb):
                base, go = GRP[side]
                w = []
                if exp_side[side]:
                    w.append(self.wA(exp_side[side]))
                if first_of_qb:
                    if qb == 0:
                        w.append(self.wV(v_done))
                if first_use[side]:
                    first_use[side] = False
                    if side == "B2":
                        w.append(self.wV(rmax_v))
                    if side == "B1":
                        w.append(self.wA(q_evac[DC - 1]))
                pm = 0
                for hh in range(2):
                    so = go + hh * 512
                    pm = self.pe(lambda q, base=base, so=so, c=c, hh=hh, kt=kt: q.matmul(
                        base[:, so:so + 512], self.K[c][hh * 64:(hh + 1) * 64, kt * 128:(kt + 1) * 128],
                        self.Q[c][hh * 64:(hh + 1) * 64, :], start=True, stop=True,
                        skip_group_check=True),
                        inc=(hh == 1), waits=(w if hh == 0 else ()))
                mask_v = 0
                if causal:
                    r = kt - 4 * qb
                    if 0 <= r <= 3:
                        mask_v = self.dve(lambda q, base=base, go=go, r=r: q.tensor_tensor(
                            base[:, go:go + 1024], base[:, go:go + 1024],
                            bass.AP(self.TM.tensor, r * 512, [[2048, 128], [0, 2], [1, 512]]),
                            ALU.add), waits=[self.wP(pm)])
                return pm, mask_v

            def emit_E(c, side, kt, pm, mask_v, first_exp):
                base, go = GRP[side]
                ptb = PTS[side]
                we = [self.wP(pm)]
                if mask_v:
                    we.append(self.wV(mask_v))
                if av_done[side]:
                    we.append(self.wP(av_done[side]))
                if first_exp:
                    we.append(cb_rdy)
                ae = self.act(lambda q, base=base, go=go, ptb=ptb: q.activation(
                    ptb, base[:, go:go + 1024], AF.Exp, bias=self.CBIA[:, 16:17], scale=1.0),
                    waits=we)
                exp_side[side] = ae
                return ae

            def emit_AV(item):
                c, side, kt, ae = item
                ptb = PTS[side]
                last = 0
                for hh in range(2):
                    so = hh * 512
                    w = [self.wA(ae)]
                    if kt == 0 and hh == 0 and ctx_banks_free[0]:
                        w.append(ctx_banks_free[0])
                    last = self.pe(lambda q, kt=kt, c=c, hh=hh, so=so, ptb=ptb, nkt=nkt: q.matmul(
                        PSA[0:65, hh * 512: hh * 512 + 512],
                        self.VH[kt][:, (2 * c + hh) * 65:(2 * c + hh) * 65 + 65],
                        ptb[:, so:so + 512],
                        start=(kt == 0), stop=(kt == nkt - 1), skip_group_check=True),
                        inc=(hh == 1), waits=(w if hh == 0 else ()))
                av_done[side] = last
                return last

            def emit_epilogue(c, av_last, ei, last_of_qb=False):
                par = ei % 2
                wev = [self.wP(av_last)]
                if epi_mult.get(ei - 2):
                    wev.append(self.wV(epi_mult[ei - 2]))
                ev = self.dve(lambda q, par=par: q.tensor_copy(
                    self.CTXRAW[par][0:65, :], PSA[0:65, 0:1024]), waits=wev)
                ctx_banks_free[0] = (self.sV, ev)
                if last_of_qb:
                    self.act(lambda q, par=par: q.activation(
                        self.RROW[:], self.CTXRAW[par][64:65, :], AF.Ln, scale=1.0),
                        waits=[self.wV(ev)])
                    ra = self.act(lambda q, par=par: q.activation(
                        self.RBS[par][0:1, :], self.RROW[:], AF.Exp, scale=-1.0), chain=True)
                    wd1 = [self.wA(ra)]
                else:
                    def _recip(q, par=par):
                        with self.nc.allow_low_precision(reason="softmax row-sum reciprocal stored bf16 by design"):
                            return q.reciprocal(self.RBS[par][0:1, :], self.CTXRAW[par][64:65, :])
                    rv = self.dve(_recip, chain=True)
                    wd1 = [self.wV(rv)]
                d1 = self.ld(self.d_rsd[par, :], self.RBS[par][0:1, :], "RS%d" % par,
                             waits=wd1)
                wd2 = [d1]
                if epi_mult.get(ei - 2):
                    wd2.append(self.wV(epi_mult[ei - 2]))
                d2 = self.ld(self.RBS[par][0:64, :],
                             bass.AP(self.d_rsd.tensor, par * 1024, [[0, 64], [1, 1024]]),
                             "RB%d" % par, waits=wd2)
                wctx = [d2]
                if ctx_free:
                    wctx.append(self.wP(ctx_free))
                self.dve(lambda q, c=c, par=par: q.tensor_tensor(
                    self.CTX[c][0:64, :], self.CTXRAW[par][0:64, 0:512],
                    self.RBS[par][0:64, 0:512], ALU.mult), waits=wctx)
                epi_mult[ei] = self.dve(lambda q, c=c, par=par: q.tensor_tensor(
                    self.CTX[c][64:128, :], self.CTXRAW[par][0:64, 512:1024],
                    self.RBS[par][0:64, 512:1024], ALU.mult), chain=True)

            groups = []
            for c in range(DC):
                for kt in range(nkt):
                    groups.append((c, SIDES[(c * nkt + kt) % 3], kt))
            fifo = []
            first_exp_of_qb = True
            for gi, (c, side, kt) in enumerate(groups):
                pm, mask_v = emit_S(c, side, kt, first_of_qb=(gi == 0))
                ae = emit_E(c, side, kt, pm, mask_v, first_exp_of_qb)
                first_exp_of_qb = False
                fifo.append((c, side, kt, ae))
                while len(fifo) > 2:
                    item = fifo.pop(0)
                    av_last = emit_AV(item)
                    if item[2] == nkt - 1:
                        ei = epi_idx[0]
                        emit_epilogue(item[0], av_last, ei, last_of_qb=(item[0] >= DC - 2))
                        epi_idx[0] = ei + 1
            while fifo:
                item = fifo.pop(0)
                av_last = emit_AV(item)
                if item[2] == nkt - 1:
                    ei = epi_idx[0]
                    emit_epilogue(item[0], av_last, ei, last_of_qb=(item[0] >= DC - 2))
                    epi_idx[0] = ei + 1
            ctx_done = epi_mult[epi_idx[0] - 1]

            op_v = {}
            ws_cons[-2], ws_cons[-1] = ws_cons[DC - 2], ws_cons[DC - 1]
            for m in range(DC):
                slot = m % 2
                w = [self.wP(ws_cons[m - 2])] if ws_cons[m - 2] else []
                wv_ = self.wld(self.WS[slot][:], wo_d[m], "WS%d" % slot, waits=w)
                op = PSB[:, 1024:1536] if m % 2 == 0 else PSB[:, 1536:2048]
                w0 = [wv_]
                if m == 0:
                    w0 += [self.wV(ctx_done), self.wA(exp_side["B2"])]
                if m >= 2:
                    w0.append(self.wV(op_v[m - 2]))
                last = 0
                for k in range(DC):
                    last = self.pe(lambda q, op=op, k=k, slot=slot: q.matmul(
                        op, self.WS[slot][:, k * 128:(k + 1) * 128], self.CTX[k][:],
                        start=(k == 0), stop=(k == DC - 1), skip_group_check=True),
                        inc=(k == DC - 1), waits=(w0 if k == 0 else ()))
                ws_cons[m] = last
                bias_col = self.b_bo(m) if with_bias else self.b_z
                wst = [self.wP(last)] + (self.x_dep if (qb == 0 and m == 0) else [])
                op_v[m] = self.dve(lambda q, m=m, qb=qb, op=op, bias_col=bias_col: q.scalar_tensor_tensor(
                    self.X[m][:, qb * 512:(qb + 1) * 512], op, bias_col,
                    self.X[m][:, qb * 512:(qb + 1) * 512], ALU.add, ALU.add),
                    waits=wst)
            ctx_free = ws_cons[DC - 1]
            prev_op_done = op_v[DC - 1]
        self.ws_free = ws_cons[DC - 1]
        self.ln_gate = []
        for m in range(DC):
            self.x_chunk_dep[m] = [self.wV(op_v[m])]
        self.x_dep = [self.wV(self.C["V"])]
        self.xn_free = q_last

    def emit_ffn(self, w0_d, w1_d, w2_d, gw_d):
        PSA, PSB = self.PSA, self.PSB
        xn_ready = self.xn_ready
        wg_tgt = self.wld(self.WG[:], gw_d, "WG", waits=self.phase_gate())
        last = 0
        for tt in range(NTT):
            for dc in range(DC):
                last = self.pe(lambda q, tt=tt, dc=dc: q.matmul(
                    PSA[:, tt * 8:(tt + 1) * 8], self.XN[dc][:, tt * 128:(tt + 1) * 128],
                    self.WG[:, dc * 8:(dc + 1) * 8],
                    start=(dc == 0), stop=(dc == DC - 1), skip_group_check=True),
                    inc=(dc == DC - 1 and tt == NTT - 1),
                    waits=([wg_tgt, self.wV(xn_ready)] if (tt == 0 and dc == 0) else ()))
        bo = BIAS_OFF
        gb_rep = bass.AP(self.BIAS.tensor, bo["gb"], [[BIAS_W, 128], [0, 16], [1, 8]])
        g3 = bass.AP(self.GF.tensor, 0, [[128, 128], [8, 16], [1, 8]])
        gm3 = bass.AP(self.GM.tensor, 0, [[128, 128], [8, 16], [1, 8]])
        v1rep = bass.AP(self.GV.tensor, 0, [[64, 128], [1, 16], [0, 8]])
        v2rep = bass.AP(self.GV.tensor, 16, [[64, 128], [1, 16], [0, 8]])
        w1rep = bass.AP(self.GV.tensor, 32, [[64, 128], [1, 16], [0, 8]])
        w2rep = bass.AP(self.GV.tensor, 48, [[64, 128], [1, 16], [0, 8]])
        self.dve(lambda q: q.tensor_tensor(self.GF[:], PSA[:, 0:128], gb_rep, ALU.add),
                 waits=[self.wP(last)])
        self.dve(lambda q: q.tensor_reduce(self.GV[:, 0:16], g3, axis=AX.X, op=ALU.max), chain=True)
        self.dve(lambda q: q.tensor_tensor(self.GI[:], self.GF[:], v1rep, ALU.is_equal), chain=True)
        self.dve(lambda q: q.scalar_tensor_tensor(self.GM[:], self.GI[:], NEG, self.GF[:],
                                                  ALU.mult, ALU.add), chain=True)
        self.dve(lambda q: q.tensor_reduce(self.GV[:, 16:32], gm3, axis=AX.X, op=ALU.max), chain=True)
        self.dve(lambda q: q.tensor_tensor(self.GM[:], self.GM[:], v2rep, ALU.is_equal), chain=True)
        vd = self.dve(lambda q: q.tensor_tensor(self.GV[:, 32:48], self.GV[:, 16:32], self.GV[:, 0:16],
                                                ALU.subtract), chain=True)
        asg = self.act(lambda q: q.activation(self.GV[:, 48:64], self.GV[:, 32:48], AF.Sigmoid,
                                              scale=1.0), waits=[self.wV(vd)])
        self.dve(lambda q: q.tensor_scalar(self.GV[:, 32:48], self.GV[:, 48:64], -1.0, 1.0,
                                           ALU.mult, ALU.add), waits=[self.wA(asg)], chain=True)
        self.dve(lambda q: q.tensor_tensor(self.GI[:], self.GI[:], w1rep, ALU.mult), chain=True)
        self.dve(lambda q: q.tensor_tensor(self.GM[:], self.GM[:], w2rep, ALU.mult), chain=True)
        vm = self.dve(lambda q: q.tensor_tensor(self.MSC[:], self.GI[:], self.GM[:], ALU.add),
                      chain=True)
        gate_v = {}
        mt_v = 0
        tprev = [0, 0]
        for tt in range(NTT):
            spot = 1024 + (tt % 2) * 128
            wtm = [self.wV(vm)] if tt == 0 else []
            if tprev[tt % 2]:
                wtm.append(self.wV(tprev[tt % 2]))
            mt_m = self.pe(lambda q, tt=tt, spot=spot: q.matmul(
                PSA[0:8, spot:spot + 128], self.MSC[:, tt * 8:(tt + 1) * 8], self.cb_id,
                start=True, stop=True, skip_group_check=True),
                inc=True, waits=wtm)
            mt_v = self.dve(lambda q, tt=tt, spot=spot: q.tensor_copy(
                self.MT[0:8, tt * 128:(tt + 1) * 128], PSA[0:8, spot:spot + 128]),
                waits=[self.wP(mt_m)])
            tprev[tt % 2] = mt_v
        gate_done = mt_v

        w01_cons = {-2: self.w01_free, -1: self.w01_free}
        xn_last = 0
        last_stt = dict(self.prev_ffn_stt)
        last_cmb = self.prev_cmb
        for e in range(E):
            if e > 0:
                w01_cons[-2] = w01_cons[HC - 2]
                w01_cons[-1] = w01_cons[HC - 1]
            act_v = {}
            for hh in range(HC):
                slot = hh % 2
                w = [self.wP(w01_cons[hh - 2])] if w01_cons[hh - 2] else []
                wa = self.wld(self.W01[slot][:, 0:1024], w0_d[e, hh], "W01%d" % slot, waits=w)
                wb = self.wld(self.W01[slot][:, 1024:2048], w1_d[e, hh], "W01%d" % slot)
                for half in range(2):
                    hs = half * 1024
                    w0 = [wb]
                    if e == 0 and hh == 0 and half == 0:
                        w0.append(self.wV(gate_done))
                    if hh == 0 and last_cmb:
                        w0.append(self.wV(last_cmb))
                    if act_v.get((hh - 1, half)):
                        w0.append(self.wV(act_v[(hh - 1, half)]))
                    last = 0
                    for dc in range(DC):
                        for sub in range(2):
                            ss = hs + sub * 512
                            self.pe(lambda q, ss=ss, dc=dc, slot=slot: q.matmul(
                                PSA[:, ss:ss + 512], self.W01[slot][:, dc * 128:(dc + 1) * 128],
                                self.XN[dc][:, ss:ss + 512], start=(dc == 0), stop=(dc == DC - 1),
                                skip_group_check=True),
                                waits=(w0 if (dc == 0 and sub == 0) else ()))
                            last = self.pe(lambda q, ss=ss, dc=dc, slot=slot: q.matmul(
                                PSB[:, ss:ss + 512], self.W01[slot][:, 1024 + dc * 128: 1024 + (dc + 1) * 128],
                                self.XN[dc][:, ss:ss + 512], start=(dc == 0), stop=(dc == DC - 1),
                                skip_group_check=True),
                                inc=(dc == DC - 1 and sub == 1))
                    if half == 1:
                        w01_cons[hh] = last
                    xn_last = last
                    wsl = [self.wP(last)]
                    prev_stt = act_v.get((hh - 1, half), 0) or last_stt.get(half, 0)
                    if prev_stt:
                        wsl.append(self.wV(prev_stt))
                    a = self.act(lambda q, hs=hs, e=e, hh=hh: q.activation(
                        self.SIL[:, hs:hs + 1024], PSB[:, hs:hs + 1024], AF.Silu,
                        bias=self.b_b1(e, hh), scale=1.0), waits=wsl)
                    act_v[(hh, half)] = self.dve(lambda q, hs=hs, e=e, hh=hh: q.scalar_tensor_tensor(
                        self.ACT[hh][:, hs:hs + 1024], PSA[:, hs:hs + 1024], self.b_b0(e, hh),
                        self.SIL[:, hs:hs + 1024], ALU.add, ALU.mult), waits=[self.wA(a)])
            act_done = act_v[(HC - 1, 1)]
            mb_a = 0
            for ts in range(4):
                w0 = [self.wV(act_done)] if ts == 0 else [self.wA(mb_a)]
                if ts == 0 and last_cmb:
                    w0.append(self.wV(last_cmb))
                mb_m = self.pe(lambda q, ts=ts, e=e: q.matmul(
                    PSB[:, 0:512], self.cb_sel(e), self.MT[0:8, ts * 512:(ts + 1) * 512],
                    start=True, stop=True, skip_group_check=True), inc=True, waits=w0)
                mb_a = self.act(lambda q, ts=ts: q.activation(self.MBS[:, ts * 512:(ts + 1) * 512],
                                                              PSB[:, 0:512], AF.Copy, scale=1.0),
                                inc=True, waits=[self.wP(mb_m)])
            if e == 0:
                self._w2carry = [self.w2b_free, self.w2b_free]
            w2_cons = {-2: self._w2carry[0], -1: self._w2carry[1]}
            cmb_v = {}
            for dc in range(DC):
                slot = dc % 2
                w = [self.wP(w2_cons[dc - 2])] if w2_cons[dc - 2] else []
                wv_ = self.wld(self.W2B[slot][:], w2_d[e, dc], "W2B%d" % slot, waits=w)
                for ts in range(4):
                    op = PSA[:, ts * 512:(ts + 1) * 512]
                    w0 = [wv_]
                    if dc == 0 and ts == 0:
                        w0.append(self.wA(mb_a))
                    if cmb_v.get((dc - 1, ts)):
                        w0.append(self.wV(cmb_v[(dc - 1, ts)]))
                    last = 0
                    for hc in range(HC):
                        last = self.pe(lambda q, op=op, hc=hc, ts=ts, slot=slot: q.matmul(
                            op, self.W2B[slot][:, hc * 128:(hc + 1) * 128],
                            self.ACT[hc][:, ts * 512:(ts + 1) * 512],
                            start=(hc == 0), stop=(hc == HC - 1), skip_group_check=True),
                            inc=(hc == HC - 1), waits=(w0 if hc == 0 else ()))
                    if ts == 3:
                        w2_cons[dc] = last
                    wst = [self.wP(last)]
                    if e == 0 and dc == 0 and ts == 0:
                        wst += self.x_dep
                    self.dve(lambda q, op=op, e=e, dc=dc, ts=ts: q.scalar_tensor_tensor(
                        self.TMPF[:], op, self.b_b2(e, dc), self.MBS[:, ts * 512:(ts + 1) * 512],
                        ALU.add, ALU.mult), waits=wst, chain=True)
                    cmb_v[(dc, ts)] = self.dve(lambda q, dc=dc, ts=ts: q.tensor_tensor(
                        self.X[dc][:, ts * 512:(ts + 1) * 512], self.X[dc][:, ts * 512:(ts + 1) * 512],
                        self.TMPF[:], ALU.add), chain=True)
            self._w2carry = [w2_cons[DC - 2], w2_cons[DC - 1]]
            last_cmb = cmb_v[(DC - 1, 3)]
            last_stt = {0: act_v[(HC - 1, 0)], 1: act_v[(HC - 1, 1)]}
        self.w01_free = w01_cons[HC - 1]
        self.w2b_free = w2_cons[DC - 1]
        self.ln_gate = [self.wP(w2_cons[DC - 1]), self.wV(cmb_v[(DC - 1, 3)])]
        for dc in range(DC):
            self.x_chunk_dep[dc] = [self.wV(cmb_v[(dc, 3)])]
        self.prev_cmb = last_cmb
        self.prev_ffn_stt = dict(last_stt)
        self.x_dep = [self.wV(self.C["V"])]
        self.xn_free = xn_last

    def emit_all(self):
        self.emit_prologue()
        store_done = None
        for s in range(NSEQ):
            self.emit_load_x(s, after=([store_done] if s else ()))
            subs = ["attn_a", "ffn", "attn_b", "ffn"][: self.n_sub]
            for sub in subs:
                self.emit_ln()
                if sub == "attn_a":
                    self.emit_attn(True, self.d_wq, self.d_wk, self.d_wv, self.d_wo, True)
                elif sub == "attn_b":
                    self.emit_attn(False, self.d_uq, self.d_uk, self.d_uv, self.d_uo, False)
                else:
                    self.emit_ffn(self.d_w0t, self.d_w1t, self.d_w2t, self.d_gwt)
            store_done = self.emit_store_x(s)
            self.xn_free = 0
            self.xn_free = self.C["P"] and 0
        self.final_out = store_done

    def finalize(self):
        nc = self.nc
        OPS = self.OPS
        final_out = self.final_out
        with nc.Block() as block:
            @block.sync
            def _(q):
                for fn in OPS["sync"]:
                    fn(q)
                q.wait_ge(final_out[0], final_out[1])

            @block.gpsimd
            def _(q):
                for fn in OPS["gp"]:
                    fn(q)

            @block.tensor
            def _(q):
                for fn in OPS["pe"]:
                    fn(q)

            @block.scalar
            def _(q):
                for fn in OPS["act"]:
                    fn(q)

            @block.vector
            def _(q):
                for fn in OPS["dve"]:
                    fn(q)
        for cm in reversed(self._ctx):
            cm.__exit__(None, None, None)
        return nc


def build_program(n_sub=4):
    b = B(n_sub)
    b.emit_all()
    return b.finalize()


def kernel(**inputs):
    ins = {k: np.ascontiguousarray(np.asarray(v, dtype=np.float32)) for k, v in inputs.items()}
    blobs = _host_blobs(ins)
    x = ins["L0_attn_a_x"]
    nc = build_program(4)
    shared = {k: blobs[k] for k in ["wq", "wk", "wv", "wo", "uq", "uk", "uv", "uo",
                                    "w0t", "w1t", "w2t", "gwt", "biasblob", "constsb", "trimask"]}
    in_maps = []
    for c in range(NCORE):
        xt = np.ascontiguousarray(x[c * NSEQ:(c + 1) * NSEQ].transpose(0, 2, 1))
        m = dict(shared)
        m["xt"] = xt
        in_maps.append(m)
    res = run_bass_kernel_spmd(nc, in_maps, core_ids=list(range(NCORE)))
    outs = []
    for c in range(NCORE):
        o = res.results[c]["out"]
        outs.append(np.asarray(o).transpose(0, 2, 1))
    return np.ascontiguousarray(np.concatenate(outs, axis=0))


# version 10 (final)
# speedup vs baseline: 1.1368x; 1.0100x over previous
import numpy as np
import concourse.bass as bass
import concourse.mybir as mybir
from concourse.bass_utils import run_bass_kernel_spmd

F32 = mybir.dt.float32
BF16 = mybir.dt.bfloat16
AF = mybir.ActivationFunctionType
ALU = mybir.AluOpType
AX = mybir.AxisListType

D, T, H, HD, E, HF = 1024, 2048, 16, 64, 8, 2048
DC, HC, NTT, NQB = 8, 16, 16, 4
NCORE, NSEQ = 8, 2
NEG = -1e30

BIAS_OFF = {"b0": 0, "b1": 128, "b2": 256, "bo": 320, "gb": 328, "z": 336,
            "ones": 337, "id16": 465, "id128": 481, "fc": 609}
EPS_BIAS = float(1024.0 * 1024.0 * 1e-5)
LN1024 = float(np.log(1024.0))
FCONSTS = [EPS_BIAS, 1024.0, -1.0, 1.0, -1e30, 2.0, 0.0, 0.125, LN1024, 0.5, -0.5]
BIAS_W = 609 + len(FCONSTS)
CB_W = 1408


def _host_blobs(ins):
    f = np.float32
    import ml_dtypes
    out = {}

    def mtile(W):
        return np.ascontiguousarray(W.reshape(8, 128, 8, 128).transpose(2, 1, 0, 3)).reshape(8, 128, 1024)

    out["wq"] = mtile(ins["L0_attn_a_Wq"])
    out["wk"] = mtile(ins["L0_attn_a_Wk"])
    out["wo"] = mtile(ins["L0_attn_a_Wo"])
    out["wv"] = np.ascontiguousarray(ins["L0_attn_a_Wv"])
    uq = np.ascontiguousarray(ins["L0_attn_b_W_q"].transpose(1, 0, 2).reshape(1024, 1024))
    uk = np.ascontiguousarray(ins["L0_attn_b_W_k"].transpose(1, 0, 2).reshape(1024, 1024))
    uv = np.ascontiguousarray(ins["L0_attn_b_W_v"].transpose(1, 0, 2).reshape(1024, 1024))
    out["uq"] = mtile(uq)
    out["uk"] = mtile(uk)
    out["uo"] = mtile(ins["L0_attn_b_W_o"])
    out["uv"] = uv
    out["w0t"] = np.ascontiguousarray(
        ins["L0_ffn_w0"].reshape(8, 16, 128, 8, 128).transpose(0, 1, 4, 3, 2)).reshape(8, 16, 128, 1024)
    out["w1t"] = np.ascontiguousarray(
        ins["L0_ffn_w1"].reshape(8, 16, 128, 8, 128).transpose(0, 1, 4, 3, 2)).reshape(8, 16, 128, 1024)
    out["w2t"] = np.ascontiguousarray(
        ins["L0_ffn_w2"].reshape(8, 8, 128, 16, 128).transpose(0, 1, 4, 3, 2)).reshape(8, 8, 128, 2048)
    out["gwt"] = np.ascontiguousarray(
        ins["L0_ffn_gate_w"].T.reshape(8, 128, 8).transpose(1, 0, 2)).reshape(128, 64)

    b0c = np.ascontiguousarray(ins["L0_ffn_b0"].reshape(8, 16, 128).transpose(2, 0, 1)).reshape(128, 128)
    b1c = np.ascontiguousarray(ins["L0_ffn_b1"].reshape(8, 16, 128).transpose(2, 0, 1)).reshape(128, 128)
    b2c = np.ascontiguousarray(ins["L0_ffn_b2"].reshape(8, 8, 128).transpose(2, 0, 1)).reshape(128, 64)
    boc = np.ascontiguousarray(ins["L0_attn_a_bo"].reshape(8, 128).T)
    gbb = np.broadcast_to(ins["L0_ffn_gate_b"].astype(f), (128, 8))
    zeros = np.zeros((128, 1), f)
    onesf = np.ones((128, 128), f)
    idf16 = np.zeros((128, 16), f)
    idf16[:16, :] = np.eye(16, dtype=f)
    idf128 = np.eye(128, dtype=f)
    fcc = np.broadcast_to(np.asarray(FCONSTS, f)[None, :], (128, len(FCONSTS)))
    blob = np.concatenate([b0c, b1c, b2c, boc, gbb, zeros, onesf, idf16, idf128, fcc],
                          axis=1).astype(f)
    assert blob.shape[1] == BIAS_W, blob.shape
    out["biasblob"] = np.ascontiguousarray(blob)

    idn = np.eye(128, dtype=f)
    hind = np.zeros((128, 128), f)
    for c in range(8):
        for p in range(128):
            hind[p, c * 16 + (2 * c + p // 64)] = 1.0
    onesb = np.ones((128, 128), f)
    sel = np.zeros((128, 8 * 128), f)
    for e in range(8):
        sel[e, e * 128:(e + 1) * 128] = 1.0
    cb = np.concatenate([idn, hind, onesb, sel], axis=1)
    assert cb.shape[1] == CB_W
    out["constsb"] = np.ascontiguousarray(cb.astype(ml_dtypes.bfloat16))

    tm = np.full((128, 2048), NEG, f)
    for r in range(4):
        for p in range(128):
            j0 = r * 128 + p
            if j0 < 512:
                tm[p, r * 512 + j0: (r + 1) * 512] = 0.0
    out["trimask"] = tm.astype(ml_dtypes.bfloat16)
    return out


class B:
    def __init__(self, n_sub=4):
        self.n_sub = n_sub
        nc = bass.Bass("TRN2", target_bir_lowering=False)
        self.nc = nc
        self._ctx = []
        self.d_xt = nc.dram_tensor("xt", [NSEQ, D, T], F32, kind="ExternalInput").ap()
        self.d_out = nc.dram_tensor("out", [NSEQ, D, T], F32, kind="ExternalOutput").ap()
        for n, shp in [("wq", [8, 128, 1024]), ("wk", [8, 128, 1024]), ("wo", [8, 128, 1024]),
                       ("wv", [1024, 1024]), ("uq", [8, 128, 1024]), ("uk", [8, 128, 1024]),
                       ("uo", [8, 128, 1024]), ("uv", [1024, 1024]),
                       ("w0t", [8, 16, 128, 1024]), ("w1t", [8, 16, 128, 1024]),
                       ("w2t", [8, 8, 128, 2048]), ("gwt", [128, 64])]:
            setattr(self, "d_" + n, nc.dram_tensor(n, shp, F32, kind="ExternalInput").ap())
        self.d_trimask = nc.dram_tensor("trimask", [128, 2048], BF16, kind="ExternalInput").ap()
        self.d_bias = nc.dram_tensor("biasblob", [128, BIAS_W], F32, kind="ExternalInput").ap()
        self.d_rsd = nc.dram_tensor("rsd", [2, 1024], BF16, kind="Internal").ap()
        self.d_ncd = nc.dram_tensor("ncd", [32], F32, kind="Internal").ap()
        self.d_lnr = nc.dram_tensor("lnr", [2, 2048], F32, kind="Internal").ap()
        self.d_cb = nc.dram_tensor("constsb", [128, CB_W], BF16, kind="ExternalInput").ap()

        off = [16512]

        def alloc(name, shape, dt, at=None):
            nbytes = int(np.prod(shape[1:])) * (2 if dt == BF16 else 4)
            if at is None:
                at = off[0]
                off[0] = (at + nbytes + 31) & ~31
            return nc.alloc_sbuf_tensor_at(name, shape, dt, offset=at).ap()

        self.X = [alloc(f"x{c}", [128, 2048], F32) for c in range(DC)]
        self.XN = [alloc(f"xn{c}", [128, 2048], BF16) for c in range(DC)]
        self.BIAS = alloc("bias", [128, BIAS_W], F32)
        self.CB = alloc("cb", [128, CB_W], BF16)
        self.KMX = alloc("kmx", [16, 1], F32)
        AR0 = off[0]
        self.AR0 = AR0
        assert 229344 - AR0 >= 109000, (AR0, 229344 - AR0)

        def ar(name, shape, dt, o):
            return alloc(name, shape, dt, at=AR0 + o)

        self.K = [ar(f"k{c}", [128, 2048], BF16, c * 4096) for c in range(DC)]
        self.VH = [ar(f"vh{t}", [128, 1040], BF16, 32768 + t * 2080) for t in range(NTT)]
        self.Q = [ar(f"q{c}", [128, 512], BF16, 66048 + c * 1024) for c in range(DC)]
        self.CTX = [ar(f"ctx{c}", [128, 512], BF16, 74240 + c * 1024) for c in range(DC)]
        self.PTB = ar("ptb", [128, 2048], BF16, 82432)
        self.WS = [ar(f"ws{i}", [128, 1024], BF16, 86528 + i * 2048) for i in range(2)]
        self.WV = ar("wvs", [128, 8 * 1024], BF16, 90624)
        self.TM = ar("tm", [128, 2048], BF16, 90624)
        self.RBS = [ar("rbs0", [64, 1024], BF16, 94720),
                    ar("rbs1", [64, 1024], BF16, 96768)]
        self.RROW = ar("rrow", [1, 1024], F32, 98816)
        self.CTXRAW = [ar("ctxraw0", [66, 1024], BF16, 102912),
                       ar("ctxraw1", [66, 1024], BF16, 104960)]
        self.KSQ = ar("ksq", [128, 2048], BF16, 66048)
        self.PTA = ar("pta", [128, 1024], BF16, 107008)
        self.QSQ = ar("qsq", [128, 512], BF16, 107008)
        self.CBIA = ar("cbia", [128, 32], F32, 109056)

        self.ACT = [ar(f"act{hh}", [128, 2048], BF16, hh * 4096) for hh in range(HC)]
        self.W01 = [ar(f"w01{i}", [128, 2048], BF16, 65536 + i * 4096) for i in range(2)]
        self.W2B = [ar(f"w2b{i}", [128, 2048], BF16, 73728 + i * 4096) for i in range(2)]
        self.SIL = ar("sil", [128, 2048], F32, 81920)
        self.MBS = ar("mbs", [128, 2048], F32, 90112)
        self.MT = ar("mt", [8, 2048], BF16, 98304)
        self.WG = ar("wg", [128, 64], BF16, 102400)
        self.GF = ar("gf", [128, 128], F32, 102528)
        self.GM = ar("gm", [128, 128], F32, 103040)
        self.GI = ar("gi", [128, 128], F32, 103552)
        self.GV = ar("gv", [128, 64], F32, 104064)
        self.MSC = ar("msc", [128, 128], BF16, 104320)
        self.TMPF = ar("tmpf", [128, 512], F32, 104576)

        self.XB = [ar(f"xb{i}", [128, 2048], BF16, i * 4096) for i in range(2)]
        self.XQ = [ar(f"xq{i}", [128, 2048], BF16, 8192 + i * 4096) for i in range(2)]
        self.LNA = ar("lna", [1, 2048], F32, 16384)
        self.LNB = ar("lnb", [1, 2048], F32, 24576)
        self.LNS = ar("lns", [1, 1024], F32, 32768)
        self.TXN = ar("txn", [128, 2048], F32, 36864)
        self.ABT = ar("abt", [128, 2048], BF16, 45056)
        self.B2T = ar("b2t", [128, 2048], BF16, 49152)

        self.PSA = self._cm(nc.psum_tensor("psa", [128, 2048], F32))
        self.PSB = self._cm(nc.psum_tensor("psb", [128, 2048], F32))

        self.DSEM = {}
        self.DCNT = {}
        for nm in ["C", "XL", "TM", "O", "WS0", "WS1", "WV", "W010", "W011",
                   "W2B0", "W2B1", "WG", "RS0", "RS1", "RB0", "RB1", "NC1", "NC2", "NC3", "NC4",
                   "LR1", "LR2", "LB1", "LB2"] + ["XC%d" % c for c in range(DC)]:
            self.DSEM[nm] = self._cm(nc.semaphore("s" + nm))
            self.DCNT[nm] = 0
        self.sP = self._cm(nc.semaphore("sP"))
        self.sA = self._cm(nc.semaphore("sA"))
        self.sV = self._cm(nc.semaphore("sV"))
        self.C = {"P": 0, "A": 0, "V": 0}
        self.OPS = {"sync": [], "gp": [], "pe": [], "act": [], "dve": []}

        bo = BIAS_OFF
        self.b_b0 = lambda e, hh: self.BIAS[:, bo["b0"] + e * 16 + hh: bo["b0"] + e * 16 + hh + 1]
        self.b_b1 = lambda e, hh: self.BIAS[:, bo["b1"] + e * 16 + hh: bo["b1"] + e * 16 + hh + 1]
        self.b_b2 = lambda e, dc: self.BIAS[:, bo["b2"] + e * 8 + dc: bo["b2"] + e * 8 + dc + 1]
        self.b_bo = lambda dc: self.BIAS[:, bo["bo"] + dc: bo["bo"] + dc + 1]
        self.b_gb = self.BIAS[:, bo["gb"]: bo["gb"] + 8]
        self.b_z = self.BIAS[:, bo["z"]: bo["z"] + 1]
        self.onesf_row = lambda n: self.BIAS[0:1, bo["ones"]: bo["ones"] + n]
        self.id16 = self.BIAS[0:16, bo["id16"]: bo["id16"] + 16]
        for j, v in enumerate(FCONSTS):
            nc.const_aps.register(F32, v, self.BIAS[:, bo["fc"] + j: bo["fc"] + j + 1])
        self.cb_id = self.CB[:, 0:128]
        self.cb_hind = lambda c: self.CB[:, 128 + c * 16: 128 + c * 16 + 16]
        self.cb_onecol = self.CB[:, 256:257]
        self.cb_sel = lambda e: self.CB[0:8, 384 + e * 128: 384 + (e + 1) * 128]

        self.x_dep = []
        self.xn_free = 0
        self.xn_ready = 0
        self.ws_free = 0
        self.w01_free = 0
        self.w2b_free = 0
        self.prev_cmb = 0
        self.prev_ffn_stt = {0: 0, 1: 0}
        self.nc2_prev = None
        self.nc4_prev = None
        self.ln_apply_done = 0
        self.x_chunk_dep = [[] for _ in range(DC)]
        self.ln_gate = []

    def _cm(self, cm):
        v = cm.__enter__()
        self._ctx.append(cm)
        return v

    def wP(self, v): return (self.sP, v)
    def wA(self, v): return (self.sA, v)
    def wV(self, v): return (self.sV, v)

    def _emit(self, eng, fn, inc, waits, chain=False):
        sem = {"pe": self.sP, "act": self.sA, "dve": self.sV}[eng]
        cname = {"pe": "P", "act": "A", "dve": "V"}[eng]
        if chain:
            waits = list(waits) + [(sem, self.C[cname])]
        for s_, v_ in waits:
            if v_ > 0:
                self.OPS[eng].append(lambda q, s=s_, v=v_: q.wait_ge(s, v))
        if inc:
            self.C[cname] += 1
            self.OPS[eng].append(lambda q, fn=fn, sem=sem: fn(q).then_inc(sem, 1))
        else:
            self.OPS[eng].append(lambda q, fn=fn: fn(q))
        return self.C[cname]

    def pe(self, fn, inc=False, waits=(), chain=False):
        return self._emit("pe", fn, inc, waits, chain)

    def act(self, fn, inc=True, waits=(), chain=False):
        return self._emit("act", fn, True, waits, chain)

    def dve(self, fn, inc=True, waits=(), chain=False):
        return self._emit("dve", fn, True, waits, chain)

    def ld(self, dst, src, sem, waits=()):
        for s_, v_ in waits:
            if v_ > 0:
                self.OPS["sync"].append(lambda q, s=s_, v=v_: q.wait_ge(s, v))
        self.DCNT[sem] += 16
        sh = self.DSEM[sem]
        self.OPS["sync"].append(
            lambda q, dst=dst, src=src, sh=sh: q.dma_start(out=dst, in_=src).then_inc(sh, 16))
        return (sh, self.DCNT[sem])

    def wld(self, dst, src, sem, waits=()):
        for s_, v_ in waits:
            if v_ > 0:
                self.OPS["gp"].append(lambda q, s=s_, v=v_: q.wait_ge(s, v))
        self.DCNT[sem] += 16
        sh = self.DSEM[sem]
        self.OPS["gp"].append(
            lambda q, dst=dst, src=src, sh=sh: q.dma_start(out=dst, in_=src).then_inc(sh, 16))
        return (sh, self.DCNT[sem])

    def phase_gate(self):
        return [self.wP(self.C["P"]), self.wA(self.C["A"]), self.wV(self.C["V"])]

    def emit_prologue(self):
        self.ld(self.BIAS[:], self.d_bias, "C")
        self.consts_ready = self.ld(self.CB[:], self.d_cb, "C")

    def emit_load_x(self, s, after=()):
        for c in range(DC):
            nm = "XC%d" % c
            w = [(self.DSEM[nm], self.DCNT[nm])] if self.DCNT[nm] else []
            t = self.ld(self.X[c][:], self.d_xt[s, c * 128:(c + 1) * 128, :], nm, waits=w)
            self.x_chunk_dep[c] = [t]
        self.x_dep = [(self.DSEM["XC%d" % c], self.DCNT["XC%d" % c]) for c in range(DC)]

    def emit_store_x(self, s):
        for c in range(DC):
            self.ld(self.d_out[s, c * 128:(c + 1) * 128, :], self.X[c][:], "XC%d" % c,
                    waits=self.x_chunk_dep[c])
        return [(self.DSEM["XC%d" % c], self.DCNT["XC%d" % c]) for c in range(DC)]

    def emit_ln(self):
        PSA, PSB = self.PSA, self.PSB
        xn_free = self.xn_free
        a_xb, v_xq = [], []
        stats_m = {}
        for c in range(DC):
            i = c % 2
            w = list(self.x_chunk_dep[c])
            if c == 0:
                w.append(self.consts_ready)
                w += self.ln_gate
            if c >= 2:
                w.append(self.wP(stats_m[c - 2]))
            a = self.act(lambda q, c=c, i=i: q.activation(self.XB[i][:], self.X[c][:], AF.Copy, scale=1.0),
                         waits=w)
            a_xb.append(a)
            v = self.dve(lambda q, i=i: q.tensor_tensor(self.XQ[i][:], self.XB[i][:], self.XB[i][:], ALU.mult),
                         waits=[self.wA(a)])
            v_xq.append(v)
            for g in range(4):
                self.pe(lambda q, i=i, g=g, c=c: q.matmul(
                    PSA[0:1, g * 512:(g + 1) * 512], self.cb_onecol, self.XB[i][:, g * 512:(g + 1) * 512],
                    start=(c == 0), stop=(c == DC - 1), skip_group_check=True),
                    waits=([self.wA(a)] if g == 0 else ()))
            for g in range(4):
                m = self.pe(lambda q, i=i, g=g, c=c: q.matmul(
                    PSB[0:1, g * 512:(g + 1) * 512], self.cb_onecol, self.XQ[i][:, g * 512:(g + 1) * 512],
                    start=(c == 0), stop=(c == DC - 1), skip_group_check=True),
                    inc=(g == 3), waits=([self.wV(v)] if g == 0 else ()))
            stats_m[c] = m
        stats_done = stats_m[DC - 1]
        vrows = 0
        for g in range(4):
            S1 = PSA[0:1, g * 512:(g + 1) * 512]
            S2 = PSB[0:1, g * 512:(g + 1) * 512]
            mmr = self.LNS[0:1, 0:512]
            rr = self.LNS[0:1, 512:1024]
            Ar = self.LNA[0:1, g * 512:(g + 1) * 512]
            B2r = self.LNB[0:1, g * 512:(g + 1) * 512]
            asq = self.act(lambda q, S1=S1, mmr=mmr: q.activation(mmr, S1, AF.Square, scale=1.0),
                           waits=([self.wP(stats_done)] if g == 0 else ()), chain=True)
            v0 = self.dve(lambda q, S2=S2, mmr=mmr: q.scalar_tensor_tensor(
                mmr, S2, 1024.0, mmr, ALU.mult, ALU.subtract), waits=[self.wA(asq)])
            self.act(lambda q, mmr=mmr: q.activation(mmr, mmr, AF.Ln, bias=EPS_BIAS, scale=1.0),
                     waits=[self.wV(v0)], chain=True)
            a = self.act(lambda q, mmr=mmr, rr=rr: q.activation(rr, mmr, AF.Exp, scale=-0.5),
                         chain=True)
            self.dve(lambda q, rr=rr, Ar=Ar: q.tensor_scalar(Ar, rr, 1024.0, None, ALU.mult),
                     waits=[self.wA(a)])
            vrows = self.dve(lambda q, S1=S1, rr=rr, B2r=B2r: q.tensor_tensor(B2r, S1, rr, ALU.mult),
                             chain=True)
        da = self.ld(self.d_lnr[0, :], self.LNA[0:1, :], "LR1", waits=[self.wV(vrows)])
        db = self.ld(self.d_lnr[1, :], self.LNB[0:1, :], "LR2")
        wba = [da]
        if self.ln_apply_done:
            wba.append(self.wV(self.ln_apply_done))
        ba = self.wld(self.ABT[:], bass.AP(self.d_lnr.tensor, 0, [[0, 128], [1, 2048]]), "LB1",
                      waits=wba)
        bb = self.wld(self.B2T[:], bass.AP(self.d_lnr.tensor, 2048, [[0, 128], [1, 2048]]), "LB2",
                      waits=[db])
        lastv = 0
        for c in range(DC):
            w2 = [ba, bb] if c == 0 else []
            if c == 0 and xn_free:
                w2.append(self.wP(xn_free))
            self.dve(lambda q, c=c: q.tensor_tensor(self.TXN[:], self.X[c][:], self.ABT[:], ALU.mult),
                     waits=w2, chain=True)
            lastv = self.dve(lambda q, c=c: q.tensor_tensor(self.XN[c][:], self.TXN[:], self.B2T[:],
                                                            ALU.subtract), chain=True)
        self.ln_apply_done = lastv
        self.xn_ready = lastv

    def emit_attn(self, causal, wq_d, wk_d, wv_d, wo_d, with_bias):
        PSA, PSB = self.PSA, self.PSB
        xn_ready = self.xn_ready

        gate = self.phase_gate()
        ws_cons = {-2: self.ws_free, -1: self.ws_free}
        k_evac = {}
        for m in range(DC):
            slot = m % 2
            w = [self.wP(ws_cons[m - 2])] if ws_cons[m - 2] else []
            if m == 0:
                w += gate
            wv_ = self.wld(self.WS[slot][:], wk_d[m], "WS%d" % slot, waits=w)
            ps = PSA if m % 2 == 0 else PSB
            first_w = [wv_, self.wV(xn_ready)]
            if m >= 2:
                first_w.append(self.wA(k_evac[m - 2]))
            last = 0
            for g in range(4):
                for k in range(DC):
                    last = self.pe(lambda q, ps=ps, g=g, k=k, slot=slot: q.matmul(
                        ps[:, g * 512:(g + 1) * 512], self.WS[slot][:, k * 128:(k + 1) * 128],
                        self.XN[k][:, g * 512:(g + 1) * 512], start=(k == 0), stop=(k == DC - 1),
                        skip_group_check=True),
                        inc=(g == 3 and k == DC - 1),
                        waits=(first_w if (g == 0 and k == 0) else ()))
            ws_cons[m] = last
            k_evac[m] = self.act(lambda q, m=m, ps=ps: q.activation(self.K[m][:], ps[:], AF.Copy, scale=0.125),
                                 inc=True, waits=[self.wP(last)])
        kn_done = 0
        for m in range(DC):
            wk_ = [self.wA(k_evac[m])]
            if m > 0:
                wk_.append(self.wP(kn_done))
            v = self.dve(lambda q, m=m: q.tensor_tensor(self.KSQ[:], self.K[m][:], self.K[m][:], ALU.mult),
                         waits=wk_)
            for g in range(4):
                wkn = ([self.wV(v)] if g == 0 else [])
                if m == 0 and g == 0:
                    wkn.append(self.wA(k_evac[DC - 2]))
                kn_done = self.pe(lambda q, m=m, g=g: q.matmul(
                    PSA[0:16, g * 512:(g + 1) * 512], self.cb_hind(m), self.KSQ[:, g * 512:(g + 1) * 512],
                    start=(m == 0), stop=(m == DC - 1), skip_group_check=True),
                    inc=(g == 3), waits=wkn)
        km_v = self.dve(lambda q: q.reduce_max(self.KMX[:], PSA[0:16, 0:2048], axis=AX.X),
                        inc=True, waits=[self.wP(kn_done)])

        wv_last = 0
        for k in range(DC):
            wv_last = self.wld(self.WV[:, k * 1024:(k + 1) * 1024], wv_d[k * 128:(k + 1) * 128, :],
                               "WV", waits=([self.wP(kn_done)] if k == 0 else ()))
        ones_done = 0
        for t in range(NTT):
            ones_done = self.dve(lambda q, t=t: q.memset(
                bass.AP(self.VH[t].tensor, 64, [[1040, 128], [65, 16]]), 1.0),
                inc=(t == NTT - 1), waits=([self.wV(km_v)] if t == 0 else ()))
        v_evac = {}
        for t in range(NTT):
            ps = PSA if t % 2 == 0 else PSB
            w0 = [wv_last, self.wV(km_v)]
            if t >= 2:
                w0.append(self.wV(v_evac[t - 2]))
            last = 0
            for k in range(DC):
                for hh in range(2):
                    last = self.pe(lambda q, ps=ps, t=t, k=k, hh=hh: q.matmul(
                        ps[:, hh * 512:(hh + 1) * 512], self.XN[k][:, t * 128:(t + 1) * 128],
                        self.WV[:, k * 1024 + hh * 512: k * 1024 + (hh + 1) * 512],
                        start=(k == 0), stop=(k == DC - 1), skip_group_check=True),
                        inc=(k == DC - 1 and hh == 1),
                        waits=(w0 if (k == 0 and hh == 0) else ()))
            v_evac[t] = self.dve(lambda q, t=t, ps=ps: q.tensor_copy(
                bass.AP(self.VH[t].tensor, 0, [[1040, 128], [65, 16], [1, 64]]),
                bass.AP(ps, 0, [[2048, 128], [64, 16], [1, 64]])),
                inc=True, waits=[self.wP(last), self.wV(ones_done)])
        v_done = v_evac[NTT - 1]

        if causal:
            tm_tgt = self.ld(self.TM[:], self.d_trimask, "TM", waits=[self.wV(v_done)])
        else:
            tm_tgt = None

        exp_side = {"A": 0, "B1": 0, "B2": 0}
        av_done = {"A": 0, "B1": 0, "B2": 0}
        epi_mult = {-1: 0}
        ctx_free = 0
        prev_op_done = 0
        q_last = 0
        epi_idx = [0]
        ctx_banks_free = [0]
        for qb in range(NQB):
            q_evac = {}
            ws_cons[-2], ws_cons[-1] = ws_cons[DC - 2], ws_cons[DC - 1]
            for m in range(DC):
                slot = m % 2
                w = [self.wP(ws_cons[m - 2])] if ws_cons[m - 2] else []
                wv_ = self.wld(self.WS[slot][:], wq_d[m], "WS%d" % slot, waits=w)
                psq = PSB[:, 0:512] if m % 2 == 0 else PSB[:, 512:1024]
                w0 = [wv_]
                if qb == 0 and m == 0:
                    w0.append(self.wV(v_done))
                    if causal:
                        w0.append(tm_tgt)
                if qb > 0 and m == 0:
                    w0.append(self.wA(exp_side["B1"]))
                if m >= 2:
                    w0.append(self.wA(q_evac[m - 2]))
                last = 0
                for k in range(DC):
                    last = self.pe(lambda q, psq=psq, k=k, slot=slot, qb=qb: q.matmul(
                        psq, self.WS[slot][:, k * 128:(k + 1) * 128],
                        self.XN[k][:, qb * 512:(qb + 1) * 512], start=(k == 0), stop=(k == DC - 1),
                        skip_group_check=True),
                        inc=(k == DC - 1), waits=(w0 if k == 0 else ()))
                ws_cons[m] = last
                q_last = last
                q_evac[m] = self.act(lambda q, m=m, psq=psq: q.activation(self.Q[m][:], psq, AF.Copy, scale=1.0),
                                     waits=[self.wP(last)])
            qn = 0
            for m in range(DC):
                wq_ = [self.wA(q_evac[m])]
                if m > 0:
                    wq_.append(self.wP(qn))
                if m == 0 and prev_op_done:
                    wq_.append(self.wV(prev_op_done))
                v = self.dve(lambda q, m=m: q.tensor_tensor(self.QSQ[:], self.Q[m][:], self.Q[m][:], ALU.mult),
                             waits=wq_)
                qn = self.pe(lambda q, m=m: q.matmul(
                    PSB[0:16, 1024:1536], self.cb_hind(m), self.QSQ[:],
                    start=(m == 0), stop=(m == DC - 1), skip_group_check=True),
                    inc=True, waits=[self.wV(v)])
            rmax_v = self.dve(lambda q: q.reduce_max(self.CBIA[0:16, 17:18], PSB[0:16, 1024:1536],
                                                     axis=AX.X), waits=[self.wP(qn)])
            vcs = self.dve(lambda q: q.tensor_tensor(self.CBIA[0:16, 18:19], self.CBIA[0:16, 17:18],
                                                     self.KMX[:], ALU.mult), chain=True)
            wnc = [self.wV(vcs)]
            if self.nc2_prev:
                wnc.append(self.nc2_prev)
            d1 = self.ld(bass.AP(self.d_ncd.tensor, 0, [[1, 16], [1, 1]]), self.CBIA[0:16, 18:19],
                         "NC1", waits=wnc)
            d2 = self.ld(self.CBIA[0:1, 0:16], bass.AP(self.d_ncd.tensor, 0, [[16, 1], [1, 16]]),
                         "NC2", waits=[d1])
            self.nc2_prev = d2
            vgm = self.dve(lambda q: q.reduce_max(self.CBIA[0:1, 19:20], self.CBIA[0:1, 0:16], axis=AX.X),
                           waits=[d2], chain=True)
            self.act(lambda q: q.activation(self.CBIA[0:1, 20:21], self.CBIA[0:1, 19:20], AF.Ln, scale=1.0),
                     waits=[self.wV(vgm)])
            ace = self.act(lambda q: q.activation(self.CBIA[0:1, 20:21], self.CBIA[0:1, 20:21], AF.Exp,
                                                  scale=0.5), chain=True)
            vng = self.dve(lambda q: q.tensor_scalar(self.CBIA[0:1, 21:22], self.CBIA[0:1, 20:21],
                                                     -1.0, None, ALU.mult), waits=[self.wA(ace)])
            wnc3 = [self.wV(vng)]
            if self.nc4_prev:
                wnc3.append(self.nc4_prev)
            d3 = self.ld(bass.AP(self.d_ncd.tensor, 16, [[1, 1], [1, 1]]), self.CBIA[0:1, 21:22],
                         "NC3", waits=wnc3)
            cb_rdy = self.ld(self.CBIA[:, 16:17], bass.AP(self.d_ncd.tensor, 16, [[0, 128], [1, 1]]),
                             "NC4", waits=[d3])
            self.nc4_prev = cb_rdy

            nkt = 4 * qb + 4 if causal else 16
            SIDES = ("A", "B1", "B2")
            GRP = {"A": (PSA, 1024), "B1": (PSB, 0), "B2": (PSB, 1024)}
            PTS = {"A": self.PTA, "B1": self.PTB[:, 0:1024], "B2": self.PTB[:, 1024:2048]}
            first_use = {"A": True, "B1": True, "B2": True}

            def emit_S(c, side, kt, first_of_qb):
                base, go = GRP[side]
                w = []
                if exp_side[side]:
                    w.append(self.wA(exp_side[side]))
                if first_of_qb:
                    if qb == 0:
                        w.append(self.wV(v_done))
                if first_use[side]:
                    first_use[side] = False
                    if side == "B2":
                        w.append(self.wV(rmax_v))
                    if side == "B1":
                        w.append(self.wA(q_evac[DC - 1]))
                pm = 0
                for hh in range(2):
                    so = go + hh * 512
                    pm = self.pe(lambda q, base=base, so=so, c=c, hh=hh, kt=kt: q.matmul(
                        base[:, so:so + 512], self.K[c][hh * 64:(hh + 1) * 64, kt * 128:(kt + 1) * 128],
                        self.Q[c][hh * 64:(hh + 1) * 64, :], start=True, stop=True,
                        skip_group_check=True),
                        inc=(hh == 1), waits=(w if hh == 0 else ()))
                mask_v = 0
                if causal:
                    r = kt - 4 * qb
                    if 0 <= r <= 3:
                        mask_v = self.dve(lambda q, base=base, go=go, r=r: q.tensor_tensor(
                            base[:, go:go + 1024], base[:, go:go + 1024],
                            bass.AP(self.TM.tensor, r * 512, [[2048, 128], [0, 2], [1, 512]]),
                            ALU.add), waits=[self.wP(pm)])
                return pm, mask_v

            def emit_E(c, side, kt, pm, mask_v, first_exp):
                base, go = GRP[side]
                ptb = PTS[side]
                we = [self.wP(pm)]
                if mask_v:
                    we.append(self.wV(mask_v))
                if av_done[side]:
                    we.append(self.wP(av_done[side]))
                if first_exp:
                    we.append(cb_rdy)
                ae = self.act(lambda q, base=base, go=go, ptb=ptb: q.activation(
                    ptb, base[:, go:go + 1024], AF.Exp, bias=self.CBIA[:, 16:17], scale=1.0),
                    waits=we)
                exp_side[side] = ae
                return ae

            def emit_AV(item):
                c, side, kt, ae = item
                ptb = PTS[side]
                last = 0
                for hh in range(2):
                    so = hh * 512
                    w = [self.wA(ae)]
                    if kt == 0 and hh == 0 and ctx_banks_free[0]:
                        w.append(self.wA(ctx_banks_free[0]))
                    last = self.pe(lambda q, kt=kt, c=c, hh=hh, so=so, ptb=ptb, nkt=nkt: q.matmul(
                        PSA[0:65, hh * 512: hh * 512 + 512],
                        self.VH[kt][:, (2 * c + hh) * 65:(2 * c + hh) * 65 + 65],
                        ptb[:, so:so + 512],
                        start=(kt == 0), stop=(kt == nkt - 1), skip_group_check=True),
                        inc=(hh == 1), waits=(w if hh == 0 else ()))
                av_done[side] = last
                return last

            def emit_epilogue(c, av_last, ei):
                par = ei % 2
                wev = [self.wP(av_last)]
                if epi_mult.get(ei - 2):
                    wev.append(self.wV(epi_mult[ei - 2]))
                ev = self.act(lambda q, par=par: q.activation(
                    self.CTXRAW[par][0:65, :], PSA[0:65, 0:1024], AF.Copy, scale=1.0), waits=wev)
                ctx_banks_free[0] = ev
                self.act(lambda q, par=par: q.activation(
                    self.RROW[:], self.CTXRAW[par][64:65, :], AF.Ln, scale=1.0), chain=True)
                ra = self.act(lambda q, par=par: q.activation(
                    self.RBS[par][0:1, :], self.RROW[:], AF.Exp, scale=-1.0), chain=True)
                d1 = self.ld(self.d_rsd[par, :], self.RBS[par][0:1, :], "RS%d" % par,
                             waits=[self.wA(ra)])
                wd2 = [d1]
                if epi_mult.get(ei - 2):
                    wd2.append(self.wV(epi_mult[ei - 2]))
                d2 = self.ld(self.RBS[par][0:64, :],
                             bass.AP(self.d_rsd.tensor, par * 1024, [[0, 64], [1, 1024]]),
                             "RB%d" % par, waits=wd2)
                wctx = [d2]
                if ctx_free:
                    wctx.append(self.wP(ctx_free))
                self.dve(lambda q, c=c, par=par: q.tensor_tensor(
                    self.CTX[c][0:64, :], self.CTXRAW[par][0:64, 0:512],
                    self.RBS[par][0:64, 0:512], ALU.mult), waits=wctx)
                epi_mult[ei] = self.dve(lambda q, c=c, par=par: q.tensor_tensor(
                    self.CTX[c][64:128, :], self.CTXRAW[par][0:64, 512:1024],
                    self.RBS[par][0:64, 512:1024], ALU.mult), chain=True)

            groups = []
            for c in range(DC):
                for kt in range(nkt):
                    groups.append((c, SIDES[(c * nkt + kt) % 3], kt))
            fifo = []
            first_exp_of_qb = True
            for gi, (c, side, kt) in enumerate(groups):
                pm, mask_v = emit_S(c, side, kt, first_of_qb=(gi == 0))
                ae = emit_E(c, side, kt, pm, mask_v, first_exp_of_qb)
                first_exp_of_qb = False
                fifo.append((c, side, kt, ae))
                while len(fifo) > 2:
                    item = fifo.pop(0)
                    av_last = emit_AV(item)
                    if item[2] == nkt - 1:
                        ei = epi_idx[0]
                        emit_epilogue(item[0], av_last, ei)
                        epi_idx[0] = ei + 1
            while fifo:
                item = fifo.pop(0)
                av_last = emit_AV(item)
                if item[2] == nkt - 1:
                    ei = epi_idx[0]
                    emit_epilogue(item[0], av_last, ei)
                    epi_idx[0] = ei + 1
            ctx_done = epi_mult[epi_idx[0] - 1]

            op_v = {}
            ws_cons[-2], ws_cons[-1] = ws_cons[DC - 2], ws_cons[DC - 1]
            for m in range(DC):
                slot = m % 2
                w = [self.wP(ws_cons[m - 2])] if ws_cons[m - 2] else []
                wv_ = self.wld(self.WS[slot][:], wo_d[m], "WS%d" % slot, waits=w)
                op = PSB[:, 1024:1536] if m % 2 == 0 else PSB[:, 1536:2048]
                w0 = [wv_]
                if m == 0:
                    w0 += [self.wV(ctx_done), self.wA(exp_side["B2"])]
                if m >= 2:
                    w0.append(self.wV(op_v[m - 2]))
                last = 0
                for k in range(DC):
                    last = self.pe(lambda q, op=op, k=k, slot=slot: q.matmul(
                        op, self.WS[slot][:, k * 128:(k + 1) * 128], self.CTX[k][:],
                        start=(k == 0), stop=(k == DC - 1), skip_group_check=True),
                        inc=(k == DC - 1), waits=(w0 if k == 0 else ()))
                ws_cons[m] = last
                bias_col = self.b_bo(m) if with_bias else self.b_z
                wst = [self.wP(last)] + (self.x_dep if (qb == 0 and m == 0) else [])
                op_v[m] = self.dve(lambda q, m=m, qb=qb, op=op, bias_col=bias_col: q.scalar_tensor_tensor(
                    self.X[m][:, qb * 512:(qb + 1) * 512], op, bias_col,
                    self.X[m][:, qb * 512:(qb + 1) * 512], ALU.add, ALU.add),
                    waits=wst)
            ctx_free = ws_cons[DC - 1]
            prev_op_done = op_v[DC - 1]
        self.ws_free = ws_cons[DC - 1]
        self.ln_gate = []
        for m in range(DC):
            self.x_chunk_dep[m] = [self.wV(op_v[m])]
        self.x_dep = [self.wV(self.C["V"])]
        self.xn_free = q_last

    def emit_ffn(self, w0_d, w1_d, w2_d, gw_d):
        PSA, PSB = self.PSA, self.PSB
        xn_ready = self.xn_ready
        wg_tgt = self.wld(self.WG[:], gw_d, "WG", waits=self.phase_gate())
        last = 0
        for tt in range(NTT):
            for dc in range(DC):
                last = self.pe(lambda q, tt=tt, dc=dc: q.matmul(
                    PSA[:, tt * 8:(tt + 1) * 8], self.XN[dc][:, tt * 128:(tt + 1) * 128],
                    self.WG[:, dc * 8:(dc + 1) * 8],
                    start=(dc == 0), stop=(dc == DC - 1), skip_group_check=True),
                    inc=(dc == DC - 1 and tt == NTT - 1),
                    waits=([wg_tgt, self.wV(xn_ready)] if (tt == 0 and dc == 0) else ()))
        bo = BIAS_OFF
        gb_rep = bass.AP(self.BIAS.tensor, bo["gb"], [[BIAS_W, 128], [0, 16], [1, 8]])
        g3 = bass.AP(self.GF.tensor, 0, [[128, 128], [8, 16], [1, 8]])
        gm3 = bass.AP(self.GM.tensor, 0, [[128, 128], [8, 16], [1, 8]])
        v1rep = bass.AP(self.GV.tensor, 0, [[64, 128], [1, 16], [0, 8]])
        v2rep = bass.AP(self.GV.tensor, 16, [[64, 128], [1, 16], [0, 8]])
        w1rep = bass.AP(self.GV.tensor, 32, [[64, 128], [1, 16], [0, 8]])
        w2rep = bass.AP(self.GV.tensor, 48, [[64, 128], [1, 16], [0, 8]])
        self.dve(lambda q: q.tensor_tensor(self.GF[:], PSA[:, 0:128], gb_rep, ALU.add),
                 waits=[self.wP(last)])
        self.dve(lambda q: q.tensor_reduce(self.GV[:, 0:16], g3, axis=AX.X, op=ALU.max), chain=True)
        self.dve(lambda q: q.tensor_tensor(self.GI[:], self.GF[:], v1rep, ALU.is_equal), chain=True)
        self.dve(lambda q: q.scalar_tensor_tensor(self.GM[:], self.GI[:], NEG, self.GF[:],
                                                  ALU.mult, ALU.add), chain=True)
        self.dve(lambda q: q.tensor_reduce(self.GV[:, 16:32], gm3, axis=AX.X, op=ALU.max), chain=True)
        self.dve(lambda q: q.tensor_tensor(self.GM[:], self.GM[:], v2rep, ALU.is_equal), chain=True)
        vd = self.dve(lambda q: q.tensor_tensor(self.GV[:, 32:48], self.GV[:, 16:32], self.GV[:, 0:16],
                                                ALU.subtract), chain=True)
        asg = self.act(lambda q: q.activation(self.GV[:, 48:64], self.GV[:, 32:48], AF.Sigmoid,
                                              scale=1.0), waits=[self.wV(vd)])
        self.dve(lambda q: q.tensor_scalar(self.GV[:, 32:48], self.GV[:, 48:64], -1.0, 1.0,
                                           ALU.mult, ALU.add), waits=[self.wA(asg)], chain=True)
        self.dve(lambda q: q.tensor_tensor(self.GI[:], self.GI[:], w1rep, ALU.mult), chain=True)
        self.dve(lambda q: q.tensor_tensor(self.GM[:], self.GM[:], w2rep, ALU.mult), chain=True)
        vm = self.dve(lambda q: q.tensor_tensor(self.MSC[:], self.GI[:], self.GM[:], ALU.add),
                      chain=True)
        gate_v = {}
        mt_v = 0
        tprev = [0, 0]
        for tt in range(NTT):
            spot = 1024 + (tt % 2) * 128
            wtm = [self.wV(vm)] if tt == 0 else []
            if tprev[tt % 2]:
                wtm.append(self.wV(tprev[tt % 2]))
            mt_m = self.pe(lambda q, tt=tt, spot=spot: q.matmul(
                PSA[0:8, spot:spot + 128], self.MSC[:, tt * 8:(tt + 1) * 8], self.cb_id,
                start=True, stop=True, skip_group_check=True),
                inc=True, waits=wtm)
            mt_v = self.dve(lambda q, tt=tt, spot=spot: q.tensor_copy(
                self.MT[0:8, tt * 128:(tt + 1) * 128], PSA[0:8, spot:spot + 128]),
                waits=[self.wP(mt_m)])
            tprev[tt % 2] = mt_v
        gate_done = mt_v

        w01_cons = {-2: self.w01_free, -1: self.w01_free}
        xn_last = 0
        last_stt = dict(self.prev_ffn_stt)
        last_cmb = self.prev_cmb
        for e in range(E):
            if e > 0:
                w01_cons[-2] = w01_cons[HC - 2]
                w01_cons[-1] = w01_cons[HC - 1]
            act_v = {}
            for hh in range(HC):
                slot = hh % 2
                w = [self.wP(w01_cons[hh - 2])] if w01_cons[hh - 2] else []
                wa = self.wld(self.W01[slot][:, 0:1024], w0_d[e, hh], "W01%d" % slot, waits=w)
                wb = self.wld(self.W01[slot][:, 1024:2048], w1_d[e, hh], "W01%d" % slot)
                for half in range(2):
                    hs = half * 1024
                    w0 = [wb]
                    if e == 0 and hh == 0 and half == 0:
                        w0.append(self.wV(gate_done))
                    if hh == 0 and last_cmb:
                        w0.append(self.wV(last_cmb))
                    if act_v.get((hh - 1, half)):
                        w0.append(self.wV(act_v[(hh - 1, half)]))
                    last = 0
                    for dc in range(DC):
                        for sub in range(2):
                            ss = hs + sub * 512
                            self.pe(lambda q, ss=ss, dc=dc, slot=slot: q.matmul(
                                PSA[:, ss:ss + 512], self.W01[slot][:, dc * 128:(dc + 1) * 128],
                                self.XN[dc][:, ss:ss + 512], start=(dc == 0), stop=(dc == DC - 1),
                                skip_group_check=True),
                                waits=(w0 if (dc == 0 and sub == 0) else ()))
                            last = self.pe(lambda q, ss=ss, dc=dc, slot=slot: q.matmul(
                                PSB[:, ss:ss + 512], self.W01[slot][:, 1024 + dc * 128: 1024 + (dc + 1) * 128],
                                self.XN[dc][:, ss:ss + 512], start=(dc == 0), stop=(dc == DC - 1),
                                skip_group_check=True),
                                inc=(dc == DC - 1 and sub == 1))
                    if half == 1:
                        w01_cons[hh] = last
                    xn_last = last
                    wsl = [self.wP(last)]
                    prev_stt = act_v.get((hh - 1, half), 0) or last_stt.get(half, 0)
                    if prev_stt:
                        wsl.append(self.wV(prev_stt))
                    a = self.act(lambda q, hs=hs, e=e, hh=hh: q.activation(
                        self.SIL[:, hs:hs + 1024], PSB[:, hs:hs + 1024], AF.Silu,
                        bias=self.b_b1(e, hh), scale=1.0), waits=wsl)
                    act_v[(hh, half)] = self.dve(lambda q, hs=hs, e=e, hh=hh: q.scalar_tensor_tensor(
                        self.ACT[hh][:, hs:hs + 1024], PSA[:, hs:hs + 1024], self.b_b0(e, hh),
                        self.SIL[:, hs:hs + 1024], ALU.add, ALU.mult), waits=[self.wA(a)])
            act_done = act_v[(HC - 1, 1)]
            mb_a = 0
            for ts in range(4):
                w0 = [self.wV(act_done)] if ts == 0 else [self.wA(mb_a)]
                if ts == 0 and last_cmb:
                    w0.append(self.wV(last_cmb))
                mb_m = self.pe(lambda q, ts=ts, e=e: q.matmul(
                    PSB[:, 0:512], self.cb_sel(e), self.MT[0:8, ts * 512:(ts + 1) * 512],
                    start=True, stop=True, skip_group_check=True), inc=True, waits=w0)
                mb_a = self.act(lambda q, ts=ts: q.activation(self.MBS[:, ts * 512:(ts + 1) * 512],
                                                              PSB[:, 0:512], AF.Copy, scale=1.0),
                                inc=True, waits=[self.wP(mb_m)])
            if e == 0:
                self._w2carry = [self.w2b_free, self.w2b_free]
            w2_cons = {-2: self._w2carry[0], -1: self._w2carry[1]}
            cmb_v = {}
            for dc in range(DC):
                slot = dc % 2
                w = [self.wP(w2_cons[dc - 2])] if w2_cons[dc - 2] else []
                wv_ = self.wld(self.W2B[slot][:], w2_d[e, dc], "W2B%d" % slot, waits=w)
                for ts in range(4):
                    op = PSA[:, ts * 512:(ts + 1) * 512]
                    w0 = [wv_]
                    if dc == 0 and ts == 0:
                        w0.append(self.wA(mb_a))
                    if cmb_v.get((dc - 1, ts)):
                        w0.append(self.wV(cmb_v[(dc - 1, ts)]))
                    last = 0
                    for hc in range(HC):
                        last = self.pe(lambda q, op=op, hc=hc, ts=ts, slot=slot: q.matmul(
                            op, self.W2B[slot][:, hc * 128:(hc + 1) * 128],
                            self.ACT[hc][:, ts * 512:(ts + 1) * 512],
                            start=(hc == 0), stop=(hc == HC - 1), skip_group_check=True),
                            inc=(hc == HC - 1), waits=(w0 if hc == 0 else ()))
                    if ts == 3:
                        w2_cons[dc] = last
                    wst = [self.wP(last)]
                    if e == 0 and dc == 0 and ts == 0:
                        wst += self.x_dep
                    self.dve(lambda q, op=op, e=e, dc=dc, ts=ts: q.scalar_tensor_tensor(
                        self.TMPF[:], op, self.b_b2(e, dc), self.MBS[:, ts * 512:(ts + 1) * 512],
                        ALU.add, ALU.mult), waits=wst, chain=True)
                    cmb_v[(dc, ts)] = self.dve(lambda q, dc=dc, ts=ts: q.tensor_tensor(
                        self.X[dc][:, ts * 512:(ts + 1) * 512], self.X[dc][:, ts * 512:(ts + 1) * 512],
                        self.TMPF[:], ALU.add), chain=True)
            self._w2carry = [w2_cons[DC - 2], w2_cons[DC - 1]]
            last_cmb = cmb_v[(DC - 1, 3)]
            last_stt = {0: act_v[(HC - 1, 0)], 1: act_v[(HC - 1, 1)]}
        self.w01_free = w01_cons[HC - 1]
        self.w2b_free = w2_cons[DC - 1]
        self.ln_gate = [self.wP(w2_cons[DC - 1]), self.wV(cmb_v[(DC - 1, 3)])]
        for dc in range(DC):
            self.x_chunk_dep[dc] = [self.wV(cmb_v[(dc, 3)])]
        self.prev_cmb = last_cmb
        self.prev_ffn_stt = dict(last_stt)
        self.x_dep = [self.wV(self.C["V"])]
        self.xn_free = xn_last

    def emit_all(self):
        self.emit_prologue()
        store_done = None
        for s in range(NSEQ):
            self.emit_load_x(s)
            subs = ["attn_a", "ffn", "attn_b", "ffn"][: self.n_sub]
            for sub in subs:
                self.emit_ln()
                if sub == "attn_a":
                    self.emit_attn(True, self.d_wq, self.d_wk, self.d_wv, self.d_wo, True)
                elif sub == "attn_b":
                    self.emit_attn(False, self.d_uq, self.d_uk, self.d_uv, self.d_uo, False)
                else:
                    self.emit_ffn(self.d_w0t, self.d_w1t, self.d_w2t, self.d_gwt)
            store_done = self.emit_store_x(s)
            self.xn_free = 0
            self.xn_free = self.C["P"] and 0
        self.final_out = store_done

    def finalize(self):
        nc = self.nc
        OPS = self.OPS
        final_out = self.final_out
        with nc.Block() as block:
            @block.sync
            def _(q):
                for fn in OPS["sync"]:
                    fn(q)
                for s_, v_ in final_out:
                    q.wait_ge(s_, v_)

            @block.gpsimd
            def _(q):
                for fn in OPS["gp"]:
                    fn(q)

            @block.tensor
            def _(q):
                for fn in OPS["pe"]:
                    fn(q)

            @block.scalar
            def _(q):
                for fn in OPS["act"]:
                    fn(q)

            @block.vector
            def _(q):
                for fn in OPS["dve"]:
                    fn(q)
        for cm in reversed(self._ctx):
            cm.__exit__(None, None, None)
        return nc


def build_program(n_sub=4):
    b = B(n_sub)
    b.emit_all()
    return b.finalize()


def kernel(**inputs):
    ins = {k: np.ascontiguousarray(np.asarray(v, dtype=np.float32)) for k, v in inputs.items()}
    blobs = _host_blobs(ins)
    x = ins["L0_attn_a_x"]
    nc = build_program(4)
    shared = {k: blobs[k] for k in ["wq", "wk", "wv", "wo", "uq", "uk", "uv", "uo",
                                    "w0t", "w1t", "w2t", "gwt", "biasblob", "constsb", "trimask"]}
    in_maps = []
    for c in range(NCORE):
        xt = np.ascontiguousarray(x[c * NSEQ:(c + 1) * NSEQ].transpose(0, 2, 1))
        m = dict(shared)
        m["xt"] = xt
        in_maps.append(m)
    res = run_bass_kernel_spmd(nc, in_maps, core_ids=list(range(NCORE)))
    outs = []
    for c in range(NCORE):
        o = res.results[c]["out"]
        outs.append(np.asarray(o).transpose(0, 2, 1))
    return np.ascontiguousarray(np.concatenate(outs, axis=0))
